# version 20
# baseline (speedup 1.0000x reference)
import numpy as np
import concourse.bass as bass
import concourse.mybir as mybir
from concourse.bass_utils import run_bass_kernel_spmd

F32 = mybir.dt.float32
BF16 = mybir.dt.bfloat16
AF = mybir.ActivationFunctionType
ALU = mybir.AluOpType
AX = mybir.AxisListType.X

D, DI, NS, IF = 1024, 2048, 16, 2816
T, PAD = 2048, 3
TP = T + PAD
NC = 8
DCH, DICH, IFCH = D // 128, DI // 128, IF // 128
WSH = 6 * D // NC
EPS = 1e-5
MAGIC = float(1.5 * 2 ** 23)

GRPS = [(0, PAD)] + [(PAD + 512 * g, 512) for g in range(4)]
P0G = [(0, 0, 512), (0, PAD, 512), (1, 0, 512), (2, 0, 512), (3, 0, 512)]
P0L = [(0, 515), (515, 512), (1027, 512), (1539, 512)]

DBG = False
_CACHE = {}
_FCONSTS = [EPS, -0.5, -1.0]
_FCONSTS128 = [MAGIC, -MAGIC, 128.0 - MAGIC]


def esz(dt):
    return {F32: 4, BF16: 2}[dt]


class _MockInst:
    def then_inc(self, sem, k):
        return self


class _MockEng:
    def __getattr__(self, name):
        def f(*a, **k):
            return _MockInst()
        return f


class Sem:
    def __init__(self, nc, name):
        self.h = nc.semaphore(name).__enter__()
        self.name = name
        self.n = 0


class Ctx:

    def __init__(self, nc):
        self.nc = nc
        self.c = {}
        self.dr = {}
        self.ps = []
        self.sems = {}
        self.labels = {}
        self.pass1 = True
        self.dbg = False
        self.scratch = {}

    def sem(self, name):
        if name not in self.sems:
            self.sems[name] = Sem(self.nc, name)
        return self.sems[name]

    def inc(self, inst, semname, k=1):
        s = self.sem(semname)
        s.n += k
        if not self.pass1:
            inst.then_inc(s.h, k)
        return inst

    def idma(self, inst, semname):
        return self.inc(inst, semname, 16)

    def lbl(self, name):
        if self.pass1:
            assert name not in self.labels, f"dup label {name}"
            self.labels[name] = {k: s.n for k, s in self.sems.items()}

    def fence(self, e, semname):
        if not self.pass1:
            e.wait_ge(self.sems[semname].h, self.sems[semname].n)

    def sw(self, e, inst, semname):
        self.inc(inst, semname)
        if not self.pass1:
            e.wait_ge(self.sems[semname].h, self.sems[semname].n)
        return inst

    def wait(self, e, label, *semnames):
        if self.pass1:
            return
        snap = self.labels[label]
        for sn in semnames:
            v = snap.get(sn, 0)
            if v > 0:
                e.wait_ge(self.sems[sn].h, v)

    def run_phases(self, phases):
        self.pass1 = True
        me = _MockEng()
        for name, secs in phases:
            for engname, fn in secs:
                fn(self, me)
        for s in self.sems.values():
            s.n = 0
        self.scratch = {}
        self.pass1 = False
        for name, secs in phases:
            with self.nc.Block(name) as blk:
                for engname, fn in secs:
                    getattr(blk, engname)(lambda e, fn=fn: fn(self, e))


class Arena:
    def __init__(self, nc, base, size, tag):
        self.nc, self.base, self.size, self.tag = nc, base, size, tag
        self.off = base
        self.pos = {}

    def alloc(self, name, shape, dtype, align=32):
        nbytes = int(np.prod(shape[1:])) * esz(dtype)
        off = (self.off + align - 1) // align * align
        assert off + nbytes <= self.base + self.size, \
            f"{self.tag}:{name} overflow by {off + nbytes - self.base - self.size}"
        t = self.nc.alloc_sbuf_tensor_at(name, list(shape), dtype, offset=off)
        self.pos[name] = off
        self.off = off + nbytes
        return t


def vsl(X, name):
    off = {"gam": 0, "bet": 8, "cw": 16, "cb": 80, "dv": 96, "bd": 112,
           "cvec": 128, "mflag": 136}[name]
    w = {"gam": 8, "bet": 8, "cw": 64, "cb": 16, "dv": 16, "bd": 16,
         "cvec": 8, "mflag": 1}[name]
    return X.c["vec"][:, off:off + w]


def mcol(X, name, k):
    base = {"sa": ("mod", 0), "ca": ("mod", 8), "ga": ("mod", 16),
            "sb": ("mod", 24), "cb2": ("mod", 32), "gb": ("mod", 40),
            "ca1": ("modd", 0), "cb1": ("modd", 8), "gap": ("modd", 16),
            "gbp": ("modd", 24), "gc": ("modd", 32), "gs": ("modd", 40)}[name]
    return X.c[base[0]][:, base[1] + k:base[1] + k + 1]


def p0xap(X, buf, k, w, off=0):
    return X.c["p0x"][:, buf * DCH * 515 + k * 515 + off:
                      buf * DCH * 515 + k * 515 + off + w]


def sb3(t, base, stride, nch, w, npart=128):
    freelen = int(t.shape[1])
    return bass.AP(t, base, [[freelen, npart], [stride, nch], [1, w]])


def p0grp(X, tname, buf, w, stride=515, nch=DCH):
    t = X.c[tname]
    return sb3(t, buf * nch * stride, stride, nch, w)


def build_program(dbg=False):
    nc = bass.Bass("TRN2", target_bir_lowering=False, detect_race_conditions=False)
    X = Ctx(nc)
    X.dbg = dbg
    dr, c = X.dr, X.c

    dr["xT"] = nc.dram_tensor("xT", [D, TP], F32, kind="ExternalInput")
    dr["w_in"] = nc.dram_tensor("w_in", [D, 2 * DI], F32, kind="ExternalInput")
    dr["wd"] = nc.dram_tensor("wd", [DI, DI], F32, kind="ExternalInput")
    dr["w_out"] = nc.dram_tensor("w_out", [DI, D], F32, kind="ExternalInput")
    dr["bT_t"] = nc.dram_tensor("bT_t", [128, NS * DICH], F32, kind="ExternalInput")
    dr["cT"] = nc.dram_tensor("cT", [NS, DI], F32, kind="ExternalInput")
    dr["wmod_sl"] = nc.dram_tensor("wmod_sl", [D, WSH], F32, kind="ExternalInput")
    dr["wgT"] = nc.dram_tensor("wgT", [D, IF], F32, kind="ExternalInput")
    dr["wuT"] = nc.dram_tensor("wuT", [D, IF], F32, kind="ExternalInput")
    dr["wdnT"] = nc.dram_tensor("wdnT", [IF, D], F32, kind="ExternalInput")
    dr["wgN"] = nc.dram_tensor("wgN", [IF, D], F32, kind="ExternalInput")
    dr["wuN"] = nc.dram_tensor("wuN", [IF, D], F32, kind="ExternalInput")
    dr["wdnN"] = nc.dram_tensor("wdnN", [D, IF], F32, kind="ExternalInput")
    dr["vecs"] = nc.dram_tensor("vecs", [128, 137], F32, kind="ExternalInput")
    dr["msks"] = nc.dram_tensor("msks", [16, 17], F32, kind="ExternalInput")
    dr["alog_in"] = nc.dram_tensor("alog_in", [NS, 1], F32, kind="ExternalInput")
    dr["clip_in"] = nc.dram_tensor("clip_in", [1, 2], F32, kind="ExternalInput")
    dr["outT"] = nc.dram_tensor("outT", [D, T], F32, kind="ExternalOutput")

    dr["sg_sp"] = nc.dram_tensor("sg_sp", [DICH, 128, T], BF16,
                                 kind="ExternalOutput" if dbg == "ph1"
                                 else "Internal")
    dr["x2_sp"] = nc.dram_tensor("x2_sp", [DCH, 128, T], F32,
                                 kind="ExternalOutput" if dbg in ("ph3", "pfull")
                                 else "Internal")
    dr["wq_g"] = nc.dram_tensor("wq_g", [IFCH, DCH, 128, 128], BF16)
    dr["wq_u"] = nc.dram_tensor("wq_u", [IFCH, DCH, 128, 128], BF16)
    dr["wq_d"] = nc.dram_tensor("wq_d", [DCH, IFCH, 128, 128], BF16)
    dr["ag_mod_i"] = nc.dram_tensor("ag_mod_i", [1, WSH], F32)
    dr["ag_mod_o"] = nc.dram_tensor("ag_mod_o", [NC, WSH], F32)
    dr["ag_car_i"] = nc.dram_tensor("ag_car_i", [1, 32], F32)
    dr["ag_car_o"] = nc.dram_tensor("ag_car_o", [NC, 32], F32)
    dr["ag_m1_i"] = nc.dram_tensor("ag_m1_i", [1, 8], F32)
    dr["ag_m1_o"] = nc.dram_tensor("ag_m1_o", [NC, 8], F32)
    dr["ag_m2_i"] = nc.dram_tensor("ag_m2_i", [1, 8], F32)
    dr["ag_m2_o"] = nc.dram_tensor("ag_m2_o", [NC, 8], F32)
    dr["srow2"] = nc.dram_tensor("srow2", [1, IF], F32, kind="Internal")
    dr["srow"] = nc.dram_tensor("srow", [1, 4096], F32)
    dr["h3_sp"] = nc.dram_tensor("h3_sp", [DCH, 128, T], BF16,
                                 kind="ExternalOutput" if dbg in ("ph3", "pfull")
                                 else "Internal")
    if dbg:
        dr["dbg_h2"] = nc.dram_tensor("dbg_h2", [128, DCH * TP], F32,
                                      kind="ExternalOutput")
        dr["dbg_mod"] = nc.dram_tensor("dbg_mod", [128, 96], F32,
                                       kind="ExternalOutput")
        dr["dbg_rows"] = nc.dram_tensor("dbg_rows", [1, 2 * TP], F32,
                                        kind="ExternalOutput")
        dr["dbg_h"] = nc.dram_tensor("dbg_h", [128, DCH * 515], F32,
                                     kind="ExternalOutput")
        if dbg == "ph1":
            dr["dbg_xs"] = nc.dram_tensor("dbg_xs", [128, DICH * T], F32,
                                          kind="ExternalOutput")
        if dbg == "ph2":
            dr["dbg_sc"] = nc.dram_tensor("dbg_sc", [16, 3 * T + 8], F32,
                                          kind="ExternalOutput")
        if dbg == "pfull":
            dr["dbg_y"] = nc.dram_tensor("dbg_y", [128, DICH * T], F32,
                                         kind="ExternalOutput")
            dr["dbg_b1"] = nc.dram_tensor("dbg_b1", [128, 1026], F32,
                                          kind="ExternalOutput")
            dr["dbg_hr"] = nc.dram_tensor("dbg_hr", [128, IFCH * T], F32,
                                          kind="ExternalOutput")
            dr["dbg_x2e"] = nc.dram_tensor("dbg_x2e", [128, DCH * T], BF16,
                                           kind="ExternalOutput")
            dr["dbg_sc2"] = nc.dram_tensor("dbg_sc2", [128, 16], F32,
                                           kind="ExternalOutput")
            dr["dbg_m5"] = nc.dram_tensor("dbg_m5", [128, IFCH * T], F32,
                                          kind="ExternalOutput")
        if dbg == "ph5":
            dr["dbg_m"] = nc.dram_tensor("dbg_m", [128, IFCH * T], F32,
                                         kind="ExternalOutput")
            dr["dbg_xr"] = nc.dram_tensor("dbg_xr", [128, DCH * T], F32,
                                          kind="ExternalOutput")
        if dbg == "ph3":
            dr["dbg_p3"] = nc.dram_tensor("dbg_p3", [128, 16], F32,
                                          kind="ExternalOutput")
            dr["dbg_wo"] = nc.dram_tensor("dbg_wo", [128, DICH * D], F32,
                                          kind="ExternalOutput")
            dr["dbg_y"] = nc.dram_tensor("dbg_y", [128, DICH * T], F32,
                                         kind="ExternalOutput")
            dr["dbg_ct"] = nc.dram_tensor("dbg_ct", [NS, DI], F32,
                                          kind="ExternalOutput")

    MB = 17408
    SZ_CONST, SZ_ROWS, SZ_STRM, SZ_H2, SZ_W4, SZ_XS = \
        7424, 37632, 34816, 32816, 32768, 65632
    A_CONST = Arena(nc, MB, SZ_CONST, "const")
    A_ROWS = Arena(nc, MB + SZ_CONST, SZ_ROWS, "rows")
    A_STRM = Arena(nc, MB + SZ_CONST + SZ_ROWS, SZ_STRM, "strm")
    R_H2 = MB + SZ_CONST + SZ_ROWS + SZ_STRM
    R_W4 = (R_H2 + SZ_H2 + 31) // 32 * 32
    R_XS = (R_W4 + SZ_W4 + 31) // 32 * 32
    assert R_XS + SZ_XS <= 229344, R_XS + SZ_XS
    X.A_STRM = A_STRM

    for n, sh, dt in [
            ("vec", [128, 137], F32), ("msk", [16, 17], F32),
            ("alog", [NS, 1], F32), ("Aneg", [NS, 1], F32),
            ("clip", [1, 2], F32), ("csil", [128, 8], F32),
            ("ones_b", [128, 1], BF16), ("onesr_f", [1, 128], F32),
            ("bT", [128, NS * DICH], BF16), ("cTs", [NS, DI], BF16),
            ("mod", [128, 48], F32), ("modd", [128, 48], F32),
            ("scg", [128, IFCH], F32), ("scu", [128, IFCH], F32),
            ("scd", [128, DCH], F32),
            ("cdg", [128, IFCH], F32), ("cdu", [128, IFCH], F32),
            ("cdd", [128, DCH], F32),
            ("sxr", [128, 1], F32), ("s2r", [128, 1], F32),
            ("macc", [128, 2], F32), ("carry", [NS, 1], F32),
            ("DcFc", [NS, 2], F32), ("gath", [NS, 16], F32),
            ("tiny", [128, 8], F32), ("tiny2", [128, 8], F32),
            ("fc", [1, 8], F32), ("fc128", [128, 8], F32),
            ("convc", [128, DICH * 4], BF16),
    ]:
        c[n] = A_CONST.alloc(n, sh, dt)

    for n, sh, dt in [
            ("r1row", [1, TP], F32), ("mr1row", [1, TP], F32),
            ("drow", [1, T], F32), ("hs", [NS, T], BF16),
            ("maccw", [128, 128], F32), ("rs1", [1, 512], F32),
            ("rs2", [1, 512], F32), ("rs3", [1, 512], F32),
            ("d16s", [NS, 512], F32),
    ]:
        c[n] = A_ROWS.alloc(n, sh, dt)
    c["rs0"] = c["drow"]
    dro = A_ROWS.pos["drow"]
    c["yscr"] = nc.alloc_sbuf_tensor_at("yscr", [128, 512], F32, offset=dro)
    c["gscr"] = [nc.alloc_sbuf_tensor_at(f"gscr{i}", [128, 512], BF16,
                                         offset=dro + 2048 + i * 1024)
                 for i in range(2)]
    c["hpre"] = [nc.alloc_sbuf_tensor_at(f"hpre{i}", [128, 512], F32,
                                         offset=dro + 4096 + i * 2048)
                 for i in range(2)]
    c["br1s"] = nc.alloc_sbuf_tensor_at("br1s", [128, 512], F32,
                                        offset=A_ROWS.pos["rs1"])
    c["bmr1s"] = nc.alloc_sbuf_tensor_at("bmr1s", [128, 512], F32,
                                         offset=A_ROWS.pos["rs2"])
    c["br3s"] = nc.alloc_sbuf_tensor_at("br3s", [128, 512], F32,
                                        offset=A_ROWS.pos["rs3"])
    c["bmr3s"] = nc.alloc_sbuf_tensor_at("bmr3s", [128, 512], F32,
                                         offset=A_ROWS.pos["d16s"])
    hso = A_ROWS.pos["hs"]
    c["x2b"] = [nc.alloc_sbuf_tensor_at(f"x2b{i}", [128, 512], BF16,
                                        offset=hso + i * 1024) for i in range(2)]
    c["sqb"] = [nc.alloc_sbuf_tensor_at(f"sqb{i}", [128, 512], BF16,
                                        offset=hso + 2048 + i * 1024)
                for i in range(2)]
    c["s3r"] = [nc.alloc_sbuf_tensor_at(f"s3r{i}", [1, 512], F32,
                                        offset=dro + i * 2048)
                for i in range(2)]
    c["gcol"] = [nc.alloc_sbuf_tensor_at(f"gcol{i}", [128, DCH * 128], BF16,
                                         offset=A_STRM.base + i * 2048)
                 for i in range(2)]
    c["ucol"] = [nc.alloc_sbuf_tensor_at(f"ucol{i}", [128, DCH * 128], BF16,
                                         offset=A_STRM.base + 4096 + i * 2048)
                 for i in range(2)]
    c["dcol"] = [nc.alloc_sbuf_tensor_at(f"dcol{i}", [128, IFCH * 128], BF16,
                                         offset=A_STRM.base + 8192 + i * 5632)
                 for i in range(2)]
    c["obuf"] = [nc.alloc_sbuf_tensor_at(f"obuf{i}", [128, 512], F32,
                                         offset=A_STRM.base + 19456 + i * 2048)
                 for i in range(2)]

    c["h2"] = nc.alloc_sbuf_tensor_at("h2", [128, DCH * TP], BF16, offset=R_H2)
    c["dA"] = nc.alloc_sbuf_tensor_at("dA", [NS, T], F32, offset=R_H2)
    c["Bx"] = nc.alloc_sbuf_tensor_at("Bx", [NS, T], F32, offset=R_H2 + 8192)
    c["h3"] = nc.alloc_sbuf_tensor_at("h3", [128, DCH * T], BF16, offset=R_H2)
    c["x2rl"] = nc.alloc_sbuf_tensor_at("x2rl", [128, 2 * DCH * 512], F32,
                                        offset=R_H2)
    c["xrel"] = nc.alloc_sbuf_tensor_at("xrel", [128, DCH * 512], F32,
                                        offset=R_H2)
    c["x2g"] = nc.alloc_sbuf_tensor_at("x2g", [128, DCH * 512], F32,
                                       offset=R_H2 + 16384)
    c["wo"] = nc.alloc_sbuf_tensor_at("wo", [128, DICH * D], BF16, offset=R_W4)

    c["m_b"] = nc.alloc_sbuf_tensor_at("m_b", [128, (IFCH - 16) * T], BF16,
                                       offset=R_W4)
    c["qnat"] = nc.alloc_sbuf_tensor_at("qnat", [128, D], F32, offset=R_W4)
    c["qtT"] = nc.alloc_sbuf_tensor_at("qtT", [128, IF], F32, offset=R_W4 + 4096)
    c["qout"] = [nc.alloc_sbuf_tensor_at("qout0", [128, IF], BF16,
                                         offset=R_W4 + 15360)] * 2
    c["rbc"] = nc.alloc_sbuf_tensor_at("rbc", [128, IF], F32,
                                       offset=R_W4 + 20992)
    c["xs"] = nc.alloc_sbuf_tensor_at("xs", [128, DICH * T], BF16, offset=R_XS)
    c["m_a"] = nc.alloc_sbuf_tensor_at("m_a", [128, 16 * T], BF16, offset=R_XS)
    c["p0x"] = nc.alloc_sbuf_tensor_at("p0x", [128, 2 * DCH * 515], BF16,
                                       offset=R_XS)
    c["p0h"] = nc.alloc_sbuf_tensor_at("p0h", [128, DCH * 515], F32,
                                       offset=R_XS + 16480)
    c["p0sq"] = nc.alloc_sbuf_tensor_at("p0sq", [128, DCH * 515], BF16,
                                        offset=R_XS + 32960)
    c["p0hb"] = nc.alloc_sbuf_tensor_at("p0hb", [128, DCH * 515], BF16,
                                        offset=R_XS + 41216)
    c["rbcd"] = nc.alloc_sbuf_tensor_at("rbcd", [128, D], F32,
                                        offset=R_H2 + 16384)
    c["dnat"] = nc.alloc_sbuf_tensor_at("dnat", [128, IF], F32,
                                        offset=R_H2 + 20480)

    X.ps = [nc.alloc_psum_tensor(f"psb{b}", [128, 512], F32) for b in range(8)]
    for i, v in enumerate(_FCONSTS):
        nc.const_aps.register(F32, v, c["fc"][0:1, i:i + 1])
    for i, v in enumerate(_FCONSTS128):
        nc.const_aps.register(F32, v, c["fc128"][:, i:i + 1])
    c["wmb"] = [A_STRM.alloc(f"wmb{i}", [128, WSH], F32) for i in range(2)]
    c["onesW"] = nc.alloc_sbuf_tensor_at("onesW", [128, 128], BF16,
                                         offset=A_STRM.base + 32768)
    c["sgg"] = [nc.alloc_sbuf_tensor_at(f"sgg{i}", [128, DICH * 512], BF16,
                                        offset=A_STRM.base + i * 16384)
                for i in range(2)]
    s1 = Arena(nc, A_STRM.base, A_STRM.size, "strm1")
    c["wcol"] = [s1.alloc(f"wcol{i}", [128, DCH * 128], BF16) for i in range(2)]
    c["xraw"] = s1.alloc("xraw", [128, TP], BF16)
    c["cacc"] = s1.alloc("cacc", [128, T], BF16)
    c["sgrow"] = [s1.alloc(f"sgrow{i}", [128, T], BF16) for i in range(2)]
    c["qrow"] = s1.alloc("qrow", [1, IF], F32)
    s2 = Arena(nc, A_STRM.base + 6144, A_STRM.size - 6144 - 11296, "strm2")
    c["wdcol"] = [s2.alloc(f"wdcol{i}", [128, DICH * 128], BF16) for i in range(2)]
    c["exps"] = [s2.alloc(f"exps{i}", [128, 512], F32) for i in range(2)]
    c["spb"] = [s2.alloc(f"spb{i}", [128, 512], BF16) for i in range(2)]

    phases = [("ph0", [("sync", ph0_sync), ("gpsimd", ph0_gpsimd),
                       ("scalar", ph0_scalar), ("tensor", ph0_tensor),
                       ("vector", ph0_vector)])]
    if dbg != "ph0":
        phases.append(("ph1", [("sync", ph1_sync), ("gpsimd", ph1_gpsimd),
                               ("scalar", ph1_scalar), ("tensor", ph1_tensor),
                               ("vector", ph1_vector)]))
    if dbg not in ("ph0", "ph1"):
        phases.append(("ph2", [("sync", ph2_sync), ("gpsimd", ph2_gpsimd),
                               ("scalar", ph2_scalar), ("tensor", ph2_tensor),
                               ("vector", ph2_vector)]))
    if dbg not in ("ph0", "ph1", "ph2"):
        phases.append(("ph3", [("sync", ph3_sync), ("gpsimd", ph3_gpsimd),
                               ("scalar", ph3_scalar), ("tensor", ph3_tensor),
                               ("vector", ph3_vector)]))
    if dbg not in ("ph0", "ph1", "ph2", "ph3"):
        phases.append(("ph5", [("sync", ph5_sync), ("gpsimd", ph5_gpsimd),
                               ("scalar", ph5_scalar), ("tensor", ph5_tensor),
                               ("vector", ph5_vector)]))
    if dbg not in ("ph0", "ph1", "ph2", "ph3", "ph5"):
        phases.append(("ph6", [("sync", ph6_sync), ("gpsimd", ph6_gpsimd),
                               ("scalar", ph6_scalar), ("tensor", ph6_tensor),
                               ("vector", ph6_vector)]))

    if dbg == "ph0":
        phases.append(("dbg", [("sync", dbg_ph0_sync),
                               ("gpsimd", dbg_ph0_gpsimd)]))
    if dbg == "ph1":
        phases.append(("dbg", [("gpsimd", dbg_ph1_gp)]))
    if dbg == "ph2":
        phases.append(("dbg", [("gpsimd", dbg_ph2_gp)]))
    if dbg == "ph3":
        phases.append(("dbg", [("gpsimd", dbg_ph3_gp)]))
    if dbg == "ph5":
        phases.append(("dbg", [("gpsimd", dbg_ph5_gp)]))
    if dbg == "pfull":
        phases.insert([i for i, (n, _) in enumerate(phases)
                       if n == "ph3"][0] + 1,
                      ("dbgy", [("gpsimd", dbg_pfull_y)]))
        phases.insert([i for i, (n, _) in enumerate(phases)
                       if n == "ph5"][0] + 1,
                      ("dbgm", [("gpsimd", dbg_pfull_m)]))
        phases.append(("dbgf", [("gpsimd", dbg_pfull_end)]))
    X.run_phases(phases)
    return nc, X


def ph0_sync(X, e):
    c, dr = X.c, X.dr
    for dst, src in [(c["vec"], dr["vecs"]), (c["msk"], dr["msks"]),
                     (c["alog"], dr["alog_in"]), (c["clip"], dr["clip_in"])]:
        X.idma(e.dma_start(out=dst[:, :], in_=src[:, :]), "qs")
    X.lbl("consts")


def ph0_gpsimd(X, e):
    c, dr = X.c, X.dr
    e.memset(c["ones_b"][:, :], 1.0)
    e.memset(c["onesW"][:, :], 1.0)
    if X.dbg:
        e.memset(c["p0h"][:, :], 0.0)
        e.memset(c["p0x"][:, :], 0.0)
        e.memset(c["p0sq"][:, :], 0.0)
        e.memset(c["p0hb"][:, :], 0.0)
    for i, v in enumerate(_FCONSTS):
        e.memset(c["fc"][0:1, i:i + 1], v)
    for i, v in enumerate(_FCONSTS128):
        e.memset(c["fc128"][:, i:i + 1], v)
    X.inc(e.memset(c["onesr_f"][:, :], 1.0), "gS")
    X.lbl("ones")
    wmb = X.c["wmb"]
    for k in range(DCH):
        if k >= 2:
            X.wait(e, f"modmm{k - 2}", "tS")
        src = bass.AP(dr["wmod_sl"], k * 128 * WSH, [[WSH, 128], [1, WSH]])
        X.idma(e.dma_start(out=wmb[k % 2][:, :], in_=src), f"qwm{k % 2}")
        X.lbl(f"wm{k}")
    X.idma(e.dma_start(out=c["bT"][:, :], in_=dr["bT_t"][:, :]), "qbc")
    X.idma(e.dma_start(out=c["cTs"][:, :], in_=dr["cT"][:, :]), "qbc")
    def xload(j):
        g0, w = P0L[j]
        src = bass.AP(dr["xT"], g0, [[TP, 128], [128 * TP, DCH], [1, w]])
        dst = sb3(c["p0x"], (j % 2) * DCH * 515, 515, DCH, w)
        X.idma(e.dma_start(out=dst, in_=src), f"qx{j % 2}")
        X.lbl(f"x{j}")
    xload(0)
    xload(1)
    X.wait(e, "mod_row", "vS")
    X.idma(e.dma_start(out=dr["ag_mod_i"][:, :], in_=c["rs0"][0:1, 0:WSH]), "qagm")
    if not X.pass1:
        e.wait_ge(X.sems["qagm"].h, X.sems["qagm"].n)
    X.inc(e.collective_compute(
        "AllGather", ALU.bypass, replica_groups=[list(range(NC))],
        ins=[dr["ag_mod_i"].ap()], outs=[dr["ag_mod_o"].ap()]), "cS")
    if not X.pass1:
        e.wait_ge(X.sems["cS"].h, X.sems["cS"].n)
    for col in range(48):
        v, ch = col // 8, col % 8
        s2 = bass.AP(dr["ag_mod_o"], v * 1024 + ch * 128, [[1, 128], [1, 1]])
        X.idma(e.dma_start(out=c["mod"][:, col:col + 1], in_=s2), "qmod")
    X.lbl("mod_in")
    xg_last = {0: 1, 1: 2, 2: 3, 3: 4}
    for j in range(2, 4):
        X.wait(e, f"h2_{xg_last[j - 2]}", "vS")
        xload(j)


def ph0_scalar(X, e):
    c = X.c
    X.wait(e, "consts", "qs")
    X.inc(e.activation(c["csil"][:, :], vsl(X, "cvec"), AF.Silu), "aS")
    X.lbl("csil")
    X.inc(e.activation(c["Aneg"][:, :], c["alog"][:, :], AF.Exp), "aS")
    X.inc(e.mul(c["Aneg"][:, :], c["Aneg"][:, :], -1.0), "aS")
    for gi, (lj, loff, w) in enumerate(P0G):
        g0 = GRPS[gi][0]
        buf = lj % 2
        X.wait(e, f"x{lj}", f"qx{lj % 2}")
        if gi >= 1:
            X.wait(e, f"h2_{gi - 1}", "vS")
        X.inc(e.activation(
            sb3(c["p0sq"], loff, 515, DCH, w),
            sb3(c["p0x"], buf * DCH * 515 + loff, 515, DCH, w),
            AF.Square), "aS")
        X.lbl(f"sq1_{gi}")

        X.wait(e, f"var1_{gi}", "vS")
        X.inc(e.activation(c["rs3"][0:1, 0:w], c["rs3"][0:1, 0:w], AF.Ln,
                           bias=EPS), "aS")
        X.inc(e.activation(c["r1row"][0:1, g0:g0 + w], c["rs3"][0:1, 0:w],
                           AF.Exp, scale=-0.5), "aS")
        X.lbl(f"r1_{gi}")
        X.wait(e, f"h_{gi}", "vS")
        X.inc(e.activation(
            sb3(c["p0sq"], loff, 515, DCH, w),
            sb3(c["p0h"], loff, 515, DCH, w),
            AF.Square), "aS")
        X.lbl(f"sq2_{gi}")
        X.wait(e, f"var2_{gi}", "vS")
        X.inc(e.activation(c["rs3"][0:1, 0:w], c["rs3"][0:1, 0:w], AF.Ln,
                           bias=EPS), "aS")
        X.inc(e.activation(c["rs2"][0:1, 0:w], c["rs3"][0:1, 0:w],
                           AF.Exp, scale=-0.5), "aS")
        X.lbl(f"r2_{gi}")


def ph0_tensor(X, e):
    c, ps = X.c, X.ps
    wmb = X.c["wmb"]
    X.wait(e, "csil", "aS")
    X.wait(e, "ones", "gS")
    for k in range(DCH):
        X.wait(e, f"wm{k}", f"qwm{k % 2}")
        for j in range(2):
            inst = e.matmul(ps[6 + j][0:1, 0:WSH // 2], c["csil"][:, k:k + 1],
                            wmb[k % 2][:, j * (WSH // 2):(j + 1) * (WSH // 2)],
                            start=(k == 0), stop=(k == DCH - 1))
        X.inc(inst, "tS")
        X.lbl(f"modmm{k}")
    X.lbl("mod_mm")
    for gi, (lj, loff, w) in enumerate(P0G):
        g0 = GRPS[gi][0]
        buf = lj % 2
        X.wait(e, f"x{lj}", f"qx{lj % 2}")
        if gi == 0:
            X.wait(e, "mod_row", "vS")
        if gi >= 1:
            X.wait(e, f"var2_{gi - 1}", "vS")
        for k in range(DCH):
            e.matmul(ps[7][:, 0:w], c["onesW"][:, :],
                     p0xap(X, buf, k, w, off=loff),
                     start=(k == 0), stop=(k == DCH - 1))
        X.wait(e, f"sq1_{gi}", "aS")
        for k in range(DCH):
            inst = e.matmul(ps[6][:, 0:w], c["onesW"][:, :],
                            c["p0sq"][:, k * 515 + loff:k * 515 + loff + w],
                            start=(k == 0), stop=(k == DCH - 1))
        X.inc(inst, "tS")
        X.lbl(f"st1_{gi}")
        X.wait(e, f"mr1_{gi}", "vS")
        if gi >= 1:
            X.wait(e, f"h_{gi - 1}", "vS")
        e.matmul(ps[5][:, 0:w], c["onesr_f"][0:1, :],
                 c["r1row"][0:1, g0:g0 + w], start=True, stop=True)
        inst = e.matmul(ps[4][:, 0:w], c["onesr_f"][0:1, :],
                        c["mr1row"][0:1, g0:g0 + w], start=True, stop=True)
        X.inc(inst, "tS")
        X.lbl(f"bc1_{gi}")
        X.wait(e, f"hb_{gi}", "vS")
        X.wait(e, f"rows1_{gi}", "vS")
        for k in range(DCH):
            e.matmul(ps[7][:, 0:w], c["onesW"][:, :],
                     c["p0hb"][:, k * 515 + loff:k * 515 + loff + w],
                     start=(k == 0), stop=(k == DCH - 1))
        X.wait(e, f"sq2_{gi}", "aS")
        for k in range(DCH):
            inst = e.matmul(ps[6][:, 0:w], c["onesW"][:, :],
                            c["p0sq"][:, k * 515 + loff:k * 515 + loff + w],
                            start=(k == 0), stop=(k == DCH - 1))
        X.inc(inst, "tS")
        X.lbl(f"st2_{gi}")
        X.wait(e, f"mr2_{gi}", "vS")
        if gi >= 1:
            X.wait(e, f"h2_{gi - 1}", "vS")
        e.matmul(ps[3][:, 0:w], c["onesr_f"][0:1, :], c["rs2"][0:1, 0:w],
                 start=True, stop=True)
        inst = e.matmul(ps[2][:, 0:w], c["onesr_f"][0:1, :], c["rs3"][0:1, 0:w],
                        start=True, stop=True)
        X.inc(inst, "tS")
        X.lbl(f"bc2_{gi}")


def ph0_vector(X, e):
    c, ps = X.c, X.ps
    X.wait(e, "mod_mm", "tS")
    e.tensor_copy(c["rs0"][0:1, 0:WSH // 2], ps[6][0:1, 0:WSH // 2])
    X.inc(e.tensor_copy(c["rs0"][0:1, WSH // 2:WSH], ps[7][0:1, 0:WSH // 2]),
          "vS")
    X.lbl("mod_row")
    X.wait(e, "mod_in", "qmod")
    md, mo = c["modd"], c["mod"]
    X.sw(e, e.tensor_scalar_add(md[:, 0:8], mo[:, 8:16], 1.0), "vS")
    X.sw(e, e.tensor_scalar_add(md[:, 8:16], mo[:, 32:40], 1.0), "vS")
    for (srcc, dstc) in ((16, 16), (40, 24)):
        g = mo[:, srcc:srcc + 8]
        X.sw(e, e.tensor_scalar_mul(c["tiny"][:, 0:8], g, -1.0), "vS")
        X.sw(e, e.tensor_tensor(c["tiny"][:, 0:8], c["tiny"][:, 0:8], g,
                                ALU.max), "vS")
        X.sw(e, e.tensor_scalar_max(c["tiny"][:, 0:8], c["tiny"][:, 0:8],
                                    0.25), "vS")
        sg = md[:, dstc:dstc + 8]
        X.sw(e, e.tensor_scalar_add(sg, g, 1e-38), "vS")
        for _ in range(5):
            X.sw(e, e.tensor_scalar_mul(sg, sg, 1e9), "vS")
            X.sw(e, e.tensor_scalar_min(sg, sg, 1.0), "vS")
            X.sw(e, e.tensor_scalar_max(sg, sg, -1.0), "vS")
        X.sw(e, e.tensor_mul(md[:, dstc:dstc + 8], md[:, dstc:dstc + 8],
                             c["tiny"][:, 0:8]), "vS")
    X.sw(e, e.tensor_mul(md[:, 32:40], md[:, 16:24], md[:, 0:8]), "vS")
    X.inc(e.tensor_mul(md[:, 40:48], md[:, 16:24], mo[:, 0:8]), "vS")
    X.lbl("mod_done")
    for gi, (lj, loff, w) in enumerate(P0G):
        g0 = GRPS[gi][0]
        buf = lj % 2
        X.wait(e, f"st1_{gi}", "tS")
        e.tensor_scalar_mul(c["rs1"][0:1, 0:w], ps[7][0:1, 0:w], 1.0 / D)
        e.tensor_scalar_mul(c["rs2"][0:1, 0:w], ps[6][0:1, 0:w], 1.0 / D)
        e.tensor_mul(c["rs3"][0:1, 0:w], c["rs1"][0:1, 0:w], c["rs1"][0:1, 0:w])
        X.inc(e.tensor_sub(c["rs3"][0:1, 0:w], c["rs2"][0:1, 0:w],
                           c["rs3"][0:1, 0:w]), "vS")
        X.lbl(f"var1_{gi}")
        X.lbl(f"rows1_{gi}")
        X.wait(e, f"r1_{gi}", "aS")
        X.inc(e.scalar_tensor_tensor(c["mr1row"][0:1, g0:g0 + w],
                                     c["rs1"][0:1, 0:w], -1.0,
                                     c["r1row"][0:1, g0:g0 + w],
                                     ALU.mult, ALU.mult), "vS")
        X.lbl(f"mr1_{gi}")
        X.wait(e, f"bc1_{gi}", "tS")
        X.wait(e, "mod_done", "vS")
        for k in range(DCH):
            kb = k * 515 + loff
            e.tensor_scalar(
                c["p0h"][:, kb:kb + w], ps[4][:, 0:w],
                mcol(X, "ca1", k), mcol(X, "sa", k), ALU.mult, ALU.add)
            e.tensor_mul(c["p0sq"][:, kb:kb + w],
                         p0xap(X, buf, k, w, off=loff), ps[5][:, 0:w])
            e.scalar_tensor_tensor(
                c["p0h"][:, kb:kb + w],
                c["p0sq"][:, kb:kb + w], mcol(X, "ca1", k),
                c["p0h"][:, kb:kb + w], ALU.mult, ALU.add)
        X.inc(e.engine_nop(), "vS")
        X.lbl(f"h_{gi}")
        X.inc(e.tensor_copy(
            sb3(c["p0hb"], loff, 515, DCH, w),
            sb3(c["p0h"], loff, 515, DCH, w)), "vS")
        X.lbl(f"hb_{gi}")
        X.wait(e, f"st2_{gi}", "tS")
        e.tensor_scalar_mul(c["rs1"][0:1, 0:w], ps[7][0:1, 0:w], 1.0 / D)
        e.tensor_scalar_mul(c["rs2"][0:1, 0:w], ps[6][0:1, 0:w], 1.0 / D)
        e.tensor_mul(c["rs3"][0:1, 0:w], c["rs1"][0:1, 0:w], c["rs1"][0:1, 0:w])
        X.inc(e.tensor_sub(c["rs3"][0:1, 0:w], c["rs2"][0:1, 0:w],
                           c["rs3"][0:1, 0:w]), "vS")
        X.lbl(f"var2_{gi}")
        X.wait(e, f"r2_{gi}", "aS")
        X.inc(e.scalar_tensor_tensor(c["rs3"][0:1, 0:w], c["rs1"][0:1, 0:w],
                                     -1.0, c["rs2"][0:1, 0:w],
                                     ALU.mult, ALU.mult), "vS")
        X.lbl(f"mr2_{gi}")
        X.wait(e, f"bc2_{gi}", "tS")
        for k in range(DCH):
            kb = k * 515 + loff
            e.tensor_scalar(
                c["p0h"][:, kb:kb + w], ps[2][:, 0:w],
                vsl(X, "gam")[:, k:k + 1], vsl(X, "bet")[:, k:k + 1],
                ALU.mult, ALU.add)
            e.tensor_mul(c["p0sq"][:, kb:kb + w],
                         c["p0hb"][:, kb:kb + w], ps[3][:, 0:w])
            e.scalar_tensor_tensor(
                c["h2"][:, k * TP + g0:k * TP + g0 + w],
                c["p0sq"][:, kb:kb + w],
                vsl(X, "gam")[:, k:k + 1],
                c["p0h"][:, kb:kb + w], ALU.mult, ALU.add)
        X.inc(e.engine_nop(), "vS")
        X.lbl(f"h2_{gi}")
    X.lbl("h2_all")


def _noop(e):
    return e.engine_nop()


def dbg_ph0_sync(X, e):
    c, dr = X.c, X.dr
    X.wait(e, "h2_all", "vS")
    X.idma(e.dma_start(out=dr["dbg_rows"][0:1, 0:TP], in_=c["r1row"][:, :]), "qs")
    X.idma(e.dma_start(out=dr["dbg_rows"][0:1, TP:2 * TP], in_=c["mr1row"][:, :]),
           "qs")
    X.idma(e.dma_start(out=dr["dbg_mod"][:, 0:48], in_=c["mod"][:, :]), "qs")
    X.idma(e.dma_start(out=dr["dbg_mod"][:, 48:96], in_=c["modd"][:, :]), "qs")
    if not X.pass1:
        e.wait_ge(X.sems["qs"].h, X.sems["qs"].n)


def dbg_ph0_gpsimd(X, e):
    c, dr = X.c, X.dr
    X.wait(e, "h2_all", "vS")
    X.idma(e.dma_start(out=dr["dbg_h2"][:, :], in_=c["h2"][:, :]), "qdbg")
    X.idma(e.dma_start(out=dr["dbg_h"][:, :], in_=c["p0h"][:, :]), "qdbg")
    if not X.pass1:
        e.wait_ge(X.sems["qdbg"].h, X.sems["qdbg"].n)


def _stage_inputs(inputs):
    x = np.asarray(inputs["L0_attn_x"])[0]
    xTfull = np.ascontiguousarray(x.T)

    def tile_vec(v, w):
        return np.ascontiguousarray(np.asarray(v).reshape(w, 128).T).astype(np.float32)

    gam = tile_vec(inputs["L0_attn_ln_gamma"], DCH)
    bet = tile_vec(inputs["L0_attn_ln_beta"], DCH)
    cwr = np.asarray(inputs["L0_attn_conv_w"]).reshape(DI, 4)
    cw = np.ascontiguousarray(
        cwr.reshape(DICH, 128, 4).transpose(1, 0, 2).reshape(128, DICH * 4))
    cb = tile_vec(inputs["L0_attn_conv_b"], DICH)
    dv = tile_vec(inputs["L0_attn_D_vec"], DICH)
    bd = tile_vec(inputs["L0_attn_bd"], DICH)
    cvec = np.ascontiguousarray(
        np.asarray(inputs["L0_c"]).reshape(DCH, 128).T).astype(np.float32)
    vecs = np.concatenate([gam, bet, cw.astype(np.float32), cb, dv, bd, cvec],
                          axis=1)
    assert vecs.shape == (128, 136)

    bT_t = np.ascontiguousarray(
        np.asarray(inputs["L0_attn_B_mat"]).T.reshape(DICH, 128, NS)
        .transpose(1, 0, 2).reshape(128, DICH * NS)).astype(np.float32)
    cT = np.ascontiguousarray(np.asarray(inputs["L0_attn_C_mat"]).T).astype(np.float32)
    alog = np.asarray(inputs["L0_attn_A_log"]).reshape(NS, 1).astype(np.float32)
    clip2 = np.array([[float(np.asarray(inputs["L0_ffn_clip_x"]).reshape(())),
                       float(np.asarray(inputs["L0_ffn_clip_down"]).reshape(()))]],
                     np.float32)
    wgT = np.ascontiguousarray(np.asarray(inputs["L0_ffn_w_gate"]).T)
    wuT = np.ascontiguousarray(np.asarray(inputs["L0_ffn_w_up"]).T)
    wdnT = np.ascontiguousarray(np.asarray(inputs["L0_ffn_w_down"]).T)
    wmod = np.asarray(inputs["L0_wmod"])

    shared = dict(
        w_in=np.ascontiguousarray(np.asarray(inputs["L0_attn_W_in"])),
        wd=np.ascontiguousarray(np.asarray(inputs["L0_attn_Wd"])),
        w_out=np.ascontiguousarray(np.asarray(inputs["L0_attn_W_out"])),
        bT_t=bT_t, cT=cT,
        wgN=np.ascontiguousarray(np.asarray(inputs["L0_ffn_w_gate"])),
        wuN=np.ascontiguousarray(np.asarray(inputs["L0_ffn_w_up"])),
        wdnN=np.ascontiguousarray(np.asarray(inputs["L0_ffn_w_down"])),
        wgT=wgT, wuT=wuT, wdnT=wdnT,
        alog_in=alog, clip_in=clip2,
    )
    in_maps = []
    for cid in range(NC):
        s = cid * T
        if cid == 0:
            xc = np.concatenate([np.zeros((D, PAD), np.float32),
                                 xTfull[:, :T]], axis=1)
        else:
            xc = np.ascontiguousarray(xTfull[:, s - PAD:s + T])
        m = np.zeros((16, 17), np.float32)
        m[:, :8] = (np.arange(8) < cid).astype(np.float32)
        m[:, 8:16] = (np.arange(8) >= cid).astype(np.float32)
        m[:, 16] = 0.0 if cid == 0 else 1.0
        im = dict(shared)
        flag = np.full((128, 1), 0.0 if cid == 0 else 1.0, np.float32)
        im["vecs"] = np.ascontiguousarray(np.concatenate([vecs, flag], axis=1))
        im["xT"] = np.ascontiguousarray(xc)
        im["msks"] = m
        im["wmod_sl"] = np.ascontiguousarray(wmod[:, cid * WSH:(cid + 1) * WSH])
        in_maps.append(im)
    return in_maps


def kernel(**inputs):
    in_maps = _stage_inputs(inputs)
    if "main" not in _CACHE:
        _CACHE["main"] = build_program(dbg=DBG)
    nc, X = _CACHE["main"]
    res = run_bass_kernel_spmd(nc, in_maps, core_ids=list(range(NC)))
    outs = [res.results[cid]["outT"] for cid in range(NC)]
    full = np.concatenate([np.asarray(o).T for o in outs], axis=0)
    return np.ascontiguousarray(full[None]).astype(np.float32)


def _ph1_tiles():
    tiles = []
    for oc in range(32):
        for gi, (g0, w) in enumerate(GRPS):
            if oc >= 16 and gi == 0:
                continue
            tiles.append((oc, gi, g0, w))
    return tiles


def _ev_sem(oc):
    return "vS" if oc < 16 else "aS"


def ph1_sync(X, e):
    c, dr = X.c, X.dr
    for ic in range(IFCH):
        if ic >= 1:
            X.wait(e, f"g_red{ic - 1}", "vS")
        src = bass.AP(dr["wgN"], ic * 128 * D, [[D, 128], [1, D]])
        X.idma(e.dma_start(out=c["qnat"][:, :], in_=src), "qn")
        X.lbl(f"g_natld{ic}")
    X.wait(e, "g_scg_done", "vS")
    for ic in range(IFCH):
        dst = bass.AP(dr["srow"], ic * 128, [[1, 128], [1, 1]])
        X.idma(e.dma_start(out=dst, in_=c["scg"][:, ic:ic + 1]), "qsr")
    if not X.pass1:
        e.wait_ge(X.sems["qsr"].h, X.sems["qsr"].n)
    X.idma(e.dma_start(out=c["qrow"][0:1, 0:IF],
                       in_=bass.AP(dr["srow"], 0, [[1, 1], [1, IF]])), "qsr")
    X.lbl("g_srow_in")
    X.wait(e, "g_recip_row", "vS")
    X.idma(e.dma_start(out=dr["srow2"].ap(), in_=c["qrow"][0:1, 0:IF]), "qrbc")
    if not X.pass1:
        e.wait_ge(X.sems["qrbc"].h, X.sems["qrbc"].n)
    X.idma(e.dma_start(out=c["rbc"][:, 0:IF],
                       in_=bass.AP(dr["srow2"], 0, [[0, 128], [1, IF]])),
           "qrbc")
    X.lbl("rbc_g")
    for hc in range(DCH):
        if hc >= 1:
            X.wait(e, f"g_qmul{hc - 1}", "aS")
            dst = bass.AP(dr["wq_g"], (hc - 1) * 128,
                          [[DCH * 128, 128], [128 * DCH * 128, IFCH],
                           [1, 128]])
            X.idma(e.dma_start(out=dst, in_=c["qout"][0][:, 0:IF]), "qqs2")
            X.lbl(f"g_qsp{hc - 1}")
        src = bass.AP(dr["wgT"], hc * 128 * IF, [[IF, 128], [1, IF]])
        X.idma(e.dma_start(out=c["qtT"][:, :], in_=src), "qt")
        X.lbl(f"g_qtld{hc}")
    X.wait(e, f"g_qmul{DCH - 1}", "aS")
    dst = bass.AP(dr["wq_g"], (DCH - 1) * 128,
                  [[DCH * 128, 128], [128 * DCH * 128, IFCH], [1, 128]])
    X.idma(e.dma_start(out=dst, in_=c["qout"][0][:, 0:IF]), "qqs2")
    X.lbl(f"g_qsp{DCH - 1}")


def ph1_gpsimd(X, e):
    c, dr = X.c, X.dr
    tiles = _ph1_tiles()
    last_of_oc = {}
    for ti, (oc, gi, g0, w) in enumerate(tiles):
        last_of_oc[oc] = ti

    def wcol_load(oc):
        if oc >= 2:
            X.wait(e, f"wmm{oc - 2}", "tS")
        src = bass.AP(dr["w_in"], oc * 128,
                      [[2 * DI, 128], [128 * 2 * DI, DCH], [1, 128]])
        X.idma(e.dma_start(out=c["wcol"][oc % 2][:, :], in_=src), f"qw{oc % 2}")
        X.lbl(f"wld{oc}")

    wcol_load(0)
    wcol_load(1)
    nxt = 2
    while nxt < 23:
        wcol_load(nxt)
        nxt += 1
    while nxt < 32:
        wcol_load(nxt)
        nxt += 1


def ph1_tensor(X, e):
    c, ps = X.c, X.ps
    tiles = _ph1_tiles()
    ev_engine = {}
    for ti, (oc, gi, g0, w) in enumerate(tiles):
        ev_engine[ti] = _ev_sem(oc)
    first_of_oc = set()
    seen = set()
    for ti, (oc, gi, g0, w) in enumerate(tiles):
        if oc not in seen:
            seen.add(oc)
            first_of_oc.add(ti)
    for ti, (oc, gi, g0, w) in enumerate(tiles):
        if ti in first_of_oc:
            X.wait(e, f"wld{oc}", f"qw{oc % 2}")
            if oc == 0:
                X.wait(e, "h2_all", "vS")
        bank = ti % 4
        if ti >= 4:
            X.wait(e, f"ev{ti - 4}", ev_engine[ti - 4])
        for k in range(DCH):
            inst = e.matmul(ps[bank][:, 0:w],
                            c["wcol"][oc % 2][:, k * 128:(k + 1) * 128],
                            c["h2"][:, k * TP + g0:k * TP + g0 + w],
                            start=(k == 0), stop=(k == DCH - 1))
        X.inc(inst, "tS")
        X.lbl(f"mmt{ti}")
        if ti == max(t2 for t2, (o2, _, _, _) in enumerate(tiles) if o2 == oc):
            X.lbl(f"wmm{oc}")


def ph1_vector(X, e):
    c, ps = X.c, X.ps
    tiles = _ph1_tiles()
    for ti, (oc, gi, g0, w) in enumerate(tiles):
        if oc >= 16:
            continue
        bank = ti % 4
        X.wait(e, f"mmt{ti}", "tS")
        X.inc(e.tensor_copy(c["xraw"][:, g0:g0 + w], ps[bank][:, 0:w]), "vS")
        X.lbl(f"ev{ti}")
        if gi == len(GRPS) - 1:
            if oc >= 1:
                X.wait(e, f"xs{oc - 1}", "aS")
            e.tensor_scalar_mul(c["xraw"][:, 0:PAD], c["xraw"][:, 0:PAD],
                                vsl(X, "mflag")[:, 0:1])
            e.tensor_scalar_mul(c["cacc"][:, :], c["xraw"][:, 0:T],
                                vsl(X, "cw")[:, oc * 4:oc * 4 + 1])
            for j in range(1, 4):
                inst = e.scalar_tensor_tensor(
                    c["cacc"][:, :], c["xraw"][:, j:j + T],
                    vsl(X, "cw")[:, oc * 4 + j:oc * 4 + j + 1],
                    c["cacc"][:, :], ALU.mult, ALU.add)
            X.inc(inst, "vS")
            X.lbl(f"conv{oc}")
            for _ in range(2):
                ic = X.scratch.setdefault("g_red_i", 0)
                if ic < IFCH:
                    X.wait(e, f"g_natld{ic}", "qn")
                    X.inc(e.reduce_max(c["scg"][:, ic:ic + 1], c["qnat"][:, :],
                                       axis=AX, apply_absolute_value=True),
                          "vS")
                    X.lbl(f"g_red{ic}")
                    X.scratch["g_red_i"] = ic + 1
    X.lbl("g_scg_done")
    X.wait(e, "g_srow_in", "qsr")
    e.tensor_scalar_mul(c["qrow"][0:1, 0:IF], c["qrow"][0:1, 0:IF], 1.0 / 127.0)
    X.inc(e.reciprocal(c["qrow"][0:1, 0:IF], c["qrow"][0:1, 0:IF]), "vS")
    X.lbl("g_recip_row")
    for hc in range(DCH):
        X.wait(e, f"g_qtld{hc}", "qt")
        if hc == 0:
            X.wait(e, "rbc_g", "qrbc")
        X.inc(e.tensor_mul(c["qtT"][:, :], c["qtT"][:, :], c["rbc"][:, 0:IF]),
              "vS")
        X.lbl(f"g_vmul{hc}")


def ph1_scalar(X, e):
    c, dr, ps = X.c, X.dr, X.ps
    tiles = _ph1_tiles()
    for oc in range(16):
        X.wait(e, f"conv{oc}", "vS")
        X.inc(e.activation(c["xs"][:, oc * T:(oc + 1) * T], c["cacc"][:, :],
                           AF.Silu, bias=vsl(X, "cb")[:, oc:oc + 1]), "aS")
        X.lbl(f"xs{oc}")
    for ti, (oc, gi, g0, w) in enumerate(tiles):
        if oc < 16:
            continue
        ic = oc - 16
        bank = ti % 4
        X.wait(e, f"mmt{ti}", "tS")
        if gi == 1 and ic >= 2:
            X.wait(e, f"sgsp{ic - 2}", f"qsg{ic % 2}")
        X.inc(e.activation(c["sgrow"][ic % 2][:, g0 - PAD:g0 - PAD + w],
                           ps[bank][:, 0:w], AF.Silu), "aS")
        X.lbl(f"ev{ti}")
        if gi == len(GRPS) - 1:
            X.fence(e, "aS")
            dst = bass.AP(dr["sg_sp"], ic * 128 * T, [[T, 128], [1, T]])
            X.idma(e.dma_start(out=dst, in_=c["sgrow"][ic % 2][:, :]),
                   f"qsg{ic % 2}")
            X.lbl(f"sgsp{ic}")
    X.lbl("ph1_done")
    for hc in range(DCH):
        X.wait(e, f"g_vmul{hc}", "vS")
        for h2 in range(2):
            sl = slice(h2 * (IF // 2), (h2 + 1) * (IF // 2))
            e.activation(c["qtT"][:, sl], c["qtT"][:, sl], AF.Identity,
                         bias=MAGIC)
        if hc >= 1:
            X.wait(e, f"g_qsp{hc - 1}", "qqs2")
        for h2 in range(2):
            sl = slice(h2 * (IF // 2), (h2 + 1) * (IF // 2))
            inst = e.activation(c["qout"][0][:, sl], c["qtT"][:, sl],
                                AF.Identity, bias=-MAGIC)
        X.inc(inst, "aS")
        X.lbl(f"g_qmul{hc}")


def dbg_ph1_gp(X, e):
    c, dr = X.c, X.dr
    X.wait(e, "xs15", "aS")
    X.wait(e, "sgsp15", f"qsg{15 % 2}")
    X.idma(e.dma_start(out=dr["dbg_xs"][:, :], in_=c["xs"][:, :]), "qdbg")
    if not X.pass1:
        e.wait_ge(X.sems["qdbg"].h, X.sems["qdbg"].n)


def ph2_sync(X, e):
    c, dr = X.c, X.dr
    for ic in range(IFCH):
        if ic >= 1:
            X.wait(e, f"u_red{ic - 1}", "vS")
        src = bass.AP(dr["wuN"], ic * 128 * D, [[D, 128], [1, D]])
        X.idma(e.dma_start(out=c["qnat"][:, :], in_=src), "qn")
        X.lbl(f"u_natld{ic}")
    X.wait(e, "u_scg_done", "vS")
    X.wait(e, "rbc_g", "qrbc")
    for ic in range(IFCH):
        dst = bass.AP(dr["srow"], ic * 128, [[1, 128], [1, 1]])
        X.idma(e.dma_start(out=dst, in_=c["scu"][:, ic:ic + 1]), "qsr")
    if not X.pass1:
        e.wait_ge(X.sems["qsr"].h, X.sems["qsr"].n)
    X.idma(e.dma_start(out=c["qrow"][0:1, 0:IF],
                       in_=bass.AP(dr["srow"], 0, [[1, 1], [1, IF]])), "qsr")
    X.lbl("u_srow_in")
    X.wait(e, "g_vmul7", "vS")
    X.wait(e, "u_recip_row", "vS")
    X.idma(e.dma_start(out=dr["srow2"].ap(), in_=c["qrow"][0:1, 0:IF]), "qrbc")
    if not X.pass1:
        e.wait_ge(X.sems["qrbc"].h, X.sems["qrbc"].n)
    X.idma(e.dma_start(out=c["rbc"][:, 0:IF],
                       in_=bass.AP(dr["srow2"], 0, [[0, 128], [1, IF]])),
           "qrbc")
    X.lbl("rbc_u")
    for hc in range(DCH):
        if hc == 0:
            X.wait(e, "wmm31", "tS")
        if hc >= 1:
            X.wait(e, f"d_red{hc - 1}", "vS")
        src = bass.AP(dr["wdnN"], hc * 128 * IF, [[IF, 128], [1, IF]])
        X.idma(e.dma_start(out=c["dnat"][:, :], in_=src), "qt")
        X.lbl(f"d_natld{hc}")
    X.wait(e, "d_scd_done", "vS")
    for hc in range(DCH):
        dst = bass.AP(dr["srow"], hc * 128, [[1, 128], [1, 1]])
        X.idma(e.dma_start(out=dst, in_=c["scd"][:, hc:hc + 1]), "qsr")
    if not X.pass1:
        e.wait_ge(X.sems["qsr"].h, X.sems["qsr"].n)
    X.idma(e.dma_start(out=c["qrow"][0:1, 0:D],
                       in_=bass.AP(dr["srow"], 0, [[1, 1], [1, D]])), "qsr")
    X.lbl("d_srow_in")
    X.wait(e, "d_recip_row", "vS")
    X.idma(e.dma_start(out=bass.AP(dr["srow2"], 0, [[1, 1], [1, D]]),
                       in_=c["qrow"][0:1, 0:D]), "qrbc")
    if not X.pass1:
        e.wait_ge(X.sems["qrbc"].h, X.sems["qrbc"].n)
    X.idma(e.dma_start(out=c["rbcd"][:, 0:D],
                       in_=bass.AP(dr["srow2"], 0, [[0, 128], [1, D]])),
           "qrbc")
    X.lbl("rbc_d")
    for hc in range(DCH):
        if hc == 0:
            X.wait(e, f"g_qmul{DCH - 1}", "aS")
        if hc >= 1:
            X.wait(e, f"u_qmul{hc - 1}", "vS")
            dst = bass.AP(dr["wq_u"], (hc - 1) * 128,
                          [[DCH * 128, 128], [128 * DCH * 128, IFCH],
                           [1, 128]])
            X.idma(e.dma_start(out=dst, in_=c["qout"][0][:, 0:IF]), "qqs2")
            X.lbl(f"u_qsp{hc - 1}")
        src = bass.AP(dr["wuT"], hc * 128 * IF, [[IF, 128], [1, IF]])
        X.idma(e.dma_start(out=c["qtT"][:, :], in_=src), "qt")
        X.lbl(f"u_qtld{hc}")
    X.wait(e, f"u_qmul{DCH - 1}", "vS")
    dst = bass.AP(dr["wq_u"], (DCH - 1) * 128,
                  [[DCH * 128, 128], [128 * DCH * 128, IFCH], [1, 128]])
    X.idma(e.dma_start(out=dst, in_=c["qout"][0][:, 0:IF]), "qqs2")
    X.lbl(f"u_qsp{DCH - 1}")
    for ic in range(IFCH):
        buf = "qnat" if ic % 2 == 0 else "qtT"
        if ic == 1:
            X.wait(e, "u_qmul7", "vS")
        if ic >= 1:
            X.wait(e, f"d_qmul{ic - 1}", "vS")
            dst = bass.AP(dr["wq_d"], (ic - 1) * 128,
                          [[IFCH * 128, 128], [128 * IFCH * 128, DCH],
                           [1, 128]])
            X.idma(e.dma_start(out=dst, in_=c["qout"][0][:, 0:D]), "qqs2")
            X.lbl(f"d_qsp{ic - 1}")
        src = bass.AP(dr["wdnT"], ic * 128 * D, [[D, 128], [1, D]])
        X.idma(e.dma_start(out=c[buf][:, 0:D], in_=src), "qt2")
        X.lbl(f"d_qtld{ic}")
    X.wait(e, f"d_qmul{IFCH - 1}", "vS")
    dst = bass.AP(dr["wq_d"], (IFCH - 1) * 128,
                  [[IFCH * 128, 128], [128 * IFCH * 128, DCH], [1, 128]])
    X.idma(e.dma_start(out=dst, in_=c["qout"][0][:, 0:D]), "qqs2")
    X.lbl(f"d_qsp{IFCH - 1}")


def ph2_gpsimd(X, e):
    c, dr = X.c, X.dr

    def wdcol_load(oc):
        if oc >= 2:
            X.wait(e, f"wdmm{oc - 2}", "tS")
        src = bass.AP(dr["wd"], oc * 128,
                      [[DI, 128], [128 * DI, DICH], [1, 128]])
        X.idma(e.dma_start(out=c["wdcol"][oc % 2][:, :], in_=src),
               f"qd{oc % 2}")
        X.lbl(f"wdld{oc}")

    for oc in range(DICH):
        wdcol_load(oc)
    X.wait(e, "DcFc_done", "vS")
    src = bass.AP(c["DcFc"], 0, [[2, 16], [1, 2]])
    X.idma(e.dma_start(out=bass.AP(dr["ag_car_i"], 0, [[1, 1], [1, 32]]),
                       in_=src), "qagc")
    if not X.pass1:
        e.wait_ge(X.sems["qagc"].h, X.sems["qagc"].n)
    X.inc(e.collective_compute(
        "AllGather", ALU.bypass, replica_groups=[list(range(NC))],
        ins=[dr["ag_car_i"].ap()], outs=[dr["ag_car_o"].ap()]), "cS")
    if not X.pass1:
        e.wait_ge(X.sems["cS"].h, X.sems["cS"].n)
    for j in range(NC):
        s1 = bass.AP(dr["ag_car_o"], j * 32, [[2, 16], [1, 1]])
        X.idma(e.dma_start(out=c["gath"][:, j:j + 1], in_=s1,
                           allow_slow_non_contiguous=True), "qgth")
        s2 = bass.AP(dr["ag_car_o"], j * 32 + 1, [[2, 16], [1, 1]])
        X.idma(e.dma_start(out=c["gath"][:, 8 + j:9 + j], in_=s2,
                           allow_slow_non_contiguous=True), "qgth")
    X.lbl("gath_in")
    X.wait(e, f"d_qsp{IFCH - 1}", "qqs2")
    src = bass.AP(dr["w_out"], 0, [[D, 128], [128 * D, DICH], [1, D]])
    dst = bass.AP(c["wo"], 0, [[DICH * D, 128], [D, DICH], [1, D]])
    X.idma(e.dma_start(out=dst, in_=src), "qwo")
    X.lbl("wo_ld")


def ph2_tensor(X, e):
    c, ps = X.c, X.ps
    X.wait(e, "ph1_done", "aS")
    for g in range(4):
        if g > 0:
            X.wait(e, f"bev{g - 1}", "vS")
        for k in range(DICH):
            inst = e.matmul(ps[7][0:16, 0:512],
                            c["bT"][:, k * NS:(k + 1) * NS],
                            c["xs"][:, k * T + g * 512:k * T + (g + 1) * 512],
                            start=(k == 0), stop=(k == DICH - 1))
        X.inc(inst, "tS")
        X.lbl(f"bmm{g}")
    ti = 0
    for oc in range(DICH):
        X.wait(e, f"wdld{oc}", f"qd{oc % 2}")
        for g in range(4):
            bank = ti % 3
            if ti >= 3:
                X.wait(e, f"wdev{ti - 3}", "aS")
            for k in range(DICH):
                inst = e.matmul(
                    ps[bank][:, 0:512],
                    c["wdcol"][oc % 2][:, k * 128:(k + 1) * 128],
                    c["xs"][:, k * T + g * 512:k * T + (g + 1) * 512],
                    start=(k == 0), stop=(k == DICH - 1))
            X.inc(inst, "tS")
            X.lbl(f"wdmt{ti}")
            X.wait(e, f"wdln{ti}", "aS")
            inst = e.matmul(ps[3 + g][:, 0:512], c["onesW"][:, :],
                            c["spb"][ti % 2][:, :],
                            start=(oc == 0), stop=(oc == DICH - 1))
            X.inc(inst, "tS")
            X.lbl(f"dmm{ti}")
            ti += 1
        X.lbl(f"wdmm{oc}")
    for g in range(4):
        X.wait(e, f"dle{g}", "vS")
        inst = e.matmul(ps[3 + g][0:16, 0:512], c["onesr_f"][0:1, 0:16],
                        c["drow"][0:1, g * 512:(g + 1) * 512],
                        start=True, stop=True)
        X.inc(inst, "tS")
        X.lbl(f"damm{g}")
    X.wait(e, "stot", "vS")
    X.wait(e, "bev3", "vS")
    inst = e.matmul(ps[7][0:16, 0:1], c["onesr_f"][0:1, 0:16],
                    c["rs1"][0:1, 0:1], start=True, stop=True)
    X.inc(inst, "tS")
    X.lbl("dcmm")


def ph2_scalar(X, e):
    c, ps = X.c, X.ps
    ti = 0
    for oc in range(DICH):
        for g in range(4):
            bank = ti % 3
            X.wait(e, f"wdmt{ti}", "tS")
            X.inc(e.activation(c["exps"][ti % 2][:, :], ps[bank][:, 0:512],
                               AF.Exp), "aS")
            X.lbl(f"wdev{ti}")
            if ti >= 2:
                X.wait(e, f"dmm{ti - 2}", "tS")
            X.inc(e.activation(c["spb"][ti % 2][:, :], c["exps"][ti % 2][:, :],
                               AF.Ln, bias=1.0), "aS")
            X.lbl(f"wdln{ti}")
            ti += 1
    for g in range(4):
        X.wait(e, f"damm{g}", "tS")
        X.inc(e.activation(c["dA"][:, g * 512:(g + 1) * 512],
                           ps[3 + g][0:16, 0:512], AF.Exp,
                           scale=c["Aneg"][:, 0:1]), "aS")
        X.lbl(f"dAa{g}")
    X.wait(e, "dcmm", "tS")
    X.inc(e.activation(c["DcFc"][:, 0:1], ps[7][0:16, 0:1], AF.Exp,
                       scale=c["Aneg"][:, 0:1]), "aS")
    X.lbl("Dc_done")


def ph2_vector(X, e):
    c, ps = X.c, X.ps
    for g in range(4):
        X.wait(e, f"bmm{g}", "tS")
        X.inc(e.tensor_copy(c["Bx"][:, g * 512:(g + 1) * 512],
                            ps[7][0:16, 0:512]), "vS")
        X.lbl(f"bev{g}")
    for ic in range(IFCH):
        X.wait(e, f"u_natld{ic}", "qn")
        X.inc(e.reduce_max(c["scu"][:, ic:ic + 1], c["qnat"][:, :], axis=AX,
                           apply_absolute_value=True), "vS")
        X.lbl(f"u_red{ic}")
    X.lbl("u_scg_done")
    X.wait(e, "u_srow_in", "qsr")
    e.tensor_scalar_mul(c["qrow"][0:1, 0:IF], c["qrow"][0:1, 0:IF], 1.0 / 127.0)
    X.inc(e.reciprocal(c["qrow"][0:1, 0:IF], c["qrow"][0:1, 0:IF]), "vS")
    X.lbl("u_recip_row")
    for hc in range(DCH):
        X.wait(e, f"d_natld{hc}", "qt")
        X.inc(e.reduce_max(c["scd"][:, hc:hc + 1], c["dnat"][:, :], axis=AX,
                           apply_absolute_value=True), "vS")
        X.lbl(f"d_red{hc}")
    X.lbl("d_scd_done")
    X.wait(e, "d_srow_in", "qsr")
    e.tensor_scalar_mul(c["qrow"][0:1, 0:D], c["qrow"][0:1, 0:D], 1.0 / 127.0)
    X.inc(e.reciprocal(c["qrow"][0:1, 0:D], c["qrow"][0:1, 0:D]), "vS")
    X.lbl("d_recip_row")
    for hc in range(DCH):
        X.wait(e, f"u_qtld{hc}", "qt")
        if hc == 0:
            X.wait(e, "rbc_u", "qrbc")
        e.tensor_mul(c["qtT"][:, :], c["qtT"][:, :], c["rbc"][:, 0:IF])
        if hc >= 1:
            X.wait(e, f"u_qsp{hc - 1}", "qqs2")
        else:
            X.wait(e, f"g_qsp{DCH - 1}", "qqs2")
        X.inc(e.tensor_scalar(c["qout"][0][:, 0:IF], c["qtT"][:, :],
                              MAGIC, MAGIC, ALU.add, ALU.subtract), "vS")
        X.lbl(f"u_qmul{hc}")
    for ic in range(IFCH):
        buf = "qnat" if ic % 2 == 0 else "qtT"
        X.wait(e, f"d_qtld{ic}", "qt2")
        if ic == 0:
            X.wait(e, "rbc_d", "qrbc")
        e.tensor_mul(c[buf][:, 0:D], c[buf][:, 0:D], c["rbcd"][:, 0:D])
        X.wait(e, f"d_qsp{ic - 1}" if ic >= 1 else f"u_qsp{DCH - 1}", "qqs2")
        X.inc(e.tensor_scalar(c["qout"][0][:, 0:D], c[buf][:, 0:D],
                              MAGIC, MAGIC, ALU.add, ALU.subtract), "vS")
        X.lbl(f"d_qmul{ic}")
    for g in range(4):
        X.wait(e, f"dmm{60 + g}", "tS")
        X.inc(e.tensor_scalar_mul(c["drow"][0:1, g * 512:(g + 1) * 512],
                                  ps[3 + g][0:1, 0:512], 1.0 / DI), "vS")
        X.lbl(f"dle{g}")
    for g in range(4):
        X.wait(e, f"dAa{g}", "aS")
        X.inc(e.tensor_mul(c["Bx"][:, g * 512:(g + 1) * 512],
                           c["Bx"][:, g * 512:(g + 1) * 512],
                           ps[3 + g][0:16, 0:512]), "vS")
        X.lbl(f"bxm{g}")
    X.sw(e, e.tensor_tensor_scan(c["hs"][:, :], c["dA"][:, :], c["Bx"][:, :],
                                 0.0, ALU.mult, ALU.add), "vS")
    X.sw(e, e.tensor_copy(c["DcFc"][:, 1:2], c["hs"][:, T - 1:T]), "vS")
    X.inc(e.reduce_sum(c["rs1"][0:1, 0:1], c["drow"][0:1, 0:T], axis=AX),
          "vS")
    X.lbl("stot")
    X.wait(e, "Dc_done", "aS")
    X.inc(e.engine_nop(), "vS")
    X.lbl("DcFc_done")
    X.wait(e, "gath_in", "qgth")
    X.sw(e, e.tensor_mul(c["gath"][:, 0:8], c["gath"][:, 0:8],
                         c["msk"][:, 0:8]), "vS")
    X.sw(e, e.tensor_add(c["gath"][:, 0:8], c["gath"][:, 0:8],
                         c["msk"][:, 8:16]), "vS")
    X.sw(e, e.tensor_mul(c["gath"][:, 8:16], c["gath"][:, 8:16],
                         c["msk"][:, 0:8]), "vS")
    X.sw(e, e.memset(c["carry"][:, :], 0.0), "vS")
    for j in range(NC):
        X.sw(e, e.tensor_mul(c["carry"][:, 0:1], c["carry"][:, 0:1],
                             c["gath"][:, j:j + 1]), "vS")
        X.sw(e, e.tensor_add(c["carry"][:, 0:1], c["carry"][:, 0:1],
                             c["gath"][:, 8 + j:9 + j]), "vS")
    X.inc(e.tensor_tensor_scan(c["hs"][:, :], c["dA"][:, :], c["Bx"][:, :],
                               c["carry"][:, 0:1], ALU.mult, ALU.add), "vS")
    X.lbl("scan2")


def dbg_ph2_gp(X, e):
    c, dr = X.c, X.dr
    X.wait(e, "scan2", "vS")
    X.wait(e, "wo_ld", "qwo")
    d = dr["dbg_sc"]
    X.idma(e.dma_start(out=bass.AP(d, 0, [[1, 1], [1, T]]),
                       in_=c["drow"][0:1, :], allow_slow_non_contiguous=True), "qdbg")
    X.idma(e.dma_start(out=bass.AP(d, (3 * T + 8) * 0 + T, [[3 * T + 8, 16], [1, T]]),
                       in_=c["dA"][:, :], allow_slow_non_contiguous=True), "qdbg")
    X.idma(e.dma_start(out=bass.AP(d, 2 * T, [[3 * T + 8, 16], [1, T]]),
                       in_=c["hs"][:, :], allow_slow_non_contiguous=True), "qdbg")
    X.idma(e.dma_start(out=bass.AP(d, 3 * T, [[3 * T + 8, 16], [1, 1]]),
                       in_=c["carry"][:, 0:1], allow_slow_non_contiguous=True), "qdbg")
    X.idma(e.dma_start(out=bass.AP(d, 3 * T + 1, [[3 * T + 8, 16], [1, 2]]),
                       in_=c["DcFc"][:, :], allow_slow_non_contiguous=True), "qdbg")
    if not X.pass1:
        e.wait_ge(X.sems["qdbg"].h, X.sems["qdbg"].n)


def ph3_sync(X, e):
    c, dr = X.c, X.dr
    X.wait(e, "sgsp14", "qsg0")
    X.wait(e, "sgsp15", "qsg1")
    for g in range(4):
        if g >= 2:
            X.wait(e, f"yg{g - 2}", "vS")
        src = bass.AP(dr["sg_sp"], g * 512, [[T, 128], [128 * T, DICH], [1, 512]])
        X.idma(e.dma_start(out=c["sgg"][g % 2][:, :], in_=src), f"qsg2{g % 2}")
        X.lbl(f"sgld{g}")
    for g in range(4):
        if g >= 1:
            X.wait(e, f"x2d{g - 1}", "vS")
        src = bass.AP(dr["xT"], PAD + g * 512,
                      [[TP, 128], [128 * TP, DCH], [1, 512]])
        X.idma(e.dma_start(out=c["xrel"][:, :], in_=src), "qxr")
        X.lbl(f"xrl{g}")
    X.wait(e, "x2sp3", "qx2s")
    X.wait(e, "macc_done", "vS")
    for oc in range(DCH):
        X.wait(e, f"h3sp3_{oc}", f"qh3{oc % 2}")
        src = bass.AP(dr["h3_sp"], oc * 128 * T, [[T, 128], [1, T]])
        X.idma(e.dma_start(out=c["h3"][:, oc * T:(oc + 1) * T], in_=src),
               "qh3r")
    X.lbl("h3rl")


def ph3_tensor(X, e):
    c, ps = X.c, X.ps
    X.wait(e, "scan2", "vS")
    ti = 0
    for g in range(4):
        for ic in range(DICH):
            bank = ti % 3
            if ti >= 3:
                X.wait(e, f"yev{ti - 3}", "vS")
            inst = e.matmul(ps[bank][:, 0:512],
                            c["cTs"][:, ic * 128:(ic + 1) * 128],
                            c["hs"][:, g * 512:(g + 1) * 512],
                            start=True, stop=True)
            X.inc(inst, "tS")
            X.lbl(f"cmm{ti}")
            ti += 1
    for g in range(4):
        g0 = PAD + g * 512
        if g > 0:
            X.wait(e, f"h3u{g - 1}_7", "vS")
        e.matmul(ps[6][:, 0:512], c["onesr_f"][0:1, :],
                 c["r1row"][0:1, g0:g0 + 512], start=True, stop=True)
        inst = e.matmul(ps[7][:, 0:512], c["onesr_f"][0:1, :],
                        c["mr1row"][0:1, g0:g0 + 512], start=True, stop=True)
        X.inc(inst, "tS")
        X.lbl(f"b1mm{g}")
        X.wait(e, f"yg{g}", "vS")
        if g == 0:
            X.wait(e, "wo_ld", "qwo")
        for oc in range(DCH):
            wi = g * DCH + oc
            bank = 3 + wi % 3
            if wi >= 3:
                X.wait(e, f"woev{wi - 3}", "vS")
            for k in range(DICH):
                inst = e.matmul(ps[bank][:, 0:512],
                                c["wo"][:, k * D + oc * 128:k * D + (oc + 1) * 128],
                                c["xs"][:, k * T + g * 512:k * T + (g + 1) * 512],
                                start=(k == 0), stop=(k == DICH - 1))
            X.inc(inst, "tS")
            X.lbl(f"womm{wi}")
        X.wait(e, f"b1ev{g}", "aS")
        for k in range(DCH):
            X.wait(e, f"x2b{g}_{k}", "aS")
            e.matmul(ps[6][:, 0:512], c["onesW"][:, :],
                     c["x2b"][k % 2][:, :], start=(k == 0), stop=(k == DCH - 1))
            inst = e.matmul(ps[7][:, 0:512], c["onesW"][:, :],
                            c["sqb"][k % 2][:, :], start=(k == 0),
                            stop=(k == DCH - 1))
            X.inc(inst, "tS")
            X.lbl(f"s3m{g}_{k}")
        X.lbl(f"s3mm{g}")
        X.wait(e, f"r3row{g}", "vS")
        e.matmul(ps[6][:, 0:512], c["onesr_f"][0:1, :],
                 c["s3r"][1][0:1, 0:512], start=True, stop=True)
        inst = e.matmul(ps[7][:, 0:512], c["onesr_f"][0:1, :],
                        c["s3r"][0][0:1, 0:512], start=True, stop=True)
        X.inc(inst, "tS")
        X.lbl(f"b2mm{g}")
    X.wait(e, "sxr_row", "vS")
    inst = e.matmul(ps[0][:, 0:1], c["onesr_f"][0:1, :],
                    c["s3r"][0][0:1, 0:1], start=True, stop=True)
    X.inc(inst, "tS")
    X.lbl("sxrmm")


def ph3_gpsimd(X, e):
    c, dr, ps = X.c, X.dr, X.ps
    X.wait(e, "macc_done", "vS")
    X.inc(e.tensor_reduce(c["s3r"][0][0:1, 0:1], c["macc"][:, 0:1],
                          axis=mybir.AxisListType.C, op=ALU.max), "gS")
    X.lbl("gmax_loc")
    X.fence(e, "gS")
    X.idma(e.dma_start(out=bass.AP(dr["ag_m1_i"], 0, [[1, 1], [1, 1]]),
                       in_=c["s3r"][0][0:1, 0:1]), "qagm1")
    if not X.pass1:
        e.wait_ge(X.sems["qagm1"].h, X.sems["qagm1"].n)
    X.inc(e.collective_compute(
        "AllGather", ALU.bypass, replica_groups=[list(range(NC))],
        ins=[bass.AP(dr["ag_m1_i"], 0, [[1, 1], [1, 1]])],
        outs=[bass.AP(dr["ag_m1_o"], 0, [[1, 1], [1, NC]])]), "cS")
    if not X.pass1:
        e.wait_ge(X.sems["cS"].h, X.sems["cS"].n)
    X.idma(e.dma_start(out=c["s3r"][1][0:1, 0:NC],
                       in_=bass.AP(dr["ag_m1_o"], 0, [[1, 1], [1, NC]])),
           "qagm1")
    X.lbl("m1_in")


def ph3_vector(X, e):
    c, ps = X.c, X.ps
    ti = 0
    for g in range(4):
        X.wait(e, f"sgld{g}", f"qsg2{g % 2}")
        for ic in range(DICH):
            bank = ti % 3
            e.tensor_mul(c["gscr"][ti % 2][:, :],
                         c["sgg"][g % 2][:, ic * 512:(ic + 1) * 512],
                         c["xs"][:, ic * T + g * 512:ic * T + (g + 1) * 512])
            X.inc(e.tensor_scalar_mul(c["gscr"][ti % 2][:, :],
                                      c["gscr"][ti % 2][:, :],
                                      vsl(X, "dv")[:, ic:ic + 1]), "v2S")
            X.lbl(f"sxd{ti}")
            X.wait(e, f"cmm{ti}", "tS")
            e.tensor_mul(c["yscr"][:, :], ps[bank][:, 0:512],
                         c["sgg"][g % 2][:, ic * 512:(ic + 1) * 512])
            X.inc(e.tensor_add(
                c["xs"][:, ic * T + g * 512:ic * T + (g + 1) * 512],
                c["yscr"][:, :], c["gscr"][ti % 2][:, :]), "vS")
            X.lbl(f"yev{ti}")
            ti += 1
        X.lbl(f"yg{g}")
    wi = 0
    for g in range(4):
        X.wait(e, f"xrl{g}", "qxr")
        X.wait(e, f"b1evs{g}", "aS")
        for oc in range(DCH):
            bank = 3 + wi % 3
            e.tensor_mul(c["hpre"][wi % 2][:, :],
                         c["xrel"][:, oc * 512:(oc + 1) * 512],
                         c["br1s"][:, :])
            X.inc(e.tensor_add(c["hpre"][wi % 2][:, :], c["hpre"][wi % 2][:, :],
                               c["bmr1s"][:, :]), "v2S")
            X.lbl(f"hpre{wi}")
            X.wait(e, f"womm{wi}", "tS")
            X.wait(e, f"haff{wi}", "aS")
            e.tensor_add(c["hpre"][wi % 2][:, :], c["hpre"][wi % 2][:, :],
                         ps[bank][:, 0:512])
            X.inc(e.scalar_tensor_tensor(
                c["x2g"][:, oc * 512:(oc + 1) * 512],
                c["hpre"][wi % 2][:, :], mcol(X, "gap", oc),
                c["xrel"][:, oc * 512:(oc + 1) * 512],
                ALU.mult, ALU.add), "vS")
            X.lbl(f"woev{wi}")
            wi += 1
        X.lbl(f"x2d{g}")
        X.wait(e, f"s3mm{g}", "tS")
        e.tensor_scalar_mul(c["s3r"][0][0:1, 0:512], ps[6][0:1, 0:512], 1.0 / D)
        e.tensor_scalar_mul(c["s3r"][1][0:1, 0:512], ps[7][0:1, 0:512], 1.0 / D)
        e.tensor_mul(c["bmr3s"][0:1, 0:512], c["s3r"][0][0:1, 0:512],
                     c["s3r"][0][0:1, 0:512])
        X.inc(e.tensor_sub(c["s3r"][1][0:1, 0:512], c["s3r"][1][0:1, 0:512],
                           c["bmr3s"][0:1, 0:512]), "vS")
        X.lbl(f"var3{g}")
        X.wait(e, f"r3a{g}", "aS")
        X.inc(e.scalar_tensor_tensor(c["s3r"][0][0:1, 0:512],
                                     c["s3r"][0][0:1, 0:512], -1.0,
                                     c["s3r"][1][0:1, 0:512],
                                     ALU.mult, ALU.mult), "vS")
        X.lbl(f"r3row{g}")
        def h3u(oc):
            X.wait(e, f"b2mm{g}", "tS")
            X.wait(e, f"s3m{g}_{oc}", "tS")
            if g > 0 or oc >= 2:
                X.wait(e, f"h3a{g if oc >= 2 else g - 1}_{(oc - 2) % DCH}",
                       "aS")
            e.tensor_mul(c["sqb"][oc % 2][:, :],
                         c["x2g"][:, oc * 512:(oc + 1) * 512], ps[6][:, 0:512])
            X.inc(e.tensor_tensor(c["sqb"][oc % 2][:, :], c["sqb"][oc % 2][:, :],
                                  ps[7][:, 0:512], ALU.add), "vS")
            X.lbl(f"h3u{g}_{oc}")

        def h3r(oc):
            X.wait(e, f"h3a{g}_{oc}", "aS")
            X.inc(e.reduce_max(c["maccw"][:, g * DCH + oc:g * DCH + oc + 1],
                               c["x2b"][oc % 2][:, :],
                               axis=AX, apply_absolute_value=True), "vS")
            X.lbl(f"h3r{g}_{oc}")

        h3u(0)
        h3u(1)
        for oc in range(2, DCH):
            h3r(oc - 2)
            h3u(oc)
        h3r(DCH - 2)
        h3r(DCH - 1)
        if g < 3:
            X.wait(e, f"x2sp{g}", "qx2s")
    X.sw(e, e.engine_nop(), "vS")
    X.inc(e.reduce_max(c["macc"][:, 0:1], c["maccw"][:, 0:32], axis=AX), "vS")
    if not X.pass1:
        e.wait_ge(X.sems["vS"].h, X.sems["vS"].n)
    X.lbl("macc_done")
    X.wait(e, "m1_in", "qagm1")
    X.sw(e, e.reduce_max(c["s3r"][0][0:1, 0:1], c["s3r"][1][0:1, 0:NC],
                         axis=AX), "vS")
    X.lbl("gmax1")
    X.wait(e, "sig1", "aS")
    X.sw(e, e.tensor_scalar_add(c["tiny"][0:1, 1:2], c["tiny"][0:1, 1:2], 1.0),
         "vS")
    X.sw(e, e.reciprocal(c["tiny"][0:1, 1:2], c["tiny"][0:1, 1:2]), "vS")
    X.sw(e, e.tensor_mul(c["s3r"][0][0:1, 0:1], c["s3r"][0][0:1, 0:1],
                         c["tiny"][0:1, 1:2]), "vS")
    X.sw(e, e.tensor_scalar_mul(c["s3r"][0][0:1, 0:1], c["s3r"][0][0:1, 0:1],
                                1.0 / 127.0), "vS")
    X.inc(e.reciprocal(c["s3r"][0][0:1, 0:1], c["s3r"][0][0:1, 0:1]), "vS")
    X.lbl("sxr_row")
    X.wait(e, "sxrmm", "tS")
    X.sw(e, e.tensor_copy(c["sxr"][:, :], ps[0][:, 0:1]), "vS")
    X.sw(e, e.reciprocal(c["s2r"][:, 0:1], c["sxr"][:, 0:1]), "vS")
    X.sw(e, e.tensor_scalar_mul(c["cdg"][:, :], c["scg"][:, :],
                                c["s2r"][:, 0:1]), "vS")
    X.sw(e, e.tensor_scalar_mul(c["cdg"][:, :], c["cdg"][:, :], 1.0 / 127.0),
         "vS")
    X.sw(e, e.tensor_scalar_mul(c["cdu"][:, :], c["scu"][:, :],
                                c["s2r"][:, 0:1]), "vS")
    X.inc(e.tensor_scalar_mul(c["cdu"][:, :], c["cdu"][:, :], 1.0 / 127.0),
          "vS")
    X.lbl("cdg_done")


def ph3_scalar(X, e):
    c, dr, ps = X.c, X.dr, X.ps
    for g in range(4):
        X.wait(e, f"b1mm{g}", "tS")
        e.activation(c["br1s"][:, :], ps[6][:, 0:512], AF.Identity)
        X.inc(e.activation(c["bmr1s"][:, :], ps[7][:, 0:512], AF.Identity),
              "aS")
        X.lbl(f"b1evs{g}")
        X.lbl(f"b1ev{g}")
        if X.dbg == "pfull" and g == 0:
            X.fence(e, "aS")
            X.idma(e.dma_start(out=X.dr["dbg_b1"][:, 0:512],
                               in_=c["br1s"][:, :]), "qdbgb")
            X.idma(e.dma_start(out=X.dr["dbg_b1"][:, 512:1024],
                               in_=c["bmr1s"][:, :]), "qdbgb")
        for oc in range(DCH):
            wi = g * DCH + oc
            X.wait(e, f"hpre{wi}", "v2S")
            X.inc(e.activation(c["hpre"][wi % 2][:, :], c["hpre"][wi % 2][:, :],
                               AF.Identity, bias=mcol(X, "sa", oc),
                               scale=mcol(X, "ca1", oc)), "aS")
            X.lbl(f"haff{wi}")
        for oc in range(DCH):
            wi = g * DCH + oc
            X.wait(e, f"woev{wi}", "vS")
            if oc >= 2:
                X.wait(e, f"s3m{g}_{oc - 2}", "tS")
            elif g > 0:
                X.wait(e, f"h3r{g - 1}_{DCH - 2 + oc}", "vS")
                X.wait(e, f"h3sp{g - 1}_{DCH - 2 + oc}", f"qh3{oc % 2}")
            e.activation(c["x2b"][oc % 2][:, :],
                         c["x2g"][:, oc * 512:(oc + 1) * 512], AF.Identity)
            X.inc(e.activation(c["sqb"][oc % 2][:, :], c["x2b"][oc % 2][:, :],
                               AF.Square), "aS")
            X.lbl(f"x2b{g}_{oc}")
        X.wait(e, f"var3{g}", "vS")
        e.activation(c["s3r"][1][0:1, 0:512], c["s3r"][1][0:1, 0:512],
                     AF.Ln, bias=EPS)
        X.inc(e.activation(c["s3r"][1][0:1, 0:512], c["s3r"][1][0:1, 0:512],
                           AF.Exp, scale=-0.5), "aS")
        X.lbl(f"r3a{g}")
        for oc in range(DCH):
            X.wait(e, f"h3u{g}_{oc}", "vS")
            if g > 0 or oc >= 2:
                po, pg = (oc - 2, g) if oc >= 2 else (DCH - 2 + oc, g - 1)
                X.wait(e, f"h3sp{pg}_{po}", f"qh3{oc % 2}")
                X.wait(e, f"h3r{pg}_{po}", "vS")
            X.inc(e.activation(c["x2b"][oc % 2][:, :], c["sqb"][oc % 2][:, :],
                               AF.Identity, bias=mcol(X, "sb", oc),
                               scale=mcol(X, "cb1", oc)), "aS")
            X.lbl(f"h3a{g}_{oc}")
            X.fence(e, "aS")
            dst = bass.AP(dr["h3_sp"], (oc * 128) * T + g * 512,
                          [[T, 128], [1, 512]])
            X.idma(e.dma_start(out=dst, in_=c["x2b"][oc % 2][:, :]),
                   f"qh3{oc % 2}")
            X.lbl(f"h3sp{g}_{oc}")
        for oc in range(DCH):
            dst = bass.AP(dr["x2_sp"], (oc * 128) * T + g * 512,
                          [[T, 128], [1, 512]])
            X.idma(e.dma_start(out=dst, in_=c["x2g"][:, oc * 512:(oc + 1) * 512]),
                   "qx2s")
        X.lbl(f"x2sp{g}")
    X.inc(e.activation(c["tiny"][0:1, 1:2], c["clip"][0:1, 0:1], AF.Exp,
                       scale=-1.0), "aS")
    X.lbl("sig1")


def dbg_ph3_gp(X, e):
    c, dr = X.c, X.dr
    X.wait(e, "cdg_done", "vS")
    X.idma(e.dma_start(out=dr["dbg_wo"][:, :], in_=c["wo"][:, :]), "qdbg")
    X.idma(e.dma_start(out=dr["dbg_y"][:, :], in_=c["xs"][:, :]), "qdbg")
    X.idma(e.dma_start(out=bass.AP(dr["dbg_wo"], 0, [[1, 1], [1, 1]]),
                       in_=c["cTs"][0:1, 0:1]) if False else e.engine_nop(),
           "qdbg") if False else None
    X.idma(e.dma_start(out=dr["dbg_ct"][:, :], in_=c["cTs"][:, :]), "qdbg")
    d = dr["dbg_p3"]
    X.idma(e.dma_start(out=bass.AP(d, 0, [[16, 128], [1, 1]]),
                       in_=c["sxr"][:, :], allow_slow_non_contiguous=True), "qdbg")
    X.idma(e.dma_start(out=bass.AP(d, 1, [[16, 128], [1, 8]]),
                       in_=c["cdg"][:, 0:8], allow_slow_non_contiguous=True), "qdbg")
    if not X.pass1:
        e.wait_ge(X.sems["qdbg"].h, X.sems["qdbg"].n)


def msl(X, k):
    if k < 16:
        return X.c["m_a"], k * T
    return X.c["m_b"], (k - 16) * T


def ph5_sync(X, e):
    c, dr = X.c, X.dr
    X.wait(e, f"d_qsp{IFCH - 1}", "qqs2")
    for oc in range(IFCH):
        if oc >= 2:
            X.wait(e, f"fmm{oc - 2}", "tS")
        src = bass.AP(dr["wq_g"], oc * DCH * 128 * 128, [[DCH * 128, 128], [1, DCH * 128]])
        X.idma(e.dma_start(out=c["gcol"][oc % 2][:, :], in_=src), f"qfg{oc % 2}")
        src = bass.AP(dr["wq_u"], oc * DCH * 128 * 128, [[DCH * 128, 128], [1, DCH * 128]])
        X.idma(e.dma_start(out=c["ucol"][oc % 2][:, :], in_=src), f"qfu{oc % 2}")
        X.lbl(f"fld{oc}")


def ph5_vector(X, e):
    c, ps = X.c, X.ps
    ti = 0
    for g in range(4):
        for k in range(DCH):
            X.wait(e, f"xra{ti}", "aS")
            X.inc(e.tensor_scalar(
                c["h3"][:, k * T + g * 512:k * T + (g + 1) * 512],
                c["obuf"][ti % 2][:, :], 255.0, 128.0,
                ALU.min, ALU.subtract), "vS")
            X.lbl(f"xr{ti}")
            ti += 1
    X.lbl("xr_all")
    fi = 0
    for oc in range(IFCH):
        mt, mb = msl(X, oc)
        for g in range(4):
            bank = 2 * (fi % 2)
            X.wait(e, f"sgf{fi}", "aS")
            X.wait(e, f"umm{fi}", "tS")
            X.inc(e.scalar_tensor_tensor(
                mt[:, mb + g * 512:mb + (g + 1) * 512],
                ps[bank + 1][:, 0:512], c["cdu"][:, oc:oc + 1],
                c["hpre"][fi % 2][:, :], ALU.mult, ALU.mult), "vS")
            X.lbl(f"mev{fi}")
            X.inc(e.reduce_max(c["maccw"][:, 32 + fi:33 + fi],
                               mt[:, mb + g * 512:mb + (g + 1) * 512],
                               axis=AX, apply_absolute_value=True), "vS")
            X.lbl(f"mred{fi}")
            fi += 1
    X.sw(e, e.engine_nop(), "vS")
    X.inc(e.reduce_max(c["macc"][:, 1:2], c["maccw"][:, 32:120], axis=AX),
          "vS")
    if not X.pass1:
        e.wait_ge(X.sems["vS"].h, X.sems["vS"].n)
    X.lbl("m_done")


def ph5_gpsimd(X, e):
    c, dr = X.c, X.dr
    X.wait(e, "m_done", "vS")
    X.inc(e.tensor_reduce(c["s3r"][0][0:1, 1:2], c["macc"][:, 1:2],
                          axis=mybir.AxisListType.C, op=ALU.max), "gS")
    X.fence(e, "gS")
    X.idma(e.dma_start(out=bass.AP(dr["ag_m2_i"], 0, [[1, 1], [1, 1]]),
                       in_=c["s3r"][0][0:1, 1:2]), "qagm2")
    if not X.pass1:
        e.wait_ge(X.sems["qagm2"].h, X.sems["qagm2"].n)
    X.inc(e.collective_compute(
        "AllGather", ALU.bypass, replica_groups=[list(range(NC))],
        ins=[bass.AP(dr["ag_m2_i"], 0, [[1, 1], [1, 1]])],
        outs=[bass.AP(dr["ag_m2_o"], 0, [[1, 1], [1, NC]])]), "cS")
    if not X.pass1:
        e.wait_ge(X.sems["cS"].h, X.sems["cS"].n)
    X.idma(e.dma_start(out=c["s3r"][1][0:1, 0:NC],
                       in_=bass.AP(dr["ag_m2_o"], 0, [[1, 1], [1, NC]])),
           "qagm2")
    X.lbl("m2_in")


def ph5_scalar(X, e):
    c, ps = X.c, X.ps
    X.wait(e, "h3rl", "qh3r")
    X.wait(e, "cdg_done", "vS")
    ti = 0
    for g in range(4):
        for k in range(DCH):
            if ti >= 2:
                X.wait(e, f"xr{ti - 2}", "vS")
            e.activation(c["obuf"][ti % 2][:, :],
                         c["h3"][:, k * T + g * 512:k * T + (g + 1) * 512],
                         AF.Identity, scale=c["sxr"][:, 0:1], bias=MAGIC)
            X.inc(e.activation(c["obuf"][ti % 2][:, :], c["obuf"][ti % 2][:, :],
                               AF.Relu, bias=128.0 - MAGIC), "aS")
            X.lbl(f"xra{ti}")
            ti += 1
    X.inc(e.activation(c["tiny"][0:1, 0:1], c["clip"][0:1, 1:2], AF.Exp,
                       scale=-1.0), "aS")
    X.lbl("sig2")
    fi = 0
    for oc in range(IFCH):
        for g in range(4):
            bank = 2 * (fi % 2)
            X.wait(e, f"gmm{fi}", "tS")
            if fi >= 2:
                X.wait(e, f"mev{fi - 2}", "vS")
            X.inc(e.activation(c["hpre"][fi % 2][:, :], ps[bank][:, 0:512],
                               AF.Silu, scale=c["cdg"][:, oc:oc + 1]), "aS")
            X.lbl(f"sgf{fi}")
            fi += 1


def ph5_tensor(X, e):
    c, ps = X.c, X.ps
    fi = 0
    for oc in range(IFCH):
        X.wait(e, f"fld{oc}", f"qfg{oc % 2}", f"qfu{oc % 2}")
        for g in range(4):
            bank = 2 * (fi % 2)
            if fi >= 2:
                X.wait(e, f"mev{fi - 2}", "vS")
                X.wait(e, f"sgf{fi - 2}", "aS")
            if oc == 0:
                X.wait(e, f"xr{g * DCH + DCH - 1}", "vS")
            for k in range(DCH):
                inst = e.matmul(ps[bank][:, 0:512],
                                c["gcol"][oc % 2][:, k * 128:(k + 1) * 128],
                                c["h3"][:, k * T + g * 512:k * T + (g + 1) * 512],
                                start=(k == 0), stop=(k == DCH - 1))
            X.inc(inst, "tS")
            X.lbl(f"gmm{fi}")
            for k in range(DCH):
                inst = e.matmul(ps[bank + 1][:, 0:512],
                                c["ucol"][oc % 2][:, k * 128:(k + 1) * 128],
                                c["h3"][:, k * T + g * 512:k * T + (g + 1) * 512],
                                start=(k == 0), stop=(k == DCH - 1))
            X.inc(inst, "tS")
            X.lbl(f"umm{fi}")
            fi += 1
        X.lbl(f"fmm{oc}")


def _noop_t(e):
    return e.engine_nop()


def ph6_sync(X, e):
    c, dr = X.c, X.dr
    X.wait(e, "x2sp3", "qx2s")
    X.wait(e, f"fmm{IFCH - 1}", "tS")
    for g in range(4):
        if g >= 2:
            X.wait(e, f"outg{g - 2}", "vS")
        src = bass.AP(dr["x2_sp"], g * 512, [[T, 128], [128 * T, DCH], [1, 512]])
        X.idma(e.dma_start(out=c["x2rl"][:, (g % 2) * DCH * 512:
                                         (g % 2) * DCH * 512 + DCH * 512],
                           in_=src), f"qx2r{g % 2}")
        X.lbl(f"x2rl{g}")
        for oc in range(DCH):
            di = g * DCH + oc
            if di >= 2:
                X.wait(e, f"dmmt{di - 2}", "tS")
            src = bass.AP(dr["wq_d"], oc * IFCH * 128 * 128,
                          [[IFCH * 128, 128], [1, IFCH * 128]])
            X.idma(e.dma_start(out=c["dcol"][di % 2][:, :], in_=src),
                   f"qfd{di % 2}")
            X.lbl(f"dld{di}")


def ph6_vector(X, e):
    c, ps = X.c, X.ps
    X.wait(e, "m2_in", "qagm2")
    X.sw(e, e.reduce_max(c["s3r"][0][0:1, 1:2], c["s3r"][1][0:1, 0:NC],
                         axis=AX), "vS")
    X.wait(e, "sig2", "aS")
    X.sw(e, e.tensor_scalar_add(c["tiny"][0:1, 0:1], c["tiny"][0:1, 0:1], 1.0),
         "vS")
    X.sw(e, e.reciprocal(c["tiny"][0:1, 0:1], c["tiny"][0:1, 0:1]), "vS")
    X.sw(e, e.tensor_mul(c["s3r"][0][0:1, 1:2], c["s3r"][0][0:1, 1:2],
                         c["tiny"][0:1, 0:1]), "vS")
    X.inc(e.tensor_scalar_mul(c["s3r"][0][0:1, 1:2], c["s3r"][0][0:1, 1:2],
                              1.0 / 127.0), "vS")
    if not X.pass1:
        e.wait_ge(X.sems["vS"].h, X.sems["vS"].n)
    X.lbl("s2_row")
    X.wait(e, "s2mm", "tS")
    X.sw(e, e.tensor_copy(c["s2r"][:, 0:1], ps[6][:, 0:1]), "vS")
    X.sw(e, e.tensor_scalar_mul(c["cdd"][:, :], c["scd"][:, :],
                                c["s2r"][:, 0:1]), "vS")
    X.sw(e, e.tensor_scalar_mul(c["cdd"][:, :], c["cdd"][:, :], 1.0 / 127.0),
         "vS")
    X.sw(e, e.tensor_mul(c["cdd"][:, :], c["cdd"][:, :], c["modd"][:, 24:32]),
         "vS")
    X.sw(e, e.reciprocal(c["s2r"][:, 0:1], c["s2r"][:, 0:1]), "vS")
    X.lbl("cdd_done")
    if X.dbg == "pfull":
        X.wait(e, "m5_dump", "qdbgm")
    ti = 0
    di = 0
    for g in range(4):
        for k in range(IFCH):
            mt, mb = msl(X, k)
            X.wait(e, f"h2a{ti}", "aS")
            X.inc(e.tensor_scalar(
                mt[:, mb + g * 512:mb + (g + 1) * 512],
                c["hpre"][ti % 2][:, :], 255.0, 128.0,
                ALU.min, ALU.subtract), "vS")
            X.lbl(f"h2r{ti}")
            ti += 1
        for oc in range(DCH):
            bank = 6 + di % 2
            X.wait(e, f"dmmt{di}", "tS")
            X.wait(e, f"x2rl{g}", f"qx2r{g % 2}")
            X.inc(e.scalar_tensor_tensor(
                c["yscr"][:, :] if False else c["obuf"][di % 2][:, :],
                ps[bank][:, 0:512], c["cdd"][:, oc:oc + 1],
                c["x2rl"][:, (g % 2) * DCH * 512 + oc * 512:
                          (g % 2) * DCH * 512 + (oc + 1) * 512],
                ALU.mult, ALU.add), "vS")
            X.lbl(f"oev{di}")
            di += 1
        X.lbl(f"outg{g}")


def ph6_gpsimd(X, e):
    c, dr = X.c, X.dr
    di = 0
    for g in range(4):
        for oc in range(DCH):
            X.wait(e, f"oev{di}", "vS")
            dst = bass.AP(dr["outT"], (oc * 128) * T + g * 512,
                          [[T, 128], [1, 512]])
            X.idma(e.dma_start(out=dst, in_=c["obuf"][di % 2][:, :]),
                   f"qout{di % 2}")
            X.lbl(f"ost{di}")
            di += 1
    if not X.pass1:
        e.wait_ge(X.sems["qout0"].h, X.sems["qout0"].n)
        e.wait_ge(X.sems["qout1"].h, X.sems["qout1"].n)


def ph6_tensor(X, e):
    c, ps = X.c, X.ps
    X.wait(e, "s2_row", "vS")
    inst = e.matmul(ps[6][:, 0:1], c["onesr_f"][0:1, :],
                    c["s3r"][0][0:1, 1:2], start=True, stop=True)
    X.inc(inst, "tS")
    X.lbl("s2mm")
    X.wait(e, "cdd_done", "vS")
    di = 0
    for g in range(4):
        for oc in range(DCH):
            bank = 6 + di % 2
            X.wait(e, f"dld{di}", f"qfd{di % 2}")
            if di >= 2:
                X.wait(e, f"oev{di - 2}", "vS")
            for k in range(IFCH):
                mt, mb = msl(X, k)
                X.wait(e, f"h2r{g * IFCH + k}", "vS")
                inst = e.matmul(ps[bank][:, 0:512],
                                c["dcol"][di % 2][:, k * 128:(k + 1) * 128],
                                mt[:, mb + g * 512:mb + (g + 1) * 512],
                                start=(k == 0), stop=(k == IFCH - 1))
            X.inc(inst, "tS")
            X.lbl(f"dmmt{di}")
            di += 1


def ph6_scalar(X, e):
    c, ps = X.c, X.ps
    X.wait(e, "m_done", "vS")
    X.wait(e, "cdd_done", "vS")
    ti = 0
    for g in range(4):
        for k in range(IFCH):
            mt, mb = msl(X, k)
            if ti >= 2:
                X.wait(e, f"h2r{ti - 2}", "vS")
            e.activation(c["hpre"][ti % 2][:, :],
                         mt[:, mb + g * 512:mb + (g + 1) * 512],
                         AF.Identity, scale=c["s2r"][:, 0:1], bias=MAGIC)
            X.inc(e.activation(c["hpre"][ti % 2][:, :], c["hpre"][ti % 2][:, :],
                               AF.Relu, bias=128.0 - MAGIC), "aS")
            X.lbl(f"h2a{ti}")
            ti += 1


def dbg_ph5_gp(X, e):
    c, dr = X.c, X.dr
    X.wait(e, "m_done", "vS")
    X.wait(e, "xr_all", "vS")
    X.idma(e.dma_start(out=dr["dbg_m"][:, 0:16 * T], in_=c["m_a"][:, :]), "qdbg")
    X.idma(e.dma_start(out=dr["dbg_m"][:, 16 * T:], in_=c["m_b"][:, :]), "qdbg")
    X.idma(e.dma_start(out=dr["dbg_xr"][:, :], in_=c["h3"][:, :]), "qdbg")
    if not X.pass1:
        e.wait_ge(X.sems["qdbg"].h, X.sems["qdbg"].n)


def dbg_pfull_m(X, e):
    c, dr = X.c, X.dr
    X.wait(e, "m_done", "vS")
    X.wait(e, "xr_all", "vS")
    X.idma(e.dma_start(out=dr["dbg_m5"][:, 0:16 * T], in_=c["m_a"][:, :]),
           "qdbgm")
    X.idma(e.dma_start(out=dr["dbg_m5"][:, 16 * T:22 * T], in_=c["m_b"][:, :]),
           "qdbgm")
    X.lbl("m5_dump")


def dbg_pfull_end(X, e):
    c, dr = X.c, X.dr
    X.wait(e, "ost30", "qout0")
    X.wait(e, "ost31", "qout1")
    X.idma(e.dma_start(out=dr["dbg_hr"][:, 0:16 * T], in_=c["m_a"][:, :]),
           "qdbgf")
    X.idma(e.dma_start(out=dr["dbg_hr"][:, 16 * T:22 * T], in_=c["m_b"][:, :]),
           "qdbgf")
    X.idma(e.dma_start(out=dr["dbg_x2e"][:, :], in_=c["h3"][:, :]), "qdbgf")
    X.idma(e.dma_start(out=dr["dbg_sc2"][:, 0:DCH], in_=c["cdd"][:, :]),
           "qdbgf")
    X.idma(e.dma_start(out=dr["dbg_sc2"][:, DCH:DCH + 1], in_=c["s2r"][:, 0:1],
                       allow_slow_non_contiguous=True), "qdbgf")
    if not X.pass1:
        e.wait_ge(X.sems["qdbgf"].h, X.sems["qdbgf"].n)


def dbg_pfull_y(X, e):
    c, dr = X.c, X.dr
    X.wait(e, "cdg_done", "vS")
    X.wait(e, "macc_done", "vS")
    X.idma(e.dma_start(out=dr["dbg_y"][:, :], in_=c["xs"][:, :]), "qdbgy")
    if not X.pass1:
        e.wait_ge(X.sems["qdbgy"].h, X.sems["qdbgy"].n)


# version 21 (final)
# speedup vs baseline: 1.0219x; 1.0219x over previous
import numpy as np
import concourse.bass as bass
import concourse.mybir as mybir
from concourse.bass_utils import run_bass_kernel_spmd

F32 = mybir.dt.float32
BF16 = mybir.dt.bfloat16
AF = mybir.ActivationFunctionType
ALU = mybir.AluOpType
AX = mybir.AxisListType.X

D, DI, NS, IF = 1024, 2048, 16, 2816
T, PAD = 2048, 3
TP = T + PAD
NC = 8
DCH, DICH, IFCH = D // 128, DI // 128, IF // 128
WSH = 6 * D // NC
EPS = 1e-5
MAGIC = float(1.5 * 2 ** 23)

GRPS = [(0, PAD)] + [(PAD + 512 * g, 512) for g in range(4)]
P0G = [(0, 0, 512), (0, PAD, 512), (1, 0, 512), (2, 0, 512), (3, 0, 512)]
P0L = [(0, 515), (515, 512), (1027, 512), (1539, 512)]

DBG = False
_CACHE = {}
_FCONSTS = [EPS, -0.5, -1.0]
_FCONSTS128 = [MAGIC, -MAGIC, 128.0 - MAGIC]


def esz(dt):
    return {F32: 4, BF16: 2}[dt]


class _MockInst:
    def then_inc(self, sem, k):
        return self


class _MockEng:
    def __getattr__(self, name):
        def f(*a, **k):
            return _MockInst()
        return f


class Sem:
    def __init__(self, nc, name):
        self.h = nc.semaphore(name).__enter__()
        self.name = name
        self.n = 0


class Ctx:

    def __init__(self, nc):
        self.nc = nc
        self.c = {}
        self.dr = {}
        self.ps = []
        self.sems = {}
        self.labels = {}
        self.pass1 = True
        self.dbg = False
        self.scratch = {}

    def sem(self, name):
        if name not in self.sems:
            self.sems[name] = Sem(self.nc, name)
        return self.sems[name]

    def inc(self, inst, semname, k=1):
        s = self.sem(semname)
        s.n += k
        if not self.pass1:
            inst.then_inc(s.h, k)
        return inst

    def idma(self, inst, semname):
        return self.inc(inst, semname, 16)

    def lbl(self, name):
        if self.pass1:
            assert name not in self.labels, f"dup label {name}"
            self.labels[name] = {k: s.n for k, s in self.sems.items()}

    def fence(self, e, semname):
        if not self.pass1:
            e.wait_ge(self.sems[semname].h, self.sems[semname].n)

    def sw(self, e, inst, semname):
        self.inc(inst, semname)
        if not self.pass1:
            e.wait_ge(self.sems[semname].h, self.sems[semname].n)
        return inst

    def wait(self, e, label, *semnames):
        if self.pass1:
            return
        snap = self.labels[label]
        for sn in semnames:
            v = snap.get(sn, 0)
            if v > 0:
                e.wait_ge(self.sems[sn].h, v)

    def run_phases(self, phases):
        self.pass1 = True
        me = _MockEng()
        for name, secs in phases:
            for engname, fn in secs:
                fn(self, me)
        for s in self.sems.values():
            s.n = 0
        self.scratch = {}
        self.pass1 = False
        for name, secs in phases:
            with self.nc.Block(name) as blk:
                for engname, fn in secs:
                    getattr(blk, engname)(lambda e, fn=fn: fn(self, e))


class Arena:
    def __init__(self, nc, base, size, tag):
        self.nc, self.base, self.size, self.tag = nc, base, size, tag
        self.off = base
        self.pos = {}

    def alloc(self, name, shape, dtype, align=32):
        nbytes = int(np.prod(shape[1:])) * esz(dtype)
        off = (self.off + align - 1) // align * align
        assert off + nbytes <= self.base + self.size, \
            f"{self.tag}:{name} overflow by {off + nbytes - self.base - self.size}"
        t = self.nc.alloc_sbuf_tensor_at(name, list(shape), dtype, offset=off)
        self.pos[name] = off
        self.off = off + nbytes
        return t


def vsl(X, name):
    off = {"gam": 0, "bet": 8, "cw": 16, "cb": 80, "dv": 96, "bd": 112,
           "cvec": 128, "mflag": 136}[name]
    w = {"gam": 8, "bet": 8, "cw": 64, "cb": 16, "dv": 16, "bd": 16,
         "cvec": 8, "mflag": 1}[name]
    return X.c["vec"][:, off:off + w]


def mcol(X, name, k):
    base = {"sa": ("mod", 0), "ca": ("mod", 8), "ga": ("mod", 16),
            "sb": ("mod", 24), "cb2": ("mod", 32), "gb": ("mod", 40),
            "ca1": ("modd", 0), "cb1": ("modd", 8), "gap": ("modd", 16),
            "gbp": ("modd", 24), "gc": ("modd", 32), "gs": ("modd", 40)}[name]
    return X.c[base[0]][:, base[1] + k:base[1] + k + 1]


def p0xap(X, buf, k, w, off=0):
    return X.c["p0x"][:, buf * DCH * 515 + k * 515 + off:
                      buf * DCH * 515 + k * 515 + off + w]


def sb3(t, base, stride, nch, w, npart=128):
    freelen = int(t.shape[1])
    return bass.AP(t, base, [[freelen, npart], [stride, nch], [1, w]])


def p0grp(X, tname, buf, w, stride=515, nch=DCH):
    t = X.c[tname]
    return sb3(t, buf * nch * stride, stride, nch, w)


def build_program(dbg=False):
    nc = bass.Bass("TRN2", target_bir_lowering=False, detect_race_conditions=False)
    X = Ctx(nc)
    X.dbg = dbg
    dr, c = X.dr, X.c

    dr["xT"] = nc.dram_tensor("xT", [D, TP], F32, kind="ExternalInput")
    dr["w_in"] = nc.dram_tensor("w_in", [D, 2 * DI], F32, kind="ExternalInput")
    dr["wd"] = nc.dram_tensor("wd", [DI, DI], F32, kind="ExternalInput")
    dr["w_out"] = nc.dram_tensor("w_out", [DI, D], F32, kind="ExternalInput")
    dr["bT_t"] = nc.dram_tensor("bT_t", [128, NS * DICH], F32, kind="ExternalInput")
    dr["cT"] = nc.dram_tensor("cT", [NS, DI], F32, kind="ExternalInput")
    dr["wmod_sl"] = nc.dram_tensor("wmod_sl", [D, WSH], F32, kind="ExternalInput")
    dr["wgT"] = nc.dram_tensor("wgT", [D, IF], F32, kind="ExternalInput")
    dr["wuT"] = nc.dram_tensor("wuT", [D, IF], F32, kind="ExternalInput")
    dr["wdnT"] = nc.dram_tensor("wdnT", [IF, D], F32, kind="ExternalInput")
    dr["wgN"] = nc.dram_tensor("wgN", [IF, D], F32, kind="ExternalInput")
    dr["wuN"] = nc.dram_tensor("wuN", [IF, D], F32, kind="ExternalInput")
    dr["wdnN"] = nc.dram_tensor("wdnN", [D, IF], F32, kind="ExternalInput")
    dr["vecs"] = nc.dram_tensor("vecs", [128, 137], F32, kind="ExternalInput")
    dr["msks"] = nc.dram_tensor("msks", [16, 17], F32, kind="ExternalInput")
    dr["alog_in"] = nc.dram_tensor("alog_in", [NS, 1], F32, kind="ExternalInput")
    dr["clip_in"] = nc.dram_tensor("clip_in", [1, 2], F32, kind="ExternalInput")
    dr["outT"] = nc.dram_tensor("outT", [D, T], F32, kind="ExternalOutput")

    dr["sg_sp"] = nc.dram_tensor("sg_sp", [DICH, 128, T], BF16,
                                 kind="ExternalOutput" if dbg == "ph1"
                                 else "Internal")
    dr["x2_sp"] = nc.dram_tensor("x2_sp", [DCH, 128, T], F32,
                                 kind="ExternalOutput" if dbg in ("ph3", "pfull")
                                 else "Internal")
    dr["wq_g"] = nc.dram_tensor("wq_g", [IFCH, DCH, 128, 128], BF16)
    dr["wq_u"] = nc.dram_tensor("wq_u", [IFCH, DCH, 128, 128], BF16)
    dr["wq_d"] = nc.dram_tensor("wq_d", [DCH, IFCH, 128, 128], BF16)
    dr["ag_mod_i"] = nc.dram_tensor("ag_mod_i", [1, WSH], F32)
    dr["ag_mod_o"] = nc.dram_tensor("ag_mod_o", [NC, WSH], F32)
    dr["ag_car_i"] = nc.dram_tensor("ag_car_i", [1, 32], F32)
    dr["ag_car_o"] = nc.dram_tensor("ag_car_o", [NC, 32], F32)
    dr["ag_m1_i"] = nc.dram_tensor("ag_m1_i", [1, 8], F32)
    dr["ag_m1_o"] = nc.dram_tensor("ag_m1_o", [NC, 8], F32)
    dr["ag_m2_i"] = nc.dram_tensor("ag_m2_i", [1, 8], F32)
    dr["ag_m2_o"] = nc.dram_tensor("ag_m2_o", [NC, 8], F32)
    dr["srow2"] = nc.dram_tensor("srow2", [1, IF], F32, kind="Internal")
    dr["srow"] = nc.dram_tensor("srow", [1, 4096], F32)
    dr["h3_sp"] = nc.dram_tensor("h3_sp", [DCH, 128, T], BF16,
                                 kind="ExternalOutput" if dbg in ("ph3", "pfull")
                                 else "Internal")
    if dbg:
        dr["dbg_h2"] = nc.dram_tensor("dbg_h2", [128, DCH * TP], F32,
                                      kind="ExternalOutput")
        dr["dbg_mod"] = nc.dram_tensor("dbg_mod", [128, 96], F32,
                                       kind="ExternalOutput")
        dr["dbg_rows"] = nc.dram_tensor("dbg_rows", [1, 2 * TP], F32,
                                        kind="ExternalOutput")
        dr["dbg_h"] = nc.dram_tensor("dbg_h", [128, DCH * 515], F32,
                                     kind="ExternalOutput")
        if dbg == "ph1":
            dr["dbg_xs"] = nc.dram_tensor("dbg_xs", [128, DICH * T], F32,
                                          kind="ExternalOutput")
        if dbg == "ph2":
            dr["dbg_sc"] = nc.dram_tensor("dbg_sc", [16, 3 * T + 8], F32,
                                          kind="ExternalOutput")
        if dbg == "pfull":
            dr["dbg_y"] = nc.dram_tensor("dbg_y", [128, DICH * T], F32,
                                         kind="ExternalOutput")
            dr["dbg_b1"] = nc.dram_tensor("dbg_b1", [128, 1026], F32,
                                          kind="ExternalOutput")
            dr["dbg_hr"] = nc.dram_tensor("dbg_hr", [128, IFCH * T], F32,
                                          kind="ExternalOutput")
            dr["dbg_x2e"] = nc.dram_tensor("dbg_x2e", [128, DCH * T], BF16,
                                           kind="ExternalOutput")
            dr["dbg_sc2"] = nc.dram_tensor("dbg_sc2", [128, 16], F32,
                                           kind="ExternalOutput")
            dr["dbg_m5"] = nc.dram_tensor("dbg_m5", [128, IFCH * T], F32,
                                          kind="ExternalOutput")
        if dbg == "ph5":
            dr["dbg_m"] = nc.dram_tensor("dbg_m", [128, IFCH * T], F32,
                                         kind="ExternalOutput")
            dr["dbg_xr"] = nc.dram_tensor("dbg_xr", [128, DCH * T], F32,
                                          kind="ExternalOutput")
        if dbg == "ph3":
            dr["dbg_p3"] = nc.dram_tensor("dbg_p3", [128, 16], F32,
                                          kind="ExternalOutput")
            dr["dbg_wo"] = nc.dram_tensor("dbg_wo", [128, DICH * D], F32,
                                          kind="ExternalOutput")
            dr["dbg_y"] = nc.dram_tensor("dbg_y", [128, DICH * T], F32,
                                         kind="ExternalOutput")
            dr["dbg_ct"] = nc.dram_tensor("dbg_ct", [NS, DI], F32,
                                          kind="ExternalOutput")

    MB = 17408
    SZ_CONST, SZ_ROWS, SZ_STRM, SZ_H2, SZ_W4, SZ_XS = \
        7424, 37632, 34816, 32816, 32768, 65632
    A_CONST = Arena(nc, MB, SZ_CONST, "const")
    A_ROWS = Arena(nc, MB + SZ_CONST, SZ_ROWS, "rows")
    A_STRM = Arena(nc, MB + SZ_CONST + SZ_ROWS, SZ_STRM, "strm")
    R_H2 = MB + SZ_CONST + SZ_ROWS + SZ_STRM
    R_W4 = (R_H2 + SZ_H2 + 31) // 32 * 32
    R_XS = (R_W4 + SZ_W4 + 31) // 32 * 32
    assert R_XS + SZ_XS <= 229344, R_XS + SZ_XS
    X.A_STRM = A_STRM

    for n, sh, dt in [
            ("vec", [128, 137], F32), ("msk", [16, 17], F32),
            ("alog", [NS, 1], F32), ("Aneg", [NS, 1], F32),
            ("clip", [1, 2], F32), ("csil", [128, 8], F32),
            ("ones_b", [128, 1], BF16), ("onesr_f", [1, 128], F32),
            ("bT", [128, NS * DICH], BF16), ("cTs", [NS, DI], BF16),
            ("mod", [128, 48], F32), ("modd", [128, 48], F32),
            ("scg", [128, IFCH], F32), ("scu", [128, IFCH], F32),
            ("scd", [128, DCH], F32),
            ("cdg", [128, IFCH], F32), ("cdu", [128, IFCH], F32),
            ("cdd", [128, DCH], F32),
            ("sxr", [128, 1], F32), ("s2r", [128, 1], F32),
            ("macc", [128, 2], F32), ("carry", [NS, 1], F32),
            ("DcFc", [NS, 2], F32), ("gath", [NS, 16], F32),
            ("tiny", [128, 8], F32), ("tiny2", [128, 8], F32),
            ("fc", [1, 8], F32), ("fc128", [128, 8], F32),
            ("convc", [128, DICH * 4], BF16),
    ]:
        c[n] = A_CONST.alloc(n, sh, dt)

    for n, sh, dt in [
            ("r1row", [1, TP], F32), ("mr1row", [1, TP], F32),
            ("drow", [1, T], F32), ("hs", [NS, T], BF16),
            ("maccw", [128, 128], F32), ("rs1", [1, 512], F32),
            ("rs2", [1, 512], F32), ("rs3", [1, 512], F32),
            ("d16s", [NS, 512], F32),
    ]:
        c[n] = A_ROWS.alloc(n, sh, dt)
    c["rs0"] = c["drow"]
    dro = A_ROWS.pos["drow"]
    c["yscr"] = nc.alloc_sbuf_tensor_at("yscr", [128, 512], F32, offset=dro)
    c["gscr"] = [nc.alloc_sbuf_tensor_at(f"gscr{i}", [128, 512], BF16,
                                         offset=dro + 2048 + i * 1024)
                 for i in range(2)]
    c["hpre"] = [nc.alloc_sbuf_tensor_at(f"hpre{i}", [128, 512], F32,
                                         offset=dro + 4096 + i * 2048)
                 for i in range(2)]
    c["br1s"] = nc.alloc_sbuf_tensor_at("br1s", [128, 512], F32,
                                        offset=A_ROWS.pos["rs1"])
    c["bmr1s"] = nc.alloc_sbuf_tensor_at("bmr1s", [128, 512], F32,
                                         offset=A_ROWS.pos["rs2"])
    c["br3s"] = nc.alloc_sbuf_tensor_at("br3s", [128, 512], F32,
                                        offset=A_ROWS.pos["rs3"])
    c["bmr3s"] = nc.alloc_sbuf_tensor_at("bmr3s", [128, 512], F32,
                                         offset=A_ROWS.pos["d16s"])
    hso = A_ROWS.pos["hs"]
    c["x2b"] = [nc.alloc_sbuf_tensor_at(f"x2b{i}", [128, 512], BF16,
                                        offset=hso + i * 1024) for i in range(2)]
    c["sqb"] = [nc.alloc_sbuf_tensor_at(f"sqb{i}", [128, 512], BF16,
                                        offset=hso + 2048 + i * 1024)
                for i in range(2)]
    c["s3r"] = [nc.alloc_sbuf_tensor_at(f"s3r{i}", [1, 512], F32,
                                        offset=dro + i * 2048)
                for i in range(2)]
    c["gcol"] = [nc.alloc_sbuf_tensor_at(f"gcol{i}", [128, DCH * 128], BF16,
                                         offset=A_STRM.base + i * 2048)
                 for i in range(2)]
    c["ucol"] = [nc.alloc_sbuf_tensor_at(f"ucol{i}", [128, DCH * 128], BF16,
                                         offset=A_STRM.base + 4096 + i * 2048)
                 for i in range(2)]
    c["dcol"] = [nc.alloc_sbuf_tensor_at(f"dcol{i}", [128, IFCH * 128], BF16,
                                         offset=A_STRM.base + 8192 + i * 5632)
                 for i in range(2)]
    c["obuf"] = [nc.alloc_sbuf_tensor_at(f"obuf{i}", [128, 512], F32,
                                         offset=A_STRM.base + 19456 + i * 2048)
                 for i in range(2)]

    c["h2"] = nc.alloc_sbuf_tensor_at("h2", [128, DCH * TP], BF16, offset=R_H2)
    c["dA"] = nc.alloc_sbuf_tensor_at("dA", [NS, T], F32, offset=R_H2)
    c["Bx"] = nc.alloc_sbuf_tensor_at("Bx", [NS, T], F32, offset=R_H2 + 8192)
    c["h3"] = nc.alloc_sbuf_tensor_at("h3", [128, DCH * T], BF16, offset=R_H2)
    c["x2rl"] = nc.alloc_sbuf_tensor_at("x2rl", [128, 2 * DCH * 512], F32,
                                        offset=R_H2)
    c["xrel"] = nc.alloc_sbuf_tensor_at("xrel", [128, DCH * 512], F32,
                                        offset=R_H2)
    c["x2g"] = nc.alloc_sbuf_tensor_at("x2g", [128, DCH * 512], F32,
                                       offset=R_H2 + 16384)
    c["wo"] = nc.alloc_sbuf_tensor_at("wo", [128, DICH * D], BF16, offset=R_W4)

    c["m_b"] = nc.alloc_sbuf_tensor_at("m_b", [128, (IFCH - 16) * T], BF16,
                                       offset=R_W4)
    c["qnat"] = nc.alloc_sbuf_tensor_at("qnat", [128, D], F32, offset=R_W4)
    c["qtT"] = nc.alloc_sbuf_tensor_at("qtT", [128, IF], F32, offset=R_W4 + 4096)
    c["qout"] = [nc.alloc_sbuf_tensor_at("qout0", [128, IF], BF16,
                                         offset=R_W4 + 15360)] * 2
    c["rbc"] = nc.alloc_sbuf_tensor_at("rbc", [128, IF], F32,
                                       offset=R_W4 + 20992)
    c["xs"] = nc.alloc_sbuf_tensor_at("xs", [128, DICH * T], BF16, offset=R_XS)
    c["m_a"] = nc.alloc_sbuf_tensor_at("m_a", [128, 16 * T], BF16, offset=R_XS)
    c["p0x"] = nc.alloc_sbuf_tensor_at("p0x", [128, 2 * DCH * 515], BF16,
                                       offset=R_XS)
    c["p0h"] = nc.alloc_sbuf_tensor_at("p0h", [128, DCH * 515], F32,
                                       offset=R_XS + 16480)
    c["p0sq"] = nc.alloc_sbuf_tensor_at("p0sq", [128, DCH * 515], BF16,
                                        offset=R_XS + 32960)
    c["p0hb"] = nc.alloc_sbuf_tensor_at("p0hb", [128, DCH * 515], BF16,
                                        offset=R_XS + 41216)
    c["rbcd"] = nc.alloc_sbuf_tensor_at("rbcd", [128, D], F32,
                                        offset=R_H2 + 16384)
    c["dnat"] = nc.alloc_sbuf_tensor_at("dnat", [128, IF], F32,
                                        offset=R_H2 + 20480)

    X.ps = [nc.alloc_psum_tensor(f"psb{b}", [128, 512], F32) for b in range(8)]
    for i, v in enumerate(_FCONSTS):
        nc.const_aps.register(F32, v, c["fc"][0:1, i:i + 1])
    for i, v in enumerate(_FCONSTS128):
        nc.const_aps.register(F32, v, c["fc128"][:, i:i + 1])
    c["wmb"] = [A_STRM.alloc(f"wmb{i}", [128, WSH], F32) for i in range(2)]
    c["onesW"] = nc.alloc_sbuf_tensor_at("onesW", [128, 128], BF16,
                                         offset=A_STRM.base + 32768)
    c["sgg"] = [nc.alloc_sbuf_tensor_at(f"sgg{i}", [128, DICH * 512], BF16,
                                        offset=A_STRM.base + i * 16384)
                for i in range(2)]
    s1 = Arena(nc, A_STRM.base, A_STRM.size, "strm1")
    c["wcol"] = [s1.alloc(f"wcol{i}", [128, DCH * 128], BF16) for i in range(2)]
    c["xraw"] = s1.alloc("xraw", [128, TP], BF16)
    c["cacc"] = s1.alloc("cacc", [128, T], BF16)
    c["sgrow"] = [s1.alloc(f"sgrow{i}", [128, T], BF16) for i in range(2)]
    c["qrow"] = s1.alloc("qrow", [1, IF], F32)
    s2 = Arena(nc, A_STRM.base + 6144, A_STRM.size - 6144 - 11296, "strm2")
    c["wdcol"] = [s2.alloc(f"wdcol{i}", [128, DICH * 128], BF16) for i in range(2)]
    c["exps"] = [s2.alloc(f"exps{i}", [128, 512], F32) for i in range(2)]
    c["spb"] = [s2.alloc(f"spb{i}", [128, 512], BF16) for i in range(2)]

    phases = [("ph0", [("sync", ph0_sync), ("gpsimd", ph0_gpsimd),
                       ("scalar", ph0_scalar), ("tensor", ph0_tensor),
                       ("vector", ph0_vector)])]
    if dbg != "ph0":
        phases.append(("ph1", [("sync", ph1_sync), ("gpsimd", ph1_gpsimd),
                               ("scalar", ph1_scalar), ("tensor", ph1_tensor),
                               ("vector", ph1_vector)]))
    if dbg not in ("ph0", "ph1"):
        phases.append(("ph2", [("sync", ph2_sync), ("gpsimd", ph2_gpsimd),
                               ("scalar", ph2_scalar), ("tensor", ph2_tensor),
                               ("vector", ph2_vector)]))
    if dbg not in ("ph0", "ph1", "ph2"):
        phases.append(("ph3", [("sync", ph3_sync), ("gpsimd", ph3_gpsimd),
                               ("scalar", ph3_scalar), ("tensor", ph3_tensor),
                               ("vector", ph3_vector)]))
    if dbg not in ("ph0", "ph1", "ph2", "ph3"):
        phases.append(("ph5", [("sync", ph5_sync), ("gpsimd", ph5_gpsimd),
                               ("scalar", ph5_scalar), ("tensor", ph5_tensor),
                               ("vector", ph5_vector)]))
    if dbg not in ("ph0", "ph1", "ph2", "ph3", "ph5"):
        phases.append(("ph6", [("sync", ph6_sync), ("gpsimd", ph6_gpsimd),
                               ("scalar", ph6_scalar), ("tensor", ph6_tensor),
                               ("vector", ph6_vector)]))

    if dbg == "ph0":
        phases.append(("dbg", [("sync", dbg_ph0_sync),
                               ("gpsimd", dbg_ph0_gpsimd)]))
    if dbg == "ph1":
        phases.append(("dbg", [("gpsimd", dbg_ph1_gp)]))
    if dbg == "ph2":
        phases.append(("dbg", [("gpsimd", dbg_ph2_gp)]))
    if dbg == "ph3":
        phases.append(("dbg", [("gpsimd", dbg_ph3_gp)]))
    if dbg == "ph5":
        phases.append(("dbg", [("gpsimd", dbg_ph5_gp)]))
    if dbg == "pfull":
        phases.insert([i for i, (n, _) in enumerate(phases)
                       if n == "ph3"][0] + 1,
                      ("dbgy", [("gpsimd", dbg_pfull_y)]))
        phases.insert([i for i, (n, _) in enumerate(phases)
                       if n == "ph5"][0] + 1,
                      ("dbgm", [("gpsimd", dbg_pfull_m)]))
        phases.append(("dbgf", [("gpsimd", dbg_pfull_end)]))
    X.run_phases(phases)
    return nc, X


def ph0_sync(X, e):
    c, dr = X.c, X.dr
    for dst, src in [(c["vec"], dr["vecs"]), (c["msk"], dr["msks"]),
                     (c["alog"], dr["alog_in"]), (c["clip"], dr["clip_in"])]:
        X.idma(e.dma_start(out=dst[:, :], in_=src[:, :]), "qs")
    X.lbl("consts")


def ph0_gpsimd(X, e):
    c, dr = X.c, X.dr
    e.memset(c["ones_b"][:, :], 1.0)
    e.memset(c["onesW"][:, :], 1.0)
    if X.dbg:
        e.memset(c["p0h"][:, :], 0.0)
        e.memset(c["p0x"][:, :], 0.0)
        e.memset(c["p0sq"][:, :], 0.0)
        e.memset(c["p0hb"][:, :], 0.0)
    for i, v in enumerate(_FCONSTS):
        e.memset(c["fc"][0:1, i:i + 1], v)
    for i, v in enumerate(_FCONSTS128):
        e.memset(c["fc128"][:, i:i + 1], v)
    X.inc(e.memset(c["onesr_f"][:, :], 1.0), "gS")
    X.lbl("ones")
    wmb = X.c["wmb"]
    for k in range(DCH):
        if k >= 2:
            X.wait(e, f"modmm{k - 2}", "tS")
        src = bass.AP(dr["wmod_sl"], k * 128 * WSH, [[WSH, 128], [1, WSH]])
        X.idma(e.dma_start(out=wmb[k % 2][:, :], in_=src), f"qwm{k % 2}")
        X.lbl(f"wm{k}")
    X.idma(e.dma_start(out=c["bT"][:, :], in_=dr["bT_t"][:, :]), "qbc")
    X.idma(e.dma_start(out=c["cTs"][:, :], in_=dr["cT"][:, :]), "qbc")
    def xload(j):
        g0, w = P0L[j]
        src = bass.AP(dr["xT"], g0, [[TP, 128], [128 * TP, DCH], [1, w]])
        dst = sb3(c["p0x"], (j % 2) * DCH * 515, 515, DCH, w)
        X.idma(e.dma_start(out=dst, in_=src), f"qx{j % 2}")
        X.lbl(f"x{j}")
    xload(0)
    xload(1)
    X.wait(e, "mod_row", "vS")
    X.idma(e.dma_start(out=dr["ag_mod_i"][:, :], in_=c["rs0"][0:1, 0:WSH]), "qagm")
    if not X.pass1:
        e.wait_ge(X.sems["qagm"].h, X.sems["qagm"].n)
    X.inc(e.collective_compute(
        "AllGather", ALU.bypass, replica_groups=[list(range(NC))],
        ins=[dr["ag_mod_i"].ap()], outs=[dr["ag_mod_o"].ap()]), "cS")
    if not X.pass1:
        e.wait_ge(X.sems["cS"].h, X.sems["cS"].n)
    for col in range(48):
        v, ch = col // 8, col % 8
        s2 = bass.AP(dr["ag_mod_o"], v * 1024 + ch * 128, [[1, 128], [1, 1]])
        X.idma(e.dma_start(out=c["mod"][:, col:col + 1], in_=s2), "qmod")
    X.lbl("mod_in")
    xg_last = {0: 1, 1: 2, 2: 3, 3: 4}
    for j in range(2, 4):
        X.wait(e, f"h2_{xg_last[j - 2]}", "vS")
        xload(j)


def ph0_scalar(X, e):
    c = X.c
    X.wait(e, "consts", "qs")
    X.inc(e.activation(c["csil"][:, :], vsl(X, "cvec"), AF.Silu), "aS")
    X.lbl("csil")
    X.inc(e.activation(c["Aneg"][:, :], c["alog"][:, :], AF.Exp), "aS")
    X.inc(e.mul(c["Aneg"][:, :], c["Aneg"][:, :], -1.0), "aS")
    for gi, (lj, loff, w) in enumerate(P0G):
        g0 = GRPS[gi][0]
        buf = lj % 2
        X.wait(e, f"x{lj}", f"qx{lj % 2}")
        if gi >= 1:
            X.wait(e, f"h2_{gi - 1}", "vS")
        X.inc(e.activation(
            sb3(c["p0sq"], loff, 515, DCH, w),
            sb3(c["p0x"], buf * DCH * 515 + loff, 515, DCH, w),
            AF.Square), "aS")
        X.lbl(f"sq1_{gi}")

        X.wait(e, f"var1_{gi}", "vS")
        X.inc(e.activation(c["rs3"][0:1, 0:w], c["rs3"][0:1, 0:w], AF.Ln,
                           bias=EPS), "aS")
        X.inc(e.activation(c["r1row"][0:1, g0:g0 + w], c["rs3"][0:1, 0:w],
                           AF.Exp, scale=-0.5), "aS")
        X.lbl(f"r1_{gi}")
        X.wait(e, f"h_{gi}", "vS")
        X.inc(e.activation(
            sb3(c["p0sq"], loff, 515, DCH, w),
            sb3(c["p0h"], loff, 515, DCH, w),
            AF.Square), "aS")
        X.lbl(f"sq2_{gi}")
        X.wait(e, f"var2_{gi}", "vS")
        X.inc(e.activation(c["rs3"][0:1, 0:w], c["rs3"][0:1, 0:w], AF.Ln,
                           bias=EPS), "aS")
        X.inc(e.activation(c["rs2"][0:1, 0:w], c["rs3"][0:1, 0:w],
                           AF.Exp, scale=-0.5), "aS")
        X.lbl(f"r2_{gi}")


def ph0_tensor(X, e):
    c, ps = X.c, X.ps
    wmb = X.c["wmb"]
    X.wait(e, "csil", "aS")
    X.wait(e, "ones", "gS")
    for k in range(DCH):
        X.wait(e, f"wm{k}", f"qwm{k % 2}")
        for j in range(2):
            inst = e.matmul(ps[6 + j][0:1, 0:WSH // 2], c["csil"][:, k:k + 1],
                            wmb[k % 2][:, j * (WSH // 2):(j + 1) * (WSH // 2)],
                            start=(k == 0), stop=(k == DCH - 1))
        X.inc(inst, "tS")
        X.lbl(f"modmm{k}")
    X.lbl("mod_mm")
    for gi, (lj, loff, w) in enumerate(P0G):
        g0 = GRPS[gi][0]
        buf = lj % 2
        X.wait(e, f"x{lj}", f"qx{lj % 2}")
        if gi == 0:
            X.wait(e, "mod_row", "vS")
        if gi >= 1:
            X.wait(e, f"var2_{gi - 1}", "vS")
        for k in range(DCH):
            e.matmul(ps[7][:, 0:w], c["onesW"][:, :],
                     p0xap(X, buf, k, w, off=loff),
                     start=(k == 0), stop=(k == DCH - 1))
        X.wait(e, f"sq1_{gi}", "aS")
        for k in range(DCH):
            inst = e.matmul(ps[6][:, 0:w], c["onesW"][:, :],
                            c["p0sq"][:, k * 515 + loff:k * 515 + loff + w],
                            start=(k == 0), stop=(k == DCH - 1))
        X.inc(inst, "tS")
        X.lbl(f"st1_{gi}")
        X.wait(e, f"mr1_{gi}", "vS")
        if gi >= 1:
            X.wait(e, f"h_{gi - 1}", "vS")
        e.matmul(ps[5][:, 0:w], c["onesr_f"][0:1, :],
                 c["r1row"][0:1, g0:g0 + w], start=True, stop=True)
        inst = e.matmul(ps[4][:, 0:w], c["onesr_f"][0:1, :],
                        c["mr1row"][0:1, g0:g0 + w], start=True, stop=True)
        X.inc(inst, "tS")
        X.lbl(f"bc1_{gi}")
        X.wait(e, f"hb_{gi}", "vS")
        X.wait(e, f"rows1_{gi}", "vS")
        for k in range(DCH):
            e.matmul(ps[7][:, 0:w], c["onesW"][:, :],
                     c["p0hb"][:, k * 515 + loff:k * 515 + loff + w],
                     start=(k == 0), stop=(k == DCH - 1))
        X.wait(e, f"sq2_{gi}", "aS")
        for k in range(DCH):
            inst = e.matmul(ps[6][:, 0:w], c["onesW"][:, :],
                            c["p0sq"][:, k * 515 + loff:k * 515 + loff + w],
                            start=(k == 0), stop=(k == DCH - 1))
        X.inc(inst, "tS")
        X.lbl(f"st2_{gi}")
        X.wait(e, f"mr2_{gi}", "vS")
        if gi >= 1:
            X.wait(e, f"h2_{gi - 1}", "vS")
        e.matmul(ps[3][:, 0:w], c["onesr_f"][0:1, :], c["rs2"][0:1, 0:w],
                 start=True, stop=True)
        inst = e.matmul(ps[2][:, 0:w], c["onesr_f"][0:1, :], c["rs3"][0:1, 0:w],
                        start=True, stop=True)
        X.inc(inst, "tS")
        X.lbl(f"bc2_{gi}")


def ph0_vector(X, e):
    c, ps = X.c, X.ps
    X.wait(e, "mod_mm", "tS")
    e.tensor_copy(c["rs0"][0:1, 0:WSH // 2], ps[6][0:1, 0:WSH // 2])
    X.inc(e.tensor_copy(c["rs0"][0:1, WSH // 2:WSH], ps[7][0:1, 0:WSH // 2]),
          "vS")
    X.lbl("mod_row")
    X.wait(e, "mod_in", "qmod")
    md, mo = c["modd"], c["mod"]
    X.sw(e, e.tensor_scalar_add(md[:, 0:8], mo[:, 8:16], 1.0), "vS")
    X.sw(e, e.tensor_scalar_add(md[:, 8:16], mo[:, 32:40], 1.0), "vS")
    for (srcc, dstc) in ((16, 16), (40, 24)):
        g = mo[:, srcc:srcc + 8]
        X.sw(e, e.tensor_scalar_mul(c["tiny"][:, 0:8], g, -1.0), "vS")
        X.sw(e, e.tensor_tensor(c["tiny"][:, 0:8], c["tiny"][:, 0:8], g,
                                ALU.max), "vS")
        X.sw(e, e.tensor_scalar_max(c["tiny"][:, 0:8], c["tiny"][:, 0:8],
                                    0.25), "vS")
        sg = md[:, dstc:dstc + 8]
        X.sw(e, e.tensor_scalar_add(sg, g, 1e-38), "vS")
        for _ in range(5):
            X.sw(e, e.tensor_scalar_mul(sg, sg, 1e9), "vS")
            X.sw(e, e.tensor_scalar_min(sg, sg, 1.0), "vS")
            X.sw(e, e.tensor_scalar_max(sg, sg, -1.0), "vS")
        X.sw(e, e.tensor_mul(md[:, dstc:dstc + 8], md[:, dstc:dstc + 8],
                             c["tiny"][:, 0:8]), "vS")
    X.sw(e, e.tensor_mul(md[:, 32:40], md[:, 16:24], md[:, 0:8]), "vS")
    X.inc(e.tensor_mul(md[:, 40:48], md[:, 16:24], mo[:, 0:8]), "vS")
    X.lbl("mod_done")
    for gi, (lj, loff, w) in enumerate(P0G):
        g0 = GRPS[gi][0]
        buf = lj % 2
        X.wait(e, f"st1_{gi}", "tS")
        e.tensor_scalar_mul(c["rs1"][0:1, 0:w], ps[7][0:1, 0:w], 1.0 / D)
        e.tensor_scalar_mul(c["rs2"][0:1, 0:w], ps[6][0:1, 0:w], 1.0 / D)
        e.tensor_mul(c["rs3"][0:1, 0:w], c["rs1"][0:1, 0:w], c["rs1"][0:1, 0:w])
        X.inc(e.tensor_sub(c["rs3"][0:1, 0:w], c["rs2"][0:1, 0:w],
                           c["rs3"][0:1, 0:w]), "vS")
        X.lbl(f"var1_{gi}")
        X.lbl(f"rows1_{gi}")
        X.wait(e, f"r1_{gi}", "aS")
        X.inc(e.scalar_tensor_tensor(c["mr1row"][0:1, g0:g0 + w],
                                     c["rs1"][0:1, 0:w], -1.0,
                                     c["r1row"][0:1, g0:g0 + w],
                                     ALU.mult, ALU.mult), "vS")
        X.lbl(f"mr1_{gi}")
        X.wait(e, f"bc1_{gi}", "tS")
        X.wait(e, "mod_done", "vS")
        for k in range(DCH):
            kb = k * 515 + loff
            e.tensor_scalar(
                c["p0h"][:, kb:kb + w], ps[4][:, 0:w],
                mcol(X, "ca1", k), mcol(X, "sa", k), ALU.mult, ALU.add)
            e.tensor_mul(c["p0sq"][:, kb:kb + w],
                         p0xap(X, buf, k, w, off=loff), ps[5][:, 0:w])
            e.scalar_tensor_tensor(
                c["p0h"][:, kb:kb + w],
                c["p0sq"][:, kb:kb + w], mcol(X, "ca1", k),
                c["p0h"][:, kb:kb + w], ALU.mult, ALU.add)
        X.inc(e.engine_nop(), "vS")
        X.lbl(f"h_{gi}")
        X.inc(e.tensor_copy(
            sb3(c["p0hb"], loff, 515, DCH, w),
            sb3(c["p0h"], loff, 515, DCH, w)), "vS")
        X.lbl(f"hb_{gi}")
        X.wait(e, f"st2_{gi}", "tS")
        e.tensor_scalar_mul(c["rs1"][0:1, 0:w], ps[7][0:1, 0:w], 1.0 / D)
        e.tensor_scalar_mul(c["rs2"][0:1, 0:w], ps[6][0:1, 0:w], 1.0 / D)
        e.tensor_mul(c["rs3"][0:1, 0:w], c["rs1"][0:1, 0:w], c["rs1"][0:1, 0:w])
        X.inc(e.tensor_sub(c["rs3"][0:1, 0:w], c["rs2"][0:1, 0:w],
                           c["rs3"][0:1, 0:w]), "vS")
        X.lbl(f"var2_{gi}")
        X.wait(e, f"r2_{gi}", "aS")
        X.inc(e.scalar_tensor_tensor(c["rs3"][0:1, 0:w], c["rs1"][0:1, 0:w],
                                     -1.0, c["rs2"][0:1, 0:w],
                                     ALU.mult, ALU.mult), "vS")
        X.lbl(f"mr2_{gi}")
        X.wait(e, f"bc2_{gi}", "tS")
        for k in range(DCH):
            kb = k * 515 + loff
            e.tensor_scalar(
                c["p0h"][:, kb:kb + w], ps[2][:, 0:w],
                vsl(X, "gam")[:, k:k + 1], vsl(X, "bet")[:, k:k + 1],
                ALU.mult, ALU.add)
            e.tensor_mul(c["p0sq"][:, kb:kb + w],
                         c["p0hb"][:, kb:kb + w], ps[3][:, 0:w])
            e.scalar_tensor_tensor(
                c["h2"][:, k * TP + g0:k * TP + g0 + w],
                c["p0sq"][:, kb:kb + w],
                vsl(X, "gam")[:, k:k + 1],
                c["p0h"][:, kb:kb + w], ALU.mult, ALU.add)
        X.inc(e.engine_nop(), "vS")
        X.lbl(f"h2_{gi}")
    X.lbl("h2_all")


def _noop(e):
    return e.engine_nop()


def dbg_ph0_sync(X, e):
    c, dr = X.c, X.dr
    X.wait(e, "h2_all", "vS")
    X.idma(e.dma_start(out=dr["dbg_rows"][0:1, 0:TP], in_=c["r1row"][:, :]), "qs")
    X.idma(e.dma_start(out=dr["dbg_rows"][0:1, TP:2 * TP], in_=c["mr1row"][:, :]),
           "qs")
    X.idma(e.dma_start(out=dr["dbg_mod"][:, 0:48], in_=c["mod"][:, :]), "qs")
    X.idma(e.dma_start(out=dr["dbg_mod"][:, 48:96], in_=c["modd"][:, :]), "qs")
    if not X.pass1:
        e.wait_ge(X.sems["qs"].h, X.sems["qs"].n)


def dbg_ph0_gpsimd(X, e):
    c, dr = X.c, X.dr
    X.wait(e, "h2_all", "vS")
    X.idma(e.dma_start(out=dr["dbg_h2"][:, :], in_=c["h2"][:, :]), "qdbg")
    X.idma(e.dma_start(out=dr["dbg_h"][:, :], in_=c["p0h"][:, :]), "qdbg")
    if not X.pass1:
        e.wait_ge(X.sems["qdbg"].h, X.sems["qdbg"].n)


def _stage_inputs(inputs):
    x = np.asarray(inputs["L0_attn_x"])[0]
    xTfull = np.ascontiguousarray(x.T)

    def tile_vec(v, w):
        return np.ascontiguousarray(np.asarray(v).reshape(w, 128).T).astype(np.float32)

    gam = tile_vec(inputs["L0_attn_ln_gamma"], DCH)
    bet = tile_vec(inputs["L0_attn_ln_beta"], DCH)
    cwr = np.asarray(inputs["L0_attn_conv_w"]).reshape(DI, 4)
    cw = np.ascontiguousarray(
        cwr.reshape(DICH, 128, 4).transpose(1, 0, 2).reshape(128, DICH * 4))
    cb = tile_vec(inputs["L0_attn_conv_b"], DICH)
    dv = tile_vec(inputs["L0_attn_D_vec"], DICH)
    bd = tile_vec(inputs["L0_attn_bd"], DICH)
    cvec = np.ascontiguousarray(
        np.asarray(inputs["L0_c"]).reshape(DCH, 128).T).astype(np.float32)
    vecs = np.concatenate([gam, bet, cw.astype(np.float32), cb, dv, bd, cvec],
                          axis=1)
    assert vecs.shape == (128, 136)

    bT_t = np.ascontiguousarray(
        np.asarray(inputs["L0_attn_B_mat"]).T.reshape(DICH, 128, NS)
        .transpose(1, 0, 2).reshape(128, DICH * NS)).astype(np.float32)
    cT = np.ascontiguousarray(np.asarray(inputs["L0_attn_C_mat"]).T).astype(np.float32)
    alog = np.asarray(inputs["L0_attn_A_log"]).reshape(NS, 1).astype(np.float32)
    clip2 = np.array([[float(np.asarray(inputs["L0_ffn_clip_x"]).reshape(())),
                       float(np.asarray(inputs["L0_ffn_clip_down"]).reshape(()))]],
                     np.float32)
    wgT = np.ascontiguousarray(np.asarray(inputs["L0_ffn_w_gate"]).T)
    wuT = np.ascontiguousarray(np.asarray(inputs["L0_ffn_w_up"]).T)
    wdnT = np.ascontiguousarray(np.asarray(inputs["L0_ffn_w_down"]).T)
    wmod = np.asarray(inputs["L0_wmod"])

    shared = dict(
        w_in=np.ascontiguousarray(np.asarray(inputs["L0_attn_W_in"])),
        wd=np.ascontiguousarray(np.asarray(inputs["L0_attn_Wd"])),
        w_out=np.ascontiguousarray(np.asarray(inputs["L0_attn_W_out"])),
        bT_t=bT_t, cT=cT,
        wgN=np.ascontiguousarray(np.asarray(inputs["L0_ffn_w_gate"])),
        wuN=np.ascontiguousarray(np.asarray(inputs["L0_ffn_w_up"])),
        wdnN=np.ascontiguousarray(np.asarray(inputs["L0_ffn_w_down"])),
        wgT=wgT, wuT=wuT, wdnT=wdnT,
        alog_in=alog, clip_in=clip2,
    )
    in_maps = []
    for cid in range(NC):
        s = cid * T
        if cid == 0:
            xc = np.concatenate([np.zeros((D, PAD), np.float32),
                                 xTfull[:, :T]], axis=1)
        else:
            xc = np.ascontiguousarray(xTfull[:, s - PAD:s + T])
        m = np.zeros((16, 17), np.float32)
        m[:, :8] = (np.arange(8) < cid).astype(np.float32)
        m[:, 8:16] = (np.arange(8) >= cid).astype(np.float32)
        m[:, 16] = 0.0 if cid == 0 else 1.0
        im = dict(shared)
        flag = np.full((128, 1), 0.0 if cid == 0 else 1.0, np.float32)
        im["vecs"] = np.ascontiguousarray(np.concatenate([vecs, flag], axis=1))
        im["xT"] = np.ascontiguousarray(xc)
        im["msks"] = m
        im["wmod_sl"] = np.ascontiguousarray(wmod[:, cid * WSH:(cid + 1) * WSH])
        in_maps.append(im)
    return in_maps


def kernel(**inputs):
    in_maps = _stage_inputs(inputs)
    if "main" not in _CACHE:
        _CACHE["main"] = build_program(dbg=DBG)
    nc, X = _CACHE["main"]
    res = run_bass_kernel_spmd(nc, in_maps, core_ids=list(range(NC)))
    outs = [res.results[cid]["outT"] for cid in range(NC)]
    full = np.concatenate([np.asarray(o).T for o in outs], axis=0)
    return np.ascontiguousarray(full[None]).astype(np.float32)


def _ph1_tiles():
    tiles = []
    for oc in range(32):
        for gi, (g0, w) in enumerate(GRPS):
            if oc >= 16 and gi == 0:
                continue
            tiles.append((oc, gi, g0, w))
    return tiles


def _ev_sem(oc):
    return "vS" if oc < 16 else "aS"


def ph1_sync(X, e):
    c, dr = X.c, X.dr
    for ic in range(IFCH):
        if ic >= 1:
            X.wait(e, f"g_red{ic - 1}", "vS")
        src = bass.AP(dr["wgN"], ic * 128 * D, [[D, 128], [1, D]])
        X.idma(e.dma_start(out=c["qnat"][:, :], in_=src), "qn")
        X.lbl(f"g_natld{ic}")
    X.wait(e, "g_scg_done", "vS")
    for ic in range(IFCH):
        dst = bass.AP(dr["srow"], ic * 128, [[1, 128], [1, 1]])
        X.idma(e.dma_start(out=dst, in_=c["scg"][:, ic:ic + 1]), "qsr")
    if not X.pass1:
        e.wait_ge(X.sems["qsr"].h, X.sems["qsr"].n)
    X.idma(e.dma_start(out=c["qrow"][0:1, 0:IF],
                       in_=bass.AP(dr["srow"], 0, [[1, 1], [1, IF]])), "qsr")
    X.lbl("g_srow_in")
    X.wait(e, "g_recip_row", "vS")
    X.idma(e.dma_start(out=dr["srow2"].ap(), in_=c["qrow"][0:1, 0:IF]), "qrbc")
    if not X.pass1:
        e.wait_ge(X.sems["qrbc"].h, X.sems["qrbc"].n)
    X.idma(e.dma_start(out=c["rbc"][:, 0:IF],
                       in_=bass.AP(dr["srow2"], 0, [[0, 128], [1, IF]])),
           "qrbc")
    X.lbl("rbc_g")


def ph1_gpsimd(X, e):
    c, dr = X.c, X.dr
    tiles = _ph1_tiles()
    last_of_oc = {}
    for ti, (oc, gi, g0, w) in enumerate(tiles):
        last_of_oc[oc] = ti

    def wcol_load(oc):
        if oc >= 2:
            X.wait(e, f"wmm{oc - 2}", "tS")
        src = bass.AP(dr["w_in"], oc * 128,
                      [[2 * DI, 128], [128 * 2 * DI, DCH], [1, 128]])
        X.idma(e.dma_start(out=c["wcol"][oc % 2][:, :], in_=src), f"qw{oc % 2}")
        X.lbl(f"wld{oc}")

    wcol_load(0)
    wcol_load(1)
    nxt = 2
    while nxt < 23:
        wcol_load(nxt)
        nxt += 1
    while nxt < 32:
        wcol_load(nxt)
        nxt += 1


def ph1_tensor(X, e):
    c, ps = X.c, X.ps
    tiles = _ph1_tiles()
    ev_engine = {}
    for ti, (oc, gi, g0, w) in enumerate(tiles):
        ev_engine[ti] = _ev_sem(oc)
    first_of_oc = set()
    seen = set()
    for ti, (oc, gi, g0, w) in enumerate(tiles):
        if oc not in seen:
            seen.add(oc)
            first_of_oc.add(ti)
    for ti, (oc, gi, g0, w) in enumerate(tiles):
        if ti in first_of_oc:
            X.wait(e, f"wld{oc}", f"qw{oc % 2}")
            if oc == 0:
                X.wait(e, "h2_all", "vS")
        bank = ti % 4
        if ti >= 4:
            X.wait(e, f"ev{ti - 4}", ev_engine[ti - 4])
        for k in range(DCH):
            inst = e.matmul(ps[bank][:, 0:w],
                            c["wcol"][oc % 2][:, k * 128:(k + 1) * 128],
                            c["h2"][:, k * TP + g0:k * TP + g0 + w],
                            start=(k == 0), stop=(k == DCH - 1))
        X.inc(inst, "tS")
        X.lbl(f"mmt{ti}")
        if ti == max(t2 for t2, (o2, _, _, _) in enumerate(tiles) if o2 == oc):
            X.lbl(f"wmm{oc}")


def ph1_vector(X, e):
    c, ps = X.c, X.ps
    tiles = _ph1_tiles()
    for ti, (oc, gi, g0, w) in enumerate(tiles):
        if oc >= 16:
            continue
        bank = ti % 4
        X.wait(e, f"mmt{ti}", "tS")
        X.inc(e.tensor_copy(c["xraw"][:, g0:g0 + w], ps[bank][:, 0:w]), "vS")
        X.lbl(f"ev{ti}")
        if gi == len(GRPS) - 1:
            if oc >= 1:
                X.wait(e, f"xs{oc - 1}", "aS")
            e.tensor_scalar_mul(c["xraw"][:, 0:PAD], c["xraw"][:, 0:PAD],
                                vsl(X, "mflag")[:, 0:1])
            e.tensor_scalar_mul(c["cacc"][:, :], c["xraw"][:, 0:T],
                                vsl(X, "cw")[:, oc * 4:oc * 4 + 1])
            for j in range(1, 4):
                inst = e.scalar_tensor_tensor(
                    c["cacc"][:, :], c["xraw"][:, j:j + T],
                    vsl(X, "cw")[:, oc * 4 + j:oc * 4 + j + 1],
                    c["cacc"][:, :], ALU.mult, ALU.add)
            X.inc(inst, "vS")
            X.lbl(f"conv{oc}")
            for _ in range(2):
                ic = X.scratch.setdefault("g_red_i", 0)
                if ic < IFCH:
                    X.wait(e, f"g_natld{ic}", "qn")
                    X.inc(e.reduce_max(c["scg"][:, ic:ic + 1], c["qnat"][:, :],
                                       axis=AX, apply_absolute_value=True),
                          "vS")
                    X.lbl(f"g_red{ic}")
                    X.scratch["g_red_i"] = ic + 1
    X.lbl("g_scg_done")
    X.wait(e, "g_srow_in", "qsr")
    e.tensor_scalar_mul(c["qrow"][0:1, 0:IF], c["qrow"][0:1, 0:IF], 1.0 / 127.0)
    X.inc(e.reciprocal(c["qrow"][0:1, 0:IF], c["qrow"][0:1, 0:IF]), "vS")
    X.lbl("g_recip_row")


def ph1_scalar(X, e):
    c, dr, ps = X.c, X.dr, X.ps
    tiles = _ph1_tiles()
    for oc in range(16):
        X.wait(e, f"conv{oc}", "vS")
        X.inc(e.activation(c["xs"][:, oc * T:(oc + 1) * T], c["cacc"][:, :],
                           AF.Silu, bias=vsl(X, "cb")[:, oc:oc + 1]), "aS")
        X.lbl(f"xs{oc}")
    for ti, (oc, gi, g0, w) in enumerate(tiles):
        if oc < 16:
            continue
        ic = oc - 16
        bank = ti % 4
        X.wait(e, f"mmt{ti}", "tS")
        if gi == 1 and ic >= 2:
            X.wait(e, f"sgsp{ic - 2}", f"qsg{ic % 2}")
        X.inc(e.activation(c["sgrow"][ic % 2][:, g0 - PAD:g0 - PAD + w],
                           ps[bank][:, 0:w], AF.Silu), "aS")
        X.lbl(f"ev{ti}")
        if gi == len(GRPS) - 1:
            X.fence(e, "aS")
            dst = bass.AP(dr["sg_sp"], ic * 128 * T, [[T, 128], [1, T]])
            X.idma(e.dma_start(out=dst, in_=c["sgrow"][ic % 2][:, :]),
                   f"qsg{ic % 2}")
            X.lbl(f"sgsp{ic}")
    X.lbl("ph1_done")


def dbg_ph1_gp(X, e):
    c, dr = X.c, X.dr
    X.wait(e, "xs15", "aS")
    X.wait(e, "sgsp15", f"qsg{15 % 2}")
    X.idma(e.dma_start(out=dr["dbg_xs"][:, :], in_=c["xs"][:, :]), "qdbg")
    if not X.pass1:
        e.wait_ge(X.sems["qdbg"].h, X.sems["qdbg"].n)


def ph2_sync(X, e):
    c, dr = X.c, X.dr
    for hc in range(DCH):
        if hc >= 1:
            X.wait(e, f"g_qmul{hc - 1}", "vS")
            dst = bass.AP(dr["wq_g"], (hc - 1) * 128,
                          [[DCH * 128, 128], [128 * DCH * 128, IFCH],
                           [1, 128]])
            X.idma(e.dma_start(out=dst, in_=c["qout"][0][:, 0:IF]), "qqs2")
            X.lbl(f"g_qsp{hc - 1}")
        gsrc = bass.AP(dr["wgT"], hc * 128 * IF, [[IF, 128], [1, IF]])
        X.idma(e.dma_start(out=c["qtT"][:, :], in_=gsrc), "qt")
        X.lbl(f"g_qtld{hc}")
    X.wait(e, f"g_qmul{DCH - 1}", "vS")
    dst = bass.AP(dr["wq_g"], (DCH - 1) * 128,
                  [[DCH * 128, 128], [128 * DCH * 128, IFCH], [1, 128]])
    X.idma(e.dma_start(out=dst, in_=c["qout"][0][:, 0:IF]), "qqs2")
    X.lbl(f"g_qsp{DCH - 1}")
    for ic in range(IFCH):
        if ic >= 1:
            X.wait(e, f"u_red{ic - 1}", "vS")
        src = bass.AP(dr["wuN"], ic * 128 * D, [[D, 128], [1, D]])
        X.idma(e.dma_start(out=c["qnat"][:, :], in_=src), "qn")
        X.lbl(f"u_natld{ic}")
    X.wait(e, "u_scg_done", "vS")
    X.wait(e, "rbc_g", "qrbc")
    for ic in range(IFCH):
        dst = bass.AP(dr["srow"], ic * 128, [[1, 128], [1, 1]])
        X.idma(e.dma_start(out=dst, in_=c["scu"][:, ic:ic + 1]), "qsr")
    if not X.pass1:
        e.wait_ge(X.sems["qsr"].h, X.sems["qsr"].n)
    X.idma(e.dma_start(out=c["qrow"][0:1, 0:IF],
                       in_=bass.AP(dr["srow"], 0, [[1, 1], [1, IF]])), "qsr")
    X.lbl("u_srow_in")
    X.wait(e, "g_qmul7", "vS")
    X.wait(e, "u_recip_row", "vS")
    X.idma(e.dma_start(out=dr["srow2"].ap(), in_=c["qrow"][0:1, 0:IF]), "qrbc")
    if not X.pass1:
        e.wait_ge(X.sems["qrbc"].h, X.sems["qrbc"].n)
    X.idma(e.dma_start(out=c["rbc"][:, 0:IF],
                       in_=bass.AP(dr["srow2"], 0, [[0, 128], [1, IF]])),
           "qrbc")
    X.lbl("rbc_u")
    for hc in range(DCH):
        if hc == 0:
            X.wait(e, "wmm31", "tS")
        if hc >= 1:
            X.wait(e, f"d_red{hc - 1}", "vS")
        src = bass.AP(dr["wdnN"], hc * 128 * IF, [[IF, 128], [1, IF]])
        X.idma(e.dma_start(out=c["dnat"][:, :], in_=src), "qt")
        X.lbl(f"d_natld{hc}")
    X.wait(e, "d_scd_done", "vS")
    for hc in range(DCH):
        dst = bass.AP(dr["srow"], hc * 128, [[1, 128], [1, 1]])
        X.idma(e.dma_start(out=dst, in_=c["scd"][:, hc:hc + 1]), "qsr")
    if not X.pass1:
        e.wait_ge(X.sems["qsr"].h, X.sems["qsr"].n)
    X.idma(e.dma_start(out=c["qrow"][0:1, 0:D],
                       in_=bass.AP(dr["srow"], 0, [[1, 1], [1, D]])), "qsr")
    X.lbl("d_srow_in")
    X.wait(e, "d_recip_row", "vS")
    X.idma(e.dma_start(out=bass.AP(dr["srow2"], 0, [[1, 1], [1, D]]),
                       in_=c["qrow"][0:1, 0:D]), "qrbc")
    if not X.pass1:
        e.wait_ge(X.sems["qrbc"].h, X.sems["qrbc"].n)
    X.idma(e.dma_start(out=c["rbcd"][:, 0:D],
                       in_=bass.AP(dr["srow2"], 0, [[0, 128], [1, D]])),
           "qrbc")
    X.lbl("rbc_d")
    for hc in range(DCH):
        if hc == 0:
            X.wait(e, f"g_qmul{DCH - 1}", "vS")
        if hc >= 1:
            X.wait(e, f"u_qmul{hc - 1}", "vS")
            dst = bass.AP(dr["wq_u"], (hc - 1) * 128,
                          [[DCH * 128, 128], [128 * DCH * 128, IFCH],
                           [1, 128]])
            X.idma(e.dma_start(out=dst, in_=c["qout"][0][:, 0:IF]), "qqs2")
            X.lbl(f"u_qsp{hc - 1}")
        src = bass.AP(dr["wuT"], hc * 128 * IF, [[IF, 128], [1, IF]])
        X.idma(e.dma_start(out=c["qtT"][:, :], in_=src), "qt")
        X.lbl(f"u_qtld{hc}")
    X.wait(e, f"u_qmul{DCH - 1}", "vS")
    dst = bass.AP(dr["wq_u"], (DCH - 1) * 128,
                  [[DCH * 128, 128], [128 * DCH * 128, IFCH], [1, 128]])
    X.idma(e.dma_start(out=dst, in_=c["qout"][0][:, 0:IF]), "qqs2")
    X.lbl(f"u_qsp{DCH - 1}")
    for ic in range(IFCH):
        buf = "qnat" if ic % 2 == 0 else "qtT"
        if ic == 1:
            X.wait(e, "u_qmul7", "vS")
        if ic >= 1:
            X.wait(e, f"d_qmul{ic - 1}", "vS")
            dst = bass.AP(dr["wq_d"], (ic - 1) * 128,
                          [[IFCH * 128, 128], [128 * IFCH * 128, DCH],
                           [1, 128]])
            X.idma(e.dma_start(out=dst, in_=c["qout"][0][:, 0:D]), "qqs2")
            X.lbl(f"d_qsp{ic - 1}")
        src = bass.AP(dr["wdnT"], ic * 128 * D, [[D, 128], [1, D]])
        X.idma(e.dma_start(out=c[buf][:, 0:D], in_=src), "qt2")
        X.lbl(f"d_qtld{ic}")
    X.wait(e, f"d_qmul{IFCH - 1}", "vS")
    dst = bass.AP(dr["wq_d"], (IFCH - 1) * 128,
                  [[IFCH * 128, 128], [128 * IFCH * 128, DCH], [1, 128]])
    X.idma(e.dma_start(out=dst, in_=c["qout"][0][:, 0:D]), "qqs2")
    X.lbl(f"d_qsp{IFCH - 1}")


def ph2_gpsimd(X, e):
    c, dr = X.c, X.dr

    def wdcol_load(oc):
        if oc >= 2:
            X.wait(e, f"wdmm{oc - 2}", "tS")
        src = bass.AP(dr["wd"], oc * 128,
                      [[DI, 128], [128 * DI, DICH], [1, 128]])
        X.idma(e.dma_start(out=c["wdcol"][oc % 2][:, :], in_=src),
               f"qd{oc % 2}")
        X.lbl(f"wdld{oc}")

    for oc in range(DICH):
        wdcol_load(oc)
    X.wait(e, "DcFc_done", "vS")
    src = bass.AP(c["DcFc"], 0, [[2, 16], [1, 2]])
    X.idma(e.dma_start(out=bass.AP(dr["ag_car_i"], 0, [[1, 1], [1, 32]]),
                       in_=src), "qagc")
    if not X.pass1:
        e.wait_ge(X.sems["qagc"].h, X.sems["qagc"].n)
    X.inc(e.collective_compute(
        "AllGather", ALU.bypass, replica_groups=[list(range(NC))],
        ins=[dr["ag_car_i"].ap()], outs=[dr["ag_car_o"].ap()]), "cS")
    if not X.pass1:
        e.wait_ge(X.sems["cS"].h, X.sems["cS"].n)
    for j in range(NC):
        s1 = bass.AP(dr["ag_car_o"], j * 32, [[2, 16], [1, 1]])
        X.idma(e.dma_start(out=c["gath"][:, j:j + 1], in_=s1,
                           allow_slow_non_contiguous=True), "qgth")
        s2 = bass.AP(dr["ag_car_o"], j * 32 + 1, [[2, 16], [1, 1]])
        X.idma(e.dma_start(out=c["gath"][:, 8 + j:9 + j], in_=s2,
                           allow_slow_non_contiguous=True), "qgth")
    X.lbl("gath_in")
    X.wait(e, f"d_qsp{IFCH - 1}", "qqs2")
    src = bass.AP(dr["w_out"], 0, [[D, 128], [128 * D, DICH], [1, D]])
    dst = bass.AP(c["wo"], 0, [[DICH * D, 128], [D, DICH], [1, D]])
    X.idma(e.dma_start(out=dst, in_=src), "qwo")
    X.lbl("wo_ld")


def ph2_tensor(X, e):
    c, ps = X.c, X.ps
    X.wait(e, "ph1_done", "aS")
    ti = 0
    for oc in range(DICH):
        X.wait(e, f"wdld{oc}", f"qd{oc % 2}")
        for g in range(4):
            bank = ti % 3
            if ti >= 3:
                X.wait(e, f"wdev{ti - 3}", "aS")
            for k in range(DICH):
                inst = e.matmul(
                    ps[bank][:, 0:512],
                    c["wdcol"][oc % 2][:, k * 128:(k + 1) * 128],
                    c["xs"][:, k * T + g * 512:k * T + (g + 1) * 512],
                    start=(k == 0), stop=(k == DICH - 1))
            X.inc(inst, "tS")
            X.lbl(f"wdmt{ti}")
            X.wait(e, f"wdln{ti}", "aS")
            inst = e.matmul(ps[3 + g][:, 0:512], c["onesW"][:, :],
                            c["spb"][ti % 2][:, :],
                            start=(oc == 0), stop=(oc == DICH - 1))
            X.inc(inst, "tS")
            X.lbl(f"dmm{ti}")
            ti += 1
        X.lbl(f"wdmm{oc}")
    for g in range(4):
        if g > 0:
            X.wait(e, f"bev{g - 1}", "vS")
        for k in range(DICH):
            inst = e.matmul(ps[7][0:16, 0:512],
                            c["bT"][:, k * NS:(k + 1) * NS],
                            c["xs"][:, k * T + g * 512:k * T + (g + 1) * 512],
                            start=(k == 0), stop=(k == DICH - 1))
        X.inc(inst, "tS")
        X.lbl(f"bmm{g}")
    for g in range(4):
        X.wait(e, f"dle{g}", "vS")
        inst = e.matmul(ps[3 + g][0:16, 0:512], c["onesr_f"][0:1, 0:16],
                        c["drow"][0:1, g * 512:(g + 1) * 512],
                        start=True, stop=True)
        X.inc(inst, "tS")
        X.lbl(f"damm{g}")
    X.wait(e, "stot", "vS")
    X.wait(e, "bev3", "vS")
    inst = e.matmul(ps[7][0:16, 0:1], c["onesr_f"][0:1, 0:16],
                    c["rs1"][0:1, 0:1], start=True, stop=True)
    X.inc(inst, "tS")
    X.lbl("dcmm")


def ph2_scalar(X, e):
    c, ps = X.c, X.ps
    ti = 0
    for oc in range(DICH):
        for g in range(4):
            bank = ti % 3
            X.wait(e, f"wdmt{ti}", "tS")
            X.inc(e.activation(c["exps"][ti % 2][:, :], ps[bank][:, 0:512],
                               AF.Exp), "aS")
            X.lbl(f"wdev{ti}")
            if ti >= 2:
                X.wait(e, f"dmm{ti - 2}", "tS")
            X.inc(e.activation(c["spb"][ti % 2][:, :], c["exps"][ti % 2][:, :],
                               AF.Ln, bias=1.0), "aS")
            X.lbl(f"wdln{ti}")
            ti += 1
    for g in range(4):
        X.wait(e, f"damm{g}", "tS")
        X.inc(e.activation(c["dA"][:, g * 512:(g + 1) * 512],
                           ps[3 + g][0:16, 0:512], AF.Exp,
                           scale=c["Aneg"][:, 0:1]), "aS")
        X.lbl(f"dAa{g}")
    X.wait(e, "dcmm", "tS")
    X.inc(e.activation(c["DcFc"][:, 0:1], ps[7][0:16, 0:1], AF.Exp,
                       scale=c["Aneg"][:, 0:1]), "aS")
    X.lbl("Dc_done")


def ph2_vector(X, e):
    c, ps = X.c, X.ps
    for hc in range(DCH):
        X.wait(e, f"g_qtld{hc}", "qt")
        if hc == 0:
            X.wait(e, "rbc_g", "qrbc")
        e.tensor_mul(c["qtT"][:, :], c["qtT"][:, :], c["rbc"][:, 0:IF])
        if hc >= 1:
            X.wait(e, f"g_qsp{hc - 1}", "qqs2")
        X.inc(e.tensor_scalar(c["qout"][0][:, 0:IF], c["qtT"][:, :],
                              MAGIC, MAGIC, ALU.add, ALU.subtract), "vS")
        X.lbl(f"g_qmul{hc}")
    for ic in range(IFCH):
        X.wait(e, f"u_natld{ic}", "qn")
        X.inc(e.reduce_max(c["scu"][:, ic:ic + 1], c["qnat"][:, :], axis=AX,
                           apply_absolute_value=True), "vS")
        X.lbl(f"u_red{ic}")
    X.lbl("u_scg_done")
    X.wait(e, "u_srow_in", "qsr")
    e.tensor_scalar_mul(c["qrow"][0:1, 0:IF], c["qrow"][0:1, 0:IF], 1.0 / 127.0)
    X.inc(e.reciprocal(c["qrow"][0:1, 0:IF], c["qrow"][0:1, 0:IF]), "vS")
    X.lbl("u_recip_row")
    for hc in range(DCH):
        X.wait(e, f"d_natld{hc}", "qt")
        X.inc(e.reduce_max(c["scd"][:, hc:hc + 1], c["dnat"][:, :], axis=AX,
                           apply_absolute_value=True), "vS")
        X.lbl(f"d_red{hc}")
    X.lbl("d_scd_done")
    X.wait(e, "d_srow_in", "qsr")
    e.tensor_scalar_mul(c["qrow"][0:1, 0:D], c["qrow"][0:1, 0:D], 1.0 / 127.0)
    X.inc(e.reciprocal(c["qrow"][0:1, 0:D], c["qrow"][0:1, 0:D]), "vS")
    X.lbl("d_recip_row")
    for hc in range(DCH):
        X.wait(e, f"u_qtld{hc}", "qt")
        if hc == 0:
            X.wait(e, "rbc_u", "qrbc")
        e.tensor_mul(c["qtT"][:, :], c["qtT"][:, :], c["rbc"][:, 0:IF])
        if hc >= 1:
            X.wait(e, f"u_qsp{hc - 1}", "qqs2")
        else:
            X.wait(e, f"g_qsp{DCH - 1}", "qqs")
        X.inc(e.tensor_scalar(c["qout"][0][:, 0:IF], c["qtT"][:, :],
                              MAGIC, MAGIC, ALU.add, ALU.subtract), "vS")
        X.lbl(f"u_qmul{hc}")
    for ic in range(IFCH):
        buf = "qnat" if ic % 2 == 0 else "qtT"
        X.wait(e, f"d_qtld{ic}", "qt2")
        if ic == 0:
            X.wait(e, "rbc_d", "qrbc")
        e.tensor_mul(c[buf][:, 0:D], c[buf][:, 0:D], c["rbcd"][:, 0:D])
        X.wait(e, f"d_qsp{ic - 1}" if ic >= 1 else f"u_qsp{DCH - 1}", "qqs2")
        X.inc(e.tensor_scalar(c["qout"][0][:, 0:D], c[buf][:, 0:D],
                              MAGIC, MAGIC, ALU.add, ALU.subtract), "vS")
        X.lbl(f"d_qmul{ic}")
    for g in range(4):
        X.wait(e, f"bmm{g}", "tS")
        X.inc(e.tensor_copy(c["Bx"][:, g * 512:(g + 1) * 512],
                            ps[7][0:16, 0:512]), "vS")
        X.lbl(f"bev{g}")
    for g in range(4):
        X.wait(e, f"dmm{60 + g}", "tS")
        X.inc(e.tensor_scalar_mul(c["drow"][0:1, g * 512:(g + 1) * 512],
                                  ps[3 + g][0:1, 0:512], 1.0 / DI), "vS")
        X.lbl(f"dle{g}")
    for g in range(4):
        X.wait(e, f"dAa{g}", "aS")
        X.inc(e.tensor_mul(c["Bx"][:, g * 512:(g + 1) * 512],
                           c["Bx"][:, g * 512:(g + 1) * 512],
                           ps[3 + g][0:16, 0:512]), "vS")
        X.lbl(f"bxm{g}")
    X.sw(e, e.tensor_tensor_scan(c["hs"][:, :], c["dA"][:, :], c["Bx"][:, :],
                                 0.0, ALU.mult, ALU.add), "vS")
    X.sw(e, e.tensor_copy(c["DcFc"][:, 1:2], c["hs"][:, T - 1:T]), "vS")
    X.inc(e.reduce_sum(c["rs1"][0:1, 0:1], c["drow"][0:1, 0:T], axis=AX),
          "vS")
    X.lbl("stot")
    X.wait(e, "Dc_done", "aS")
    X.inc(e.engine_nop(), "vS")
    X.lbl("DcFc_done")
    X.wait(e, "gath_in", "qgth")
    X.sw(e, e.tensor_mul(c["gath"][:, 0:8], c["gath"][:, 0:8],
                         c["msk"][:, 0:8]), "vS")
    X.sw(e, e.tensor_add(c["gath"][:, 0:8], c["gath"][:, 0:8],
                         c["msk"][:, 8:16]), "vS")
    X.sw(e, e.tensor_mul(c["gath"][:, 8:16], c["gath"][:, 8:16],
                         c["msk"][:, 0:8]), "vS")
    X.sw(e, e.memset(c["carry"][:, :], 0.0), "vS")
    for j in range(NC):
        X.sw(e, e.tensor_mul(c["carry"][:, 0:1], c["carry"][:, 0:1],
                             c["gath"][:, j:j + 1]), "vS")
        X.sw(e, e.tensor_add(c["carry"][:, 0:1], c["carry"][:, 0:1],
                             c["gath"][:, 8 + j:9 + j]), "vS")
    X.inc(e.tensor_tensor_scan(c["hs"][:, :], c["dA"][:, :], c["Bx"][:, :],
                               c["carry"][:, 0:1], ALU.mult, ALU.add), "vS")
    X.lbl("scan2")


def dbg_ph2_gp(X, e):
    c, dr = X.c, X.dr
    X.wait(e, "scan2", "vS")
    X.wait(e, "wo_ld", "qwo")
    d = dr["dbg_sc"]
    X.idma(e.dma_start(out=bass.AP(d, 0, [[1, 1], [1, T]]),
                       in_=c["drow"][0:1, :], allow_slow_non_contiguous=True), "qdbg")
    X.idma(e.dma_start(out=bass.AP(d, (3 * T + 8) * 0 + T, [[3 * T + 8, 16], [1, T]]),
                       in_=c["dA"][:, :], allow_slow_non_contiguous=True), "qdbg")
    X.idma(e.dma_start(out=bass.AP(d, 2 * T, [[3 * T + 8, 16], [1, T]]),
                       in_=c["hs"][:, :], allow_slow_non_contiguous=True), "qdbg")
    X.idma(e.dma_start(out=bass.AP(d, 3 * T, [[3 * T + 8, 16], [1, 1]]),
                       in_=c["carry"][:, 0:1], allow_slow_non_contiguous=True), "qdbg")
    X.idma(e.dma_start(out=bass.AP(d, 3 * T + 1, [[3 * T + 8, 16], [1, 2]]),
                       in_=c["DcFc"][:, :], allow_slow_non_contiguous=True), "qdbg")
    if not X.pass1:
        e.wait_ge(X.sems["qdbg"].h, X.sems["qdbg"].n)


def ph3_sync(X, e):
    c, dr = X.c, X.dr
    X.wait(e, "sgsp14", "qsg0")
    X.wait(e, "sgsp15", "qsg1")
    for g in range(4):
        if g >= 2:
            X.wait(e, f"yg{g - 2}", "vS")
        src = bass.AP(dr["sg_sp"], g * 512, [[T, 128], [128 * T, DICH], [1, 512]])
        X.idma(e.dma_start(out=c["sgg"][g % 2][:, :], in_=src), f"qsg2{g % 2}")
        X.lbl(f"sgld{g}")
    for g in range(4):
        if g >= 1:
            X.wait(e, f"x2d{g - 1}", "vS")
        src = bass.AP(dr["xT"], PAD + g * 512,
                      [[TP, 128], [128 * TP, DCH], [1, 512]])
        X.idma(e.dma_start(out=c["xrel"][:, :], in_=src), "qxr")
        X.lbl(f"xrl{g}")
    X.wait(e, "x2sp3", "qx2s")
    X.wait(e, "macc_done", "vS")
    for oc in range(DCH):
        X.wait(e, f"h3sp3_{oc}", f"qh3{oc % 2}")
        src = bass.AP(dr["h3_sp"], oc * 128 * T, [[T, 128], [1, T]])
        X.idma(e.dma_start(out=c["h3"][:, oc * T:(oc + 1) * T], in_=src),
               "qh3r")
    X.lbl("h3rl")


def ph3_tensor(X, e):
    c, ps = X.c, X.ps
    X.wait(e, "scan2", "vS")
    ti = 0
    for g in range(4):
        for ic in range(DICH):
            bank = ti % 3
            if ti >= 3:
                X.wait(e, f"yev{ti - 3}", "vS")
            inst = e.matmul(ps[bank][:, 0:512],
                            c["cTs"][:, ic * 128:(ic + 1) * 128],
                            c["hs"][:, g * 512:(g + 1) * 512],
                            start=True, stop=True)
            X.inc(inst, "tS")
            X.lbl(f"cmm{ti}")
            ti += 1
    for g in range(4):
        g0 = PAD + g * 512
        if g > 0:
            X.wait(e, f"h3u{g - 1}_7", "vS")
        e.matmul(ps[6][:, 0:512], c["onesr_f"][0:1, :],
                 c["r1row"][0:1, g0:g0 + 512], start=True, stop=True)
        inst = e.matmul(ps[7][:, 0:512], c["onesr_f"][0:1, :],
                        c["mr1row"][0:1, g0:g0 + 512], start=True, stop=True)
        X.inc(inst, "tS")
        X.lbl(f"b1mm{g}")
        X.wait(e, f"yg{g}", "vS")
        if g == 0:
            X.wait(e, "wo_ld", "qwo")
        for oc in range(DCH):
            wi = g * DCH + oc
            bank = 3 + wi % 3
            if wi >= 3:
                X.wait(e, f"woev{wi - 3}", "vS")
            for k in range(DICH):
                inst = e.matmul(ps[bank][:, 0:512],
                                c["wo"][:, k * D + oc * 128:k * D + (oc + 1) * 128],
                                c["xs"][:, k * T + g * 512:k * T + (g + 1) * 512],
                                start=(k == 0), stop=(k == DICH - 1))
            X.inc(inst, "tS")
            X.lbl(f"womm{wi}")
        X.wait(e, f"b1ev{g}", "aS")
        for k in range(DCH):
            X.wait(e, f"x2b{g}_{k}", "aS")
            e.matmul(ps[6][:, 0:512], c["onesW"][:, :],
                     c["x2b"][k % 2][:, :], start=(k == 0), stop=(k == DCH - 1))
            inst = e.matmul(ps[7][:, 0:512], c["onesW"][:, :],
                            c["sqb"][k % 2][:, :], start=(k == 0),
                            stop=(k == DCH - 1))
            X.inc(inst, "tS")
            X.lbl(f"s3m{g}_{k}")
        X.lbl(f"s3mm{g}")
        X.wait(e, f"r3row{g}", "vS")
        e.matmul(ps[6][:, 0:512], c["onesr_f"][0:1, :],
                 c["s3r"][1][0:1, 0:512], start=True, stop=True)
        inst = e.matmul(ps[7][:, 0:512], c["onesr_f"][0:1, :],
                        c["s3r"][0][0:1, 0:512], start=True, stop=True)
        X.inc(inst, "tS")
        X.lbl(f"b2mm{g}")
    X.wait(e, "sxr_row", "vS")
    inst = e.matmul(ps[0][:, 0:1], c["onesr_f"][0:1, :],
                    c["s3r"][0][0:1, 0:1], start=True, stop=True)
    X.inc(inst, "tS")
    X.lbl("sxrmm")


def ph3_gpsimd(X, e):
    c, dr, ps = X.c, X.dr, X.ps
    X.wait(e, "macc_done", "vS")
    X.inc(e.tensor_reduce(c["s3r"][0][0:1, 0:1], c["macc"][:, 0:1],
                          axis=mybir.AxisListType.C, op=ALU.max), "gS")
    X.lbl("gmax_loc")
    X.fence(e, "gS")
    X.idma(e.dma_start(out=bass.AP(dr["ag_m1_i"], 0, [[1, 1], [1, 1]]),
                       in_=c["s3r"][0][0:1, 0:1]), "qagm1")
    if not X.pass1:
        e.wait_ge(X.sems["qagm1"].h, X.sems["qagm1"].n)
    X.inc(e.collective_compute(
        "AllGather", ALU.bypass, replica_groups=[list(range(NC))],
        ins=[bass.AP(dr["ag_m1_i"], 0, [[1, 1], [1, 1]])],
        outs=[bass.AP(dr["ag_m1_o"], 0, [[1, 1], [1, NC]])]), "cS")
    if not X.pass1:
        e.wait_ge(X.sems["cS"].h, X.sems["cS"].n)
    X.idma(e.dma_start(out=c["s3r"][1][0:1, 0:NC],
                       in_=bass.AP(dr["ag_m1_o"], 0, [[1, 1], [1, NC]])),
           "qagm1")
    X.lbl("m1_in")


def ph3_vector(X, e):
    c, ps = X.c, X.ps
    ti = 0
    for g in range(4):
        X.wait(e, f"sgld{g}", f"qsg2{g % 2}")
        for ic in range(DICH):
            bank = ti % 3
            e.tensor_mul(c["gscr"][ti % 2][:, :],
                         c["sgg"][g % 2][:, ic * 512:(ic + 1) * 512],
                         c["xs"][:, ic * T + g * 512:ic * T + (g + 1) * 512])
            X.inc(e.tensor_scalar_mul(c["gscr"][ti % 2][:, :],
                                      c["gscr"][ti % 2][:, :],
                                      vsl(X, "dv")[:, ic:ic + 1]), "v2S")
            X.lbl(f"sxd{ti}")
            X.wait(e, f"cmm{ti}", "tS")
            e.tensor_mul(c["yscr"][:, :], ps[bank][:, 0:512],
                         c["sgg"][g % 2][:, ic * 512:(ic + 1) * 512])
            X.inc(e.tensor_add(
                c["xs"][:, ic * T + g * 512:ic * T + (g + 1) * 512],
                c["yscr"][:, :], c["gscr"][ti % 2][:, :]), "vS")
            X.lbl(f"yev{ti}")
            ti += 1
        X.lbl(f"yg{g}")
    wi = 0
    for g in range(4):
        X.wait(e, f"xrl{g}", "qxr")
        X.wait(e, f"b1evs{g}", "aS")
        for oc in range(DCH):
            bank = 3 + wi % 3
            e.tensor_mul(c["hpre"][wi % 2][:, :],
                         c["xrel"][:, oc * 512:(oc + 1) * 512],
                         c["br1s"][:, :])
            X.inc(e.tensor_add(c["hpre"][wi % 2][:, :], c["hpre"][wi % 2][:, :],
                               c["bmr1s"][:, :]), "v2S")
            X.lbl(f"hpre{wi}")
            X.wait(e, f"womm{wi}", "tS")
            X.wait(e, f"haff{wi}", "aS")
            e.tensor_add(c["hpre"][wi % 2][:, :], c["hpre"][wi % 2][:, :],
                         ps[bank][:, 0:512])
            X.inc(e.scalar_tensor_tensor(
                c["x2g"][:, oc * 512:(oc + 1) * 512],
                c["hpre"][wi % 2][:, :], mcol(X, "gap", oc),
                c["xrel"][:, oc * 512:(oc + 1) * 512],
                ALU.mult, ALU.add), "vS")
            X.lbl(f"woev{wi}")
            wi += 1
        X.lbl(f"x2d{g}")
        X.wait(e, f"s3mm{g}", "tS")
        e.tensor_scalar_mul(c["s3r"][0][0:1, 0:512], ps[6][0:1, 0:512], 1.0 / D)
        e.tensor_scalar_mul(c["s3r"][1][0:1, 0:512], ps[7][0:1, 0:512], 1.0 / D)
        e.tensor_mul(c["bmr3s"][0:1, 0:512], c["s3r"][0][0:1, 0:512],
                     c["s3r"][0][0:1, 0:512])
        X.inc(e.tensor_sub(c["s3r"][1][0:1, 0:512], c["s3r"][1][0:1, 0:512],
                           c["bmr3s"][0:1, 0:512]), "vS")
        X.lbl(f"var3{g}")
        X.wait(e, f"r3a{g}", "aS")
        X.inc(e.scalar_tensor_tensor(c["s3r"][0][0:1, 0:512],
                                     c["s3r"][0][0:1, 0:512], -1.0,
                                     c["s3r"][1][0:1, 0:512],
                                     ALU.mult, ALU.mult), "vS")
        X.lbl(f"r3row{g}")
        def h3u(oc):
            X.wait(e, f"b2mm{g}", "tS")
            X.wait(e, f"s3m{g}_{oc}", "tS")
            if g > 0 or oc >= 2:
                X.wait(e, f"h3a{g if oc >= 2 else g - 1}_{(oc - 2) % DCH}",
                       "aS")
            e.tensor_mul(c["sqb"][oc % 2][:, :],
                         c["x2g"][:, oc * 512:(oc + 1) * 512], ps[6][:, 0:512])
            X.inc(e.tensor_tensor(c["sqb"][oc % 2][:, :], c["sqb"][oc % 2][:, :],
                                  ps[7][:, 0:512], ALU.add), "vS")
            X.lbl(f"h3u{g}_{oc}")

        def h3r(oc):
            X.wait(e, f"h3a{g}_{oc}", "aS")
            X.inc(e.reduce_max(c["maccw"][:, g * DCH + oc:g * DCH + oc + 1],
                               c["x2b"][oc % 2][:, :],
                               axis=AX, apply_absolute_value=True), "vS")
            X.lbl(f"h3r{g}_{oc}")

        h3u(0)
        h3u(1)
        for oc in range(2, DCH):
            h3r(oc - 2)
            h3u(oc)
        h3r(DCH - 2)
        h3r(DCH - 1)
        if g < 3:
            X.wait(e, f"x2sp{g}", "qx2s")
    X.sw(e, e.engine_nop(), "vS")
    X.inc(e.reduce_max(c["macc"][:, 0:1], c["maccw"][:, 0:32], axis=AX), "vS")
    if not X.pass1:
        e.wait_ge(X.sems["vS"].h, X.sems["vS"].n)
    X.lbl("macc_done")
    X.wait(e, "m1_in", "qagm1")
    X.sw(e, e.reduce_max(c["s3r"][0][0:1, 0:1], c["s3r"][1][0:1, 0:NC],
                         axis=AX), "vS")
    X.lbl("gmax1")
    X.wait(e, "sig1", "aS")
    X.sw(e, e.tensor_scalar_add(c["tiny"][0:1, 1:2], c["tiny"][0:1, 1:2], 1.0),
         "vS")
    X.sw(e, e.reciprocal(c["tiny"][0:1, 1:2], c["tiny"][0:1, 1:2]), "vS")
    X.sw(e, e.tensor_mul(c["s3r"][0][0:1, 0:1], c["s3r"][0][0:1, 0:1],
                         c["tiny"][0:1, 1:2]), "vS")
    X.sw(e, e.tensor_scalar_mul(c["s3r"][0][0:1, 0:1], c["s3r"][0][0:1, 0:1],
                                1.0 / 127.0), "vS")
    X.inc(e.reciprocal(c["s3r"][0][0:1, 0:1], c["s3r"][0][0:1, 0:1]), "vS")
    X.lbl("sxr_row")
    X.wait(e, "sxrmm", "tS")
    X.sw(e, e.tensor_copy(c["sxr"][:, :], ps[0][:, 0:1]), "vS")
    X.sw(e, e.reciprocal(c["s2r"][:, 0:1], c["sxr"][:, 0:1]), "vS")
    X.sw(e, e.tensor_scalar_mul(c["cdg"][:, :], c["scg"][:, :],
                                c["s2r"][:, 0:1]), "vS")
    X.sw(e, e.tensor_scalar_mul(c["cdg"][:, :], c["cdg"][:, :], 1.0 / 127.0),
         "vS")
    X.sw(e, e.tensor_scalar_mul(c["cdu"][:, :], c["scu"][:, :],
                                c["s2r"][:, 0:1]), "vS")
    X.inc(e.tensor_scalar_mul(c["cdu"][:, :], c["cdu"][:, :], 1.0 / 127.0),
          "vS")
    X.lbl("cdg_done")


def ph3_scalar(X, e):
    c, dr, ps = X.c, X.dr, X.ps
    for g in range(4):
        X.wait(e, f"b1mm{g}", "tS")
        e.activation(c["br1s"][:, :], ps[6][:, 0:512], AF.Identity)
        X.inc(e.activation(c["bmr1s"][:, :], ps[7][:, 0:512], AF.Identity),
              "aS")
        X.lbl(f"b1evs{g}")
        X.lbl(f"b1ev{g}")
        if X.dbg == "pfull" and g == 0:
            X.fence(e, "aS")
            X.idma(e.dma_start(out=X.dr["dbg_b1"][:, 0:512],
                               in_=c["br1s"][:, :]), "qdbgb")
            X.idma(e.dma_start(out=X.dr["dbg_b1"][:, 512:1024],
                               in_=c["bmr1s"][:, :]), "qdbgb")
        for oc in range(DCH):
            wi = g * DCH + oc
            X.wait(e, f"hpre{wi}", "v2S")
            X.inc(e.activation(c["hpre"][wi % 2][:, :], c["hpre"][wi % 2][:, :],
                               AF.Identity, bias=mcol(X, "sa", oc),
                               scale=mcol(X, "ca1", oc)), "aS")
            X.lbl(f"haff{wi}")
        for oc in range(DCH):
            wi = g * DCH + oc
            X.wait(e, f"woev{wi}", "vS")
            if oc >= 2:
                X.wait(e, f"s3m{g}_{oc - 2}", "tS")
            elif g > 0:
                X.wait(e, f"h3r{g - 1}_{DCH - 2 + oc}", "vS")
                X.wait(e, f"h3sp{g - 1}_{DCH - 2 + oc}", f"qh3{oc % 2}")
            e.activation(c["x2b"][oc % 2][:, :],
                         c["x2g"][:, oc * 512:(oc + 1) * 512], AF.Identity)
            X.inc(e.activation(c["sqb"][oc % 2][:, :], c["x2b"][oc % 2][:, :],
                               AF.Square), "aS")
            X.lbl(f"x2b{g}_{oc}")
        X.wait(e, f"var3{g}", "vS")
        e.activation(c["s3r"][1][0:1, 0:512], c["s3r"][1][0:1, 0:512],
                     AF.Ln, bias=EPS)
        X.inc(e.activation(c["s3r"][1][0:1, 0:512], c["s3r"][1][0:1, 0:512],
                           AF.Exp, scale=-0.5), "aS")
        X.lbl(f"r3a{g}")
        for oc in range(DCH):
            X.wait(e, f"h3u{g}_{oc}", "vS")
            if g > 0 or oc >= 2:
                po, pg = (oc - 2, g) if oc >= 2 else (DCH - 2 + oc, g - 1)
                X.wait(e, f"h3sp{pg}_{po}", f"qh3{oc % 2}")
                X.wait(e, f"h3r{pg}_{po}", "vS")
            X.inc(e.activation(c["x2b"][oc % 2][:, :], c["sqb"][oc % 2][:, :],
                               AF.Identity, bias=mcol(X, "sb", oc),
                               scale=mcol(X, "cb1", oc)), "aS")
            X.lbl(f"h3a{g}_{oc}")
            X.fence(e, "aS")
            dst = bass.AP(dr["h3_sp"], (oc * 128) * T + g * 512,
                          [[T, 128], [1, 512]])
            X.idma(e.dma_start(out=dst, in_=c["x2b"][oc % 2][:, :]),
                   f"qh3{oc % 2}")
            X.lbl(f"h3sp{g}_{oc}")
        for oc in range(DCH):
            dst = bass.AP(dr["x2_sp"], (oc * 128) * T + g * 512,
                          [[T, 128], [1, 512]])
            X.idma(e.dma_start(out=dst, in_=c["x2g"][:, oc * 512:(oc + 1) * 512]),
                   "qx2s")
        X.lbl(f"x2sp{g}")
    X.inc(e.activation(c["tiny"][0:1, 1:2], c["clip"][0:1, 0:1], AF.Exp,
                       scale=-1.0), "aS")
    X.lbl("sig1")


def dbg_ph3_gp(X, e):
    c, dr = X.c, X.dr
    X.wait(e, "cdg_done", "vS")
    X.idma(e.dma_start(out=dr["dbg_wo"][:, :], in_=c["wo"][:, :]), "qdbg")
    X.idma(e.dma_start(out=dr["dbg_y"][:, :], in_=c["xs"][:, :]), "qdbg")
    X.idma(e.dma_start(out=bass.AP(dr["dbg_wo"], 0, [[1, 1], [1, 1]]),
                       in_=c["cTs"][0:1, 0:1]) if False else e.engine_nop(),
           "qdbg") if False else None
    X.idma(e.dma_start(out=dr["dbg_ct"][:, :], in_=c["cTs"][:, :]), "qdbg")
    d = dr["dbg_p3"]
    X.idma(e.dma_start(out=bass.AP(d, 0, [[16, 128], [1, 1]]),
                       in_=c["sxr"][:, :], allow_slow_non_contiguous=True), "qdbg")
    X.idma(e.dma_start(out=bass.AP(d, 1, [[16, 128], [1, 8]]),
                       in_=c["cdg"][:, 0:8], allow_slow_non_contiguous=True), "qdbg")
    if not X.pass1:
        e.wait_ge(X.sems["qdbg"].h, X.sems["qdbg"].n)


def msl(X, k):
    if k < 16:
        return X.c["m_a"], k * T
    return X.c["m_b"], (k - 16) * T


def ph5_sync(X, e):
    c, dr = X.c, X.dr
    X.wait(e, f"d_qsp{IFCH - 1}", "qqs2")
    for oc in range(IFCH):
        if oc >= 2:
            X.wait(e, f"fmm{oc - 2}", "tS")
        src = bass.AP(dr["wq_g"], oc * DCH * 128 * 128, [[DCH * 128, 128], [1, DCH * 128]])
        X.idma(e.dma_start(out=c["gcol"][oc % 2][:, :], in_=src), f"qfg{oc % 2}")
        src = bass.AP(dr["wq_u"], oc * DCH * 128 * 128, [[DCH * 128, 128], [1, DCH * 128]])
        X.idma(e.dma_start(out=c["ucol"][oc % 2][:, :], in_=src), f"qfu{oc % 2}")
        X.lbl(f"fld{oc}")


def ph5_vector(X, e):
    c, ps = X.c, X.ps
    ti = 0
    for g in range(4):
        for k in range(DCH):
            X.wait(e, f"xra{ti}", "aS")
            X.inc(e.tensor_scalar(
                c["h3"][:, k * T + g * 512:k * T + (g + 1) * 512],
                c["obuf"][ti % 2][:, :], 255.0, 128.0,
                ALU.min, ALU.subtract), "vS")
            X.lbl(f"xr{ti}")
            ti += 1
    X.lbl("xr_all")
    fi = 0
    for oc in range(IFCH):
        mt, mb = msl(X, oc)
        for g in range(4):
            bank = 2 * (fi % 2)
            X.wait(e, f"sgf{fi}", "aS")
            X.wait(e, f"umm{fi}", "tS")
            X.inc(e.scalar_tensor_tensor(
                mt[:, mb + g * 512:mb + (g + 1) * 512],
                ps[bank + 1][:, 0:512], c["cdu"][:, oc:oc + 1],
                c["hpre"][fi % 2][:, :], ALU.mult, ALU.mult), "vS")
            X.lbl(f"mev{fi}")
            X.inc(e.reduce_max(c["maccw"][:, 32 + fi:33 + fi],
                               mt[:, mb + g * 512:mb + (g + 1) * 512],
                               axis=AX, apply_absolute_value=True), "vS")
            X.lbl(f"mred{fi}")
            fi += 1
    X.sw(e, e.engine_nop(), "vS")
    X.inc(e.reduce_max(c["macc"][:, 1:2], c["maccw"][:, 32:120], axis=AX),
          "vS")
    if not X.pass1:
        e.wait_ge(X.sems["vS"].h, X.sems["vS"].n)
    X.lbl("m_done")


def ph5_gpsimd(X, e):
    c, dr = X.c, X.dr
    X.wait(e, "m_done", "vS")
    X.inc(e.tensor_reduce(c["s3r"][0][0:1, 1:2], c["macc"][:, 1:2],
                          axis=mybir.AxisListType.C, op=ALU.max), "gS")
    X.fence(e, "gS")
    X.idma(e.dma_start(out=bass.AP(dr["ag_m2_i"], 0, [[1, 1], [1, 1]]),
                       in_=c["s3r"][0][0:1, 1:2]), "qagm2")
    if not X.pass1:
        e.wait_ge(X.sems["qagm2"].h, X.sems["qagm2"].n)
    X.inc(e.collective_compute(
        "AllGather", ALU.bypass, replica_groups=[list(range(NC))],
        ins=[bass.AP(dr["ag_m2_i"], 0, [[1, 1], [1, 1]])],
        outs=[bass.AP(dr["ag_m2_o"], 0, [[1, 1], [1, NC]])]), "cS")
    if not X.pass1:
        e.wait_ge(X.sems["cS"].h, X.sems["cS"].n)
    X.idma(e.dma_start(out=c["s3r"][1][0:1, 0:NC],
                       in_=bass.AP(dr["ag_m2_o"], 0, [[1, 1], [1, NC]])),
           "qagm2")
    X.lbl("m2_in")


def ph5_scalar(X, e):
    c, ps = X.c, X.ps
    X.wait(e, "h3rl", "qh3r")
    X.wait(e, "cdg_done", "vS")
    ti = 0
    for g in range(4):
        for k in range(DCH):
            if ti >= 2:
                X.wait(e, f"xr{ti - 2}", "vS")
            e.activation(c["obuf"][ti % 2][:, :],
                         c["h3"][:, k * T + g * 512:k * T + (g + 1) * 512],
                         AF.Identity, scale=c["sxr"][:, 0:1], bias=MAGIC)
            X.inc(e.activation(c["obuf"][ti % 2][:, :], c["obuf"][ti % 2][:, :],
                               AF.Relu, bias=128.0 - MAGIC), "aS")
            X.lbl(f"xra{ti}")
            ti += 1
    X.inc(e.activation(c["tiny"][0:1, 0:1], c["clip"][0:1, 1:2], AF.Exp,
                       scale=-1.0), "aS")
    X.lbl("sig2")
    fi = 0
    for oc in range(IFCH):
        for g in range(4):
            bank = 2 * (fi % 2)
            X.wait(e, f"gmm{fi}", "tS")
            if fi >= 2:
                X.wait(e, f"mev{fi - 2}", "vS")
            X.inc(e.activation(c["hpre"][fi % 2][:, :], ps[bank][:, 0:512],
                               AF.Silu, scale=c["cdg"][:, oc:oc + 1]), "aS")
            X.lbl(f"sgf{fi}")
            fi += 1


def ph5_tensor(X, e):
    c, ps = X.c, X.ps
    fi = 0
    for oc in range(IFCH):
        X.wait(e, f"fld{oc}", f"qfg{oc % 2}", f"qfu{oc % 2}")
        for g in range(4):
            bank = 2 * (fi % 2)
            if fi >= 2:
                X.wait(e, f"mev{fi - 2}", "vS")
                X.wait(e, f"sgf{fi - 2}", "aS")
            if oc == 0:
                X.wait(e, f"xr{g * DCH + DCH - 1}", "vS")
            for k in range(DCH):
                inst = e.matmul(ps[bank][:, 0:512],
                                c["gcol"][oc % 2][:, k * 128:(k + 1) * 128],
                                c["h3"][:, k * T + g * 512:k * T + (g + 1) * 512],
                                start=(k == 0), stop=(k == DCH - 1))
            X.inc(inst, "tS")
            X.lbl(f"gmm{fi}")
            for k in range(DCH):
                inst = e.matmul(ps[bank + 1][:, 0:512],
                                c["ucol"][oc % 2][:, k * 128:(k + 1) * 128],
                                c["h3"][:, k * T + g * 512:k * T + (g + 1) * 512],
                                start=(k == 0), stop=(k == DCH - 1))
            X.inc(inst, "tS")
            X.lbl(f"umm{fi}")
            fi += 1
        X.lbl(f"fmm{oc}")


def _noop_t(e):
    return e.engine_nop()


def ph6_sync(X, e):
    c, dr = X.c, X.dr
    X.wait(e, "x2sp3", "qx2s")
    X.wait(e, f"fmm{IFCH - 1}", "tS")
    for g in range(4):
        if g >= 2:
            X.wait(e, f"outg{g - 2}", "vS")
        src = bass.AP(dr["x2_sp"], g * 512, [[T, 128], [128 * T, DCH], [1, 512]])
        X.idma(e.dma_start(out=c["x2rl"][:, (g % 2) * DCH * 512:
                                         (g % 2) * DCH * 512 + DCH * 512],
                           in_=src), f"qx2r{g % 2}")
        X.lbl(f"x2rl{g}")
        for oc in range(DCH):
            di = g * DCH + oc
            if di >= 2:
                X.wait(e, f"dmmt{di - 2}", "tS")
            src = bass.AP(dr["wq_d"], oc * IFCH * 128 * 128,
                          [[IFCH * 128, 128], [1, IFCH * 128]])
            X.idma(e.dma_start(out=c["dcol"][di % 2][:, :], in_=src),
                   f"qfd{di % 2}")
            X.lbl(f"dld{di}")


def ph6_vector(X, e):
    c, ps = X.c, X.ps
    X.wait(e, "m2_in", "qagm2")
    X.sw(e, e.reduce_max(c["s3r"][0][0:1, 1:2], c["s3r"][1][0:1, 0:NC],
                         axis=AX), "vS")
    X.wait(e, "sig2", "aS")
    X.sw(e, e.tensor_scalar_add(c["tiny"][0:1, 0:1], c["tiny"][0:1, 0:1], 1.0),
         "vS")
    X.sw(e, e.reciprocal(c["tiny"][0:1, 0:1], c["tiny"][0:1, 0:1]), "vS")
    X.sw(e, e.tensor_mul(c["s3r"][0][0:1, 1:2], c["s3r"][0][0:1, 1:2],
                         c["tiny"][0:1, 0:1]), "vS")
    X.inc(e.tensor_scalar_mul(c["s3r"][0][0:1, 1:2], c["s3r"][0][0:1, 1:2],
                              1.0 / 127.0), "vS")
    if not X.pass1:
        e.wait_ge(X.sems["vS"].h, X.sems["vS"].n)
    X.lbl("s2_row")
    X.wait(e, "s2mm", "tS")
    X.sw(e, e.tensor_copy(c["s2r"][:, 0:1], ps[6][:, 0:1]), "vS")
    X.sw(e, e.tensor_scalar_mul(c["cdd"][:, :], c["scd"][:, :],
                                c["s2r"][:, 0:1]), "vS")
    X.sw(e, e.tensor_scalar_mul(c["cdd"][:, :], c["cdd"][:, :], 1.0 / 127.0),
         "vS")
    X.sw(e, e.tensor_mul(c["cdd"][:, :], c["cdd"][:, :], c["modd"][:, 24:32]),
         "vS")
    X.sw(e, e.reciprocal(c["s2r"][:, 0:1], c["s2r"][:, 0:1]), "vS")
    X.lbl("cdd_done")
    if X.dbg == "pfull":
        X.wait(e, "m5_dump", "qdbgm")
    ti = 0
    di = 0
    for g in range(4):
        for k in range(IFCH):
            mt, mb = msl(X, k)
            X.wait(e, f"h2a{ti}", "aS")
            X.inc(e.tensor_scalar(
                mt[:, mb + g * 512:mb + (g + 1) * 512],
                c["hpre"][ti % 2][:, :], 255.0, 128.0,
                ALU.min, ALU.subtract), "vS")
            X.lbl(f"h2r{ti}")
            ti += 1
        for oc in range(DCH):
            bank = 6 + di % 2
            X.wait(e, f"dmmt{di}", "tS")
            X.wait(e, f"x2rl{g}", f"qx2r{g % 2}")
            X.inc(e.scalar_tensor_tensor(
                c["yscr"][:, :] if False else c["obuf"][di % 2][:, :],
                ps[bank][:, 0:512], c["cdd"][:, oc:oc + 1],
                c["x2rl"][:, (g % 2) * DCH * 512 + oc * 512:
                          (g % 2) * DCH * 512 + (oc + 1) * 512],
                ALU.mult, ALU.add), "vS")
            X.lbl(f"oev{di}")
            di += 1
        X.lbl(f"outg{g}")


def ph6_gpsimd(X, e):
    c, dr = X.c, X.dr
    di = 0
    for g in range(4):
        for oc in range(DCH):
            X.wait(e, f"oev{di}", "vS")
            dst = bass.AP(dr["outT"], (oc * 128) * T + g * 512,
                          [[T, 128], [1, 512]])
            X.idma(e.dma_start(out=dst, in_=c["obuf"][di % 2][:, :]),
                   f"qout{di % 2}")
            X.lbl(f"ost{di}")
            di += 1
    if not X.pass1:
        e.wait_ge(X.sems["qout0"].h, X.sems["qout0"].n)
        e.wait_ge(X.sems["qout1"].h, X.sems["qout1"].n)


def ph6_tensor(X, e):
    c, ps = X.c, X.ps
    X.wait(e, "s2_row", "vS")
    inst = e.matmul(ps[6][:, 0:1], c["onesr_f"][0:1, :],
                    c["s3r"][0][0:1, 1:2], start=True, stop=True)
    X.inc(inst, "tS")
    X.lbl("s2mm")
    X.wait(e, "cdd_done", "vS")
    di = 0
    for g in range(4):
        for oc in range(DCH):
            bank = 6 + di % 2
            X.wait(e, f"dld{di}", f"qfd{di % 2}")
            if di >= 2:
                X.wait(e, f"oev{di - 2}", "vS")
            for k in range(IFCH):
                mt, mb = msl(X, k)
                X.wait(e, f"h2r{g * IFCH + k}", "vS")
                inst = e.matmul(ps[bank][:, 0:512],
                                c["dcol"][di % 2][:, k * 128:(k + 1) * 128],
                                mt[:, mb + g * 512:mb + (g + 1) * 512],
                                start=(k == 0), stop=(k == IFCH - 1))
            X.inc(inst, "tS")
            X.lbl(f"dmmt{di}")
            di += 1


def ph6_scalar(X, e):
    c, ps = X.c, X.ps
    X.wait(e, "m_done", "vS")
    X.wait(e, "cdd_done", "vS")
    ti = 0
    for g in range(4):
        for k in range(IFCH):
            mt, mb = msl(X, k)
            if ti >= 2:
                X.wait(e, f"h2r{ti - 2}", "vS")
            e.activation(c["hpre"][ti % 2][:, :],
                         mt[:, mb + g * 512:mb + (g + 1) * 512],
                         AF.Identity, scale=c["s2r"][:, 0:1], bias=MAGIC)
            X.inc(e.activation(c["hpre"][ti % 2][:, :], c["hpre"][ti % 2][:, :],
                               AF.Relu, bias=128.0 - MAGIC), "aS")
            X.lbl(f"h2a{ti}")
            ti += 1


def dbg_ph5_gp(X, e):
    c, dr = X.c, X.dr
    X.wait(e, "m_done", "vS")
    X.wait(e, "xr_all", "vS")
    X.idma(e.dma_start(out=dr["dbg_m"][:, 0:16 * T], in_=c["m_a"][:, :]), "qdbg")
    X.idma(e.dma_start(out=dr["dbg_m"][:, 16 * T:], in_=c["m_b"][:, :]), "qdbg")
    X.idma(e.dma_start(out=dr["dbg_xr"][:, :], in_=c["h3"][:, :]), "qdbg")
    if not X.pass1:
        e.wait_ge(X.sems["qdbg"].h, X.sems["qdbg"].n)


def dbg_pfull_m(X, e):
    c, dr = X.c, X.dr
    X.wait(e, "m_done", "vS")
    X.wait(e, "xr_all", "vS")
    X.idma(e.dma_start(out=dr["dbg_m5"][:, 0:16 * T], in_=c["m_a"][:, :]),
           "qdbgm")
    X.idma(e.dma_start(out=dr["dbg_m5"][:, 16 * T:22 * T], in_=c["m_b"][:, :]),
           "qdbgm")
    X.lbl("m5_dump")


def dbg_pfull_end(X, e):
    c, dr = X.c, X.dr
    X.wait(e, "ost30", "qout0")
    X.wait(e, "ost31", "qout1")
    X.idma(e.dma_start(out=dr["dbg_hr"][:, 0:16 * T], in_=c["m_a"][:, :]),
           "qdbgf")
    X.idma(e.dma_start(out=dr["dbg_hr"][:, 16 * T:22 * T], in_=c["m_b"][:, :]),
           "qdbgf")
    X.idma(e.dma_start(out=dr["dbg_x2e"][:, :], in_=c["h3"][:, :]), "qdbgf")
    X.idma(e.dma_start(out=dr["dbg_sc2"][:, 0:DCH], in_=c["cdd"][:, :]),
           "qdbgf")
    X.idma(e.dma_start(out=dr["dbg_sc2"][:, DCH:DCH + 1], in_=c["s2r"][:, 0:1],
                       allow_slow_non_contiguous=True), "qdbgf")
    if not X.pass1:
        e.wait_ge(X.sems["qdbgf"].h, X.sems["qdbgf"].n)


def dbg_pfull_y(X, e):
    c, dr = X.c, X.dr
    X.wait(e, "cdg_done", "vS")
    X.wait(e, "macc_done", "vS")
    X.idma(e.dma_start(out=dr["dbg_y"][:, :], in_=c["xs"][:, :]), "qdbgy")
    if not X.pass1:
        e.wait_ge(X.sems["qdbgy"].h, X.sems["qdbgy"].n)


# version 22
# speedup vs baseline: 1.0275x; 1.0056x over previous
import numpy as np
import concourse.bass as bass
import concourse.mybir as mybir
from concourse.bass_utils import run_bass_kernel_spmd

F32 = mybir.dt.float32
BF16 = mybir.dt.bfloat16
AF = mybir.ActivationFunctionType
ALU = mybir.AluOpType
AX = mybir.AxisListType.X

D, DI, NS, IF = 1024, 2048, 16, 2816
T, PAD = 2048, 3
TP = T + PAD
NC = 8
DCH, DICH, IFCH = D // 128, DI // 128, IF // 128
WSH = 6 * D // NC
EPS = 1e-5
MAGIC = float(1.5 * 2 ** 23)

GRPS = [(0, PAD)] + [(PAD + 512 * g, 512) for g in range(4)]
P0G = [(0, 0, 512), (0, PAD, 512), (1, 0, 512), (2, 0, 512), (3, 0, 512)]
P0L = [(0, 515), (515, 512), (1027, 512), (1539, 512)]

DBG = False
_CACHE = {}
_FCONSTS = [EPS, -0.5, -1.0]
_FCONSTS128 = [MAGIC, -MAGIC, 128.0 - MAGIC]


def esz(dt):
    return {F32: 4, BF16: 2}[dt]


class _MockInst:
    def then_inc(self, sem, k):
        return self


class _MockEng:
    def __getattr__(self, name):
        def f(*a, **k):
            return _MockInst()
        return f


class Sem:
    def __init__(self, nc, name):
        self.h = nc.semaphore(name).__enter__()
        self.name = name
        self.n = 0


class Ctx:

    def __init__(self, nc):
        self.nc = nc
        self.c = {}
        self.dr = {}
        self.ps = []
        self.sems = {}
        self.labels = {}
        self.pass1 = True
        self.dbg = False
        self.scratch = {}

    def sem(self, name):
        if name not in self.sems:
            self.sems[name] = Sem(self.nc, name)
        return self.sems[name]

    def inc(self, inst, semname, k=1):
        s = self.sem(semname)
        s.n += k
        if not self.pass1:
            inst.then_inc(s.h, k)
        return inst

    def idma(self, inst, semname):
        return self.inc(inst, semname, 16)

    def lbl(self, name):
        if self.pass1:
            assert name not in self.labels, f"dup label {name}"
            self.labels[name] = {k: s.n for k, s in self.sems.items()}

    def fence(self, e, semname):
        if not self.pass1:
            e.wait_ge(self.sems[semname].h, self.sems[semname].n)

    def sw(self, e, inst, semname):
        self.inc(inst, semname)
        if not self.pass1:
            e.wait_ge(self.sems[semname].h, self.sems[semname].n)
        return inst

    def wait(self, e, label, *semnames):
        if self.pass1:
            return
        snap = self.labels[label]
        for sn in semnames:
            v = snap.get(sn, 0)
            if v > 0:
                e.wait_ge(self.sems[sn].h, v)

    def run_phases(self, phases):
        self.pass1 = True
        me = _MockEng()
        for name, secs in phases:
            for engname, fn in secs:
                fn(self, me)
        for s in self.sems.values():
            s.n = 0
        self.scratch = {}
        self.pass1 = False
        for name, secs in phases:
            with self.nc.Block(name) as blk:
                for engname, fn in secs:
                    getattr(blk, engname)(lambda e, fn=fn: fn(self, e))


class Arena:
    def __init__(self, nc, base, size, tag):
        self.nc, self.base, self.size, self.tag = nc, base, size, tag
        self.off = base
        self.pos = {}

    def alloc(self, name, shape, dtype, align=32):
        nbytes = int(np.prod(shape[1:])) * esz(dtype)
        off = (self.off + align - 1) // align * align
        assert off + nbytes <= self.base + self.size, \
            f"{self.tag}:{name} overflow by {off + nbytes - self.base - self.size}"
        t = self.nc.alloc_sbuf_tensor_at(name, list(shape), dtype, offset=off)
        self.pos[name] = off
        self.off = off + nbytes
        return t


def vsl(X, name):
    off = {"gam": 0, "bet": 8, "cw": 16, "cb": 80, "dv": 96, "bd": 112,
           "cvec": 128, "mflag": 136}[name]
    w = {"gam": 8, "bet": 8, "cw": 64, "cb": 16, "dv": 16, "bd": 16,
         "cvec": 8, "mflag": 1}[name]
    return X.c["vec"][:, off:off + w]


def mcol(X, name, k):
    base = {"sa": ("mod", 0), "ca": ("mod", 8), "ga": ("mod", 16),
            "sb": ("mod", 24), "cb2": ("mod", 32), "gb": ("mod", 40),
            "ca1": ("modd", 0), "cb1": ("modd", 8), "gap": ("modd", 16),
            "gbp": ("modd", 24), "gc": ("modd", 32), "gs": ("modd", 40)}[name]
    return X.c[base[0]][:, base[1] + k:base[1] + k + 1]


def p0xap(X, buf, k, w, off=0):
    return X.c["p0x"][:, buf * DCH * 515 + k * 515 + off:
                      buf * DCH * 515 + k * 515 + off + w]


def sb3(t, base, stride, nch, w, npart=128):
    freelen = int(t.shape[1])
    return bass.AP(t, base, [[freelen, npart], [stride, nch], [1, w]])


def p0grp(X, tname, buf, w, stride=515, nch=DCH):
    t = X.c[tname]
    return sb3(t, buf * nch * stride, stride, nch, w)


def build_program(dbg=False):
    nc = bass.Bass("TRN2", target_bir_lowering=False, detect_race_conditions=False)
    X = Ctx(nc)
    X.dbg = dbg
    dr, c = X.dr, X.c

    dr["xT"] = nc.dram_tensor("xT", [D, TP], F32, kind="ExternalInput")
    dr["w_in"] = nc.dram_tensor("w_in", [D, 2 * DI], F32, kind="ExternalInput")
    dr["wd"] = nc.dram_tensor("wd", [DI, DI], F32, kind="ExternalInput")
    dr["w_out"] = nc.dram_tensor("w_out", [DI, D], F32, kind="ExternalInput")
    dr["bT_t"] = nc.dram_tensor("bT_t", [128, NS * DICH], F32, kind="ExternalInput")
    dr["cT"] = nc.dram_tensor("cT", [NS, DI], F32, kind="ExternalInput")
    dr["wmod_sl"] = nc.dram_tensor("wmod_sl", [D, WSH], F32, kind="ExternalInput")
    dr["wgT"] = nc.dram_tensor("wgT", [D, IF], F32, kind="ExternalInput")
    dr["wuT"] = nc.dram_tensor("wuT", [D, IF], F32, kind="ExternalInput")
    dr["wdnT"] = nc.dram_tensor("wdnT", [IF, D], F32, kind="ExternalInput")
    dr["wgN"] = nc.dram_tensor("wgN", [IF, D], F32, kind="ExternalInput")
    dr["wuN"] = nc.dram_tensor("wuN", [IF, D], F32, kind="ExternalInput")
    dr["wdnN"] = nc.dram_tensor("wdnN", [D, IF], F32, kind="ExternalInput")
    dr["vecs"] = nc.dram_tensor("vecs", [128, 137], F32, kind="ExternalInput")
    dr["msks"] = nc.dram_tensor("msks", [16, 17], F32, kind="ExternalInput")
    dr["alog_in"] = nc.dram_tensor("alog_in", [NS, 1], F32, kind="ExternalInput")
    dr["clip_in"] = nc.dram_tensor("clip_in", [1, 2], F32, kind="ExternalInput")
    dr["outT"] = nc.dram_tensor("outT", [D, T], F32, kind="ExternalOutput")

    dr["sg_sp"] = nc.dram_tensor("sg_sp", [DICH, 128, T], BF16,
                                 kind="ExternalOutput" if dbg == "ph1"
                                 else "Internal")
    dr["x2_sp"] = nc.dram_tensor("x2_sp", [DCH, 128, T], F32,
                                 kind="ExternalOutput" if dbg in ("ph3", "pfull")
                                 else "Internal")
    dr["wq_g"] = nc.dram_tensor("wq_g", [IFCH, DCH, 128, 128], BF16)
    dr["wq_u"] = nc.dram_tensor("wq_u", [IFCH, DCH, 128, 128], BF16)
    dr["wq_d"] = nc.dram_tensor("wq_d", [DCH, IFCH, 128, 128], BF16)
    dr["ag_mod_i"] = nc.dram_tensor("ag_mod_i", [1, WSH], F32)
    dr["ag_mod_o"] = nc.dram_tensor("ag_mod_o", [NC, WSH], F32)
    dr["ag_car_i"] = nc.dram_tensor("ag_car_i", [1, 32], F32)
    dr["ag_car_o"] = nc.dram_tensor("ag_car_o", [NC, 32], F32)
    dr["ag_m1_i"] = nc.dram_tensor("ag_m1_i", [1, 8], F32)
    dr["ag_m1_o"] = nc.dram_tensor("ag_m1_o", [NC, 8], F32)
    dr["ag_m2_i"] = nc.dram_tensor("ag_m2_i", [1, 8], F32)
    dr["ag_m2_o"] = nc.dram_tensor("ag_m2_o", [NC, 8], F32)
    dr["srow2"] = nc.dram_tensor("srow2", [1, IF], F32, kind="Internal")
    dr["srow"] = nc.dram_tensor("srow", [1, 4096], F32)
    dr["h3_sp"] = nc.dram_tensor("h3_sp", [DCH, 128, T], BF16,
                                 kind="ExternalOutput" if dbg in ("ph3", "pfull")
                                 else "Internal")
    if dbg:
        dr["dbg_h2"] = nc.dram_tensor("dbg_h2", [128, DCH * TP], F32,
                                      kind="ExternalOutput")
        dr["dbg_mod"] = nc.dram_tensor("dbg_mod", [128, 96], F32,
                                       kind="ExternalOutput")
        dr["dbg_rows"] = nc.dram_tensor("dbg_rows", [1, 2 * TP], F32,
                                        kind="ExternalOutput")
        dr["dbg_h"] = nc.dram_tensor("dbg_h", [128, DCH * 515], F32,
                                     kind="ExternalOutput")
        if dbg == "ph1":
            dr["dbg_xs"] = nc.dram_tensor("dbg_xs", [128, DICH * T], F32,
                                          kind="ExternalOutput")
        if dbg == "ph2":
            dr["dbg_sc"] = nc.dram_tensor("dbg_sc", [16, 3 * T + 8], F32,
                                          kind="ExternalOutput")
        if dbg == "pfull":
            dr["dbg_y"] = nc.dram_tensor("dbg_y", [128, DICH * T], F32,
                                         kind="ExternalOutput")
            dr["dbg_b1"] = nc.dram_tensor("dbg_b1", [128, 1026], F32,
                                          kind="ExternalOutput")
            dr["dbg_hr"] = nc.dram_tensor("dbg_hr", [128, IFCH * T], F32,
                                          kind="ExternalOutput")
            dr["dbg_x2e"] = nc.dram_tensor("dbg_x2e", [128, DCH * T], BF16,
                                           kind="ExternalOutput")
            dr["dbg_sc2"] = nc.dram_tensor("dbg_sc2", [128, 16], F32,
                                           kind="ExternalOutput")
            dr["dbg_m5"] = nc.dram_tensor("dbg_m5", [128, IFCH * T], F32,
                                          kind="ExternalOutput")
        if dbg == "ph5":
            dr["dbg_m"] = nc.dram_tensor("dbg_m", [128, IFCH * T], F32,
                                         kind="ExternalOutput")
            dr["dbg_xr"] = nc.dram_tensor("dbg_xr", [128, DCH * T], F32,
                                          kind="ExternalOutput")
        if dbg == "ph3":
            dr["dbg_p3"] = nc.dram_tensor("dbg_p3", [128, 16], F32,
                                          kind="ExternalOutput")
            dr["dbg_wo"] = nc.dram_tensor("dbg_wo", [128, DICH * D], F32,
                                          kind="ExternalOutput")
            dr["dbg_y"] = nc.dram_tensor("dbg_y", [128, DICH * T], F32,
                                         kind="ExternalOutput")
            dr["dbg_ct"] = nc.dram_tensor("dbg_ct", [NS, DI], F32,
                                          kind="ExternalOutput")

    MB = 17408
    SZ_CONST, SZ_ROWS, SZ_STRM, SZ_H2, SZ_W4, SZ_XS = \
        7424, 37632, 34816, 32816, 32768, 65632
    A_CONST = Arena(nc, MB, SZ_CONST, "const")
    A_ROWS = Arena(nc, MB + SZ_CONST, SZ_ROWS, "rows")
    A_STRM = Arena(nc, MB + SZ_CONST + SZ_ROWS, SZ_STRM, "strm")
    R_H2 = MB + SZ_CONST + SZ_ROWS + SZ_STRM
    R_W4 = (R_H2 + SZ_H2 + 31) // 32 * 32
    R_XS = (R_W4 + SZ_W4 + 31) // 32 * 32
    assert R_XS + SZ_XS <= 229344, R_XS + SZ_XS
    X.A_STRM = A_STRM

    for n, sh, dt in [
            ("vec", [128, 137], F32), ("msk", [16, 17], F32),
            ("alog", [NS, 1], F32), ("Aneg", [NS, 1], F32),
            ("clip", [1, 2], F32), ("csil", [128, 8], F32),
            ("ones_b", [128, 1], BF16), ("onesr_f", [1, 128], F32),
            ("bT", [128, NS * DICH], BF16), ("cTs", [NS, DI], BF16),
            ("mod", [128, 48], F32), ("modd", [128, 48], F32),
            ("scg", [128, IFCH], F32), ("scu", [128, IFCH], F32),
            ("scd", [128, DCH], F32),
            ("cdg", [128, IFCH], F32), ("cdu", [128, IFCH], F32),
            ("cdd", [128, DCH], F32),
            ("sxr", [128, 1], F32), ("s2r", [128, 1], F32),
            ("macc", [128, 2], F32), ("carry", [NS, 1], F32),
            ("DcFc", [NS, 2], F32), ("gath", [NS, 16], F32),
            ("tiny", [128, 8], F32), ("tiny2", [128, 8], F32),
            ("fc", [1, 8], F32), ("fc128", [128, 8], F32),
            ("convc", [128, DICH * 4], BF16),
    ]:
        c[n] = A_CONST.alloc(n, sh, dt)

    for n, sh, dt in [
            ("r1row", [1, TP], F32), ("mr1row", [1, TP], F32),
            ("drow", [1, T], F32), ("hs", [NS, T], BF16),
            ("maccw", [128, 128], F32), ("rs1", [1, 512], F32),
            ("rs2", [1, 512], F32), ("rs3", [1, 512], F32),
            ("d16s", [NS, 512], F32),
    ]:
        c[n] = A_ROWS.alloc(n, sh, dt)
    c["rs0"] = c["drow"]
    dro = A_ROWS.pos["drow"]
    c["yscr"] = nc.alloc_sbuf_tensor_at("yscr", [128, 512], F32, offset=dro)
    c["gscr"] = [nc.alloc_sbuf_tensor_at(f"gscr{i}", [128, 512], BF16,
                                         offset=dro + 2048 + i * 1024)
                 for i in range(2)]
    c["hpre"] = [nc.alloc_sbuf_tensor_at(f"hpre{i}", [128, 512], F32,
                                         offset=dro + 4096 + i * 2048)
                 for i in range(2)]
    c["br1s"] = nc.alloc_sbuf_tensor_at("br1s", [128, 512], F32,
                                        offset=A_ROWS.pos["rs1"])
    c["bmr1s"] = nc.alloc_sbuf_tensor_at("bmr1s", [128, 512], F32,
                                         offset=A_ROWS.pos["rs2"])
    c["br3s"] = nc.alloc_sbuf_tensor_at("br3s", [128, 512], F32,
                                        offset=A_ROWS.pos["rs3"])
    c["bmr3s"] = nc.alloc_sbuf_tensor_at("bmr3s", [128, 512], F32,
                                         offset=A_ROWS.pos["d16s"])
    hso = A_ROWS.pos["hs"]
    c["x2b"] = [nc.alloc_sbuf_tensor_at(f"x2b{i}", [128, 512], BF16,
                                        offset=hso + i * 1024) for i in range(2)]
    c["sqb"] = [nc.alloc_sbuf_tensor_at(f"sqb{i}", [128, 512], BF16,
                                        offset=hso + 2048 + i * 1024)
                for i in range(2)]
    c["s3r"] = [nc.alloc_sbuf_tensor_at(f"s3r{i}", [1, 512], F32,
                                        offset=dro + i * 2048)
                for i in range(2)]
    c["gcol"] = [nc.alloc_sbuf_tensor_at(f"gcol{i}", [128, DCH * 128], BF16,
                                         offset=A_STRM.base + i * 2048)
                 for i in range(2)]
    c["ucol"] = [nc.alloc_sbuf_tensor_at(f"ucol{i}", [128, DCH * 128], BF16,
                                         offset=A_STRM.base + 4096 + i * 2048)
                 for i in range(2)]
    c["dcol"] = [nc.alloc_sbuf_tensor_at(f"dcol{i}", [128, IFCH * 128], BF16,
                                         offset=A_STRM.base + 8192 + i * 5632)
                 for i in range(2)]
    c["obuf"] = [nc.alloc_sbuf_tensor_at(f"obuf{i}", [128, 512], F32,
                                         offset=A_STRM.base + 19456 + i * 2048)
                 for i in range(2)]

    c["h2"] = nc.alloc_sbuf_tensor_at("h2", [128, DCH * TP], BF16, offset=R_H2)
    c["dA"] = nc.alloc_sbuf_tensor_at("dA", [NS, T], F32, offset=R_H2)
    c["Bx"] = nc.alloc_sbuf_tensor_at("Bx", [NS, T], F32, offset=R_H2 + 8192)
    c["h3"] = nc.alloc_sbuf_tensor_at("h3", [128, DCH * T], BF16, offset=R_H2)
    c["x2rl"] = nc.alloc_sbuf_tensor_at("x2rl", [128, 2 * DCH * 512], F32,
                                        offset=R_H2)
    c["xrel"] = nc.alloc_sbuf_tensor_at("xrel", [128, DCH * 512], F32,
                                        offset=R_H2)
    c["x2g"] = nc.alloc_sbuf_tensor_at("x2g", [128, DCH * 512], F32,
                                       offset=R_H2 + 16384)
    c["wo"] = nc.alloc_sbuf_tensor_at("wo", [128, DICH * D], BF16, offset=R_W4)

    c["m_b"] = nc.alloc_sbuf_tensor_at("m_b", [128, (IFCH - 16) * T], BF16,
                                       offset=R_W4)
    c["qnat"] = nc.alloc_sbuf_tensor_at("qnat", [128, D], F32, offset=R_W4)
    c["qtT"] = nc.alloc_sbuf_tensor_at("qtT", [128, IF], F32, offset=R_W4 + 4096)
    c["qout"] = [nc.alloc_sbuf_tensor_at("qout0", [128, IF], BF16,
                                         offset=R_W4 + 15360)] * 2
    c["rbc"] = nc.alloc_sbuf_tensor_at("rbc", [128, IF], F32,
                                       offset=R_W4 + 20992)
    c["xs"] = nc.alloc_sbuf_tensor_at("xs", [128, DICH * T], BF16, offset=R_XS)
    c["m_a"] = nc.alloc_sbuf_tensor_at("m_a", [128, 16 * T], BF16, offset=R_XS)
    c["p0x"] = nc.alloc_sbuf_tensor_at("p0x", [128, 2 * DCH * 515], BF16,
                                       offset=R_XS)
    c["p0h"] = nc.alloc_sbuf_tensor_at("p0h", [128, DCH * 515], F32,
                                       offset=R_XS + 16480)
    c["p0sq"] = nc.alloc_sbuf_tensor_at("p0sq", [128, DCH * 515], BF16,
                                        offset=R_XS + 32960)
    c["p0hb"] = nc.alloc_sbuf_tensor_at("p0hb", [128, DCH * 515], BF16,
                                        offset=R_XS + 41216)
    c["rbcd"] = nc.alloc_sbuf_tensor_at("rbcd", [128, D], F32,
                                        offset=R_H2 + 16384)
    c["dnat"] = nc.alloc_sbuf_tensor_at("dnat", [128, IF], F32,
                                        offset=R_H2 + 20480)

    X.ps = [nc.alloc_psum_tensor(f"psb{b}", [128, 512], F32) for b in range(8)]
    for i, v in enumerate(_FCONSTS):
        nc.const_aps.register(F32, v, c["fc"][0:1, i:i + 1])
    for i, v in enumerate(_FCONSTS128):
        nc.const_aps.register(F32, v, c["fc128"][:, i:i + 1])
    c["wmb"] = [A_STRM.alloc(f"wmb{i}", [128, WSH], F32) for i in range(2)]
    c["onesW"] = nc.alloc_sbuf_tensor_at("onesW", [128, 128], BF16,
                                         offset=A_STRM.base + 32768)
    c["sgg"] = [nc.alloc_sbuf_tensor_at(f"sgg{i}", [128, DICH * 512], BF16,
                                        offset=A_STRM.base + i * 16384)
                for i in range(2)]
    s1 = Arena(nc, A_STRM.base, A_STRM.size, "strm1")
    c["wcol"] = [s1.alloc(f"wcol{i}", [128, DCH * 128], BF16) for i in range(2)]
    c["xraw"] = s1.alloc("xraw", [128, TP], BF16)
    c["cacc"] = s1.alloc("cacc", [128, T], BF16)
    c["sgrow"] = [s1.alloc(f"sgrow{i}", [128, T], BF16) for i in range(2)]
    c["qrow"] = s1.alloc("qrow", [1, IF], F32)
    s2 = Arena(nc, A_STRM.base + 6144, A_STRM.size - 6144 - 11296, "strm2")
    c["wdcol"] = [s2.alloc(f"wdcol{i}", [128, DICH * 128], BF16) for i in range(2)]
    c["exps"] = [s2.alloc(f"exps{i}", [128, 512], F32) for i in range(2)]
    c["spb"] = [s2.alloc(f"spb{i}", [128, 512], BF16) for i in range(2)]

    phases = [("ph0", [("sync", ph0_sync), ("gpsimd", ph0_gpsimd),
                       ("scalar", ph0_scalar), ("tensor", ph0_tensor),
                       ("vector", ph0_vector)])]
    if dbg != "ph0":
        phases.append(("ph1", [("sync", ph1_sync), ("gpsimd", ph1_gpsimd),
                               ("scalar", ph1_scalar), ("tensor", ph1_tensor),
                               ("vector", ph1_vector)]))
    if dbg not in ("ph0", "ph1"):
        phases.append(("ph2", [("sync", ph2_sync), ("gpsimd", ph2_gpsimd),
                               ("scalar", ph2_scalar), ("tensor", ph2_tensor),
                               ("vector", ph2_vector)]))
    if dbg not in ("ph0", "ph1", "ph2"):
        phases.append(("ph3", [("sync", ph3_sync), ("gpsimd", ph3_gpsimd),
                               ("scalar", ph3_scalar), ("tensor", ph3_tensor),
                               ("vector", ph3_vector)]))
    if dbg not in ("ph0", "ph1", "ph2", "ph3"):
        phases.append(("ph5", [("sync", ph5_sync), ("gpsimd", ph5_gpsimd),
                               ("scalar", ph5_scalar), ("tensor", ph5_tensor),
                               ("vector", ph5_vector)]))
    if dbg not in ("ph0", "ph1", "ph2", "ph3", "ph5"):
        phases.append(("ph6", [("sync", ph6_sync), ("gpsimd", ph6_gpsimd),
                               ("scalar", ph6_scalar), ("tensor", ph6_tensor),
                               ("vector", ph6_vector)]))

    if dbg == "ph0":
        phases.append(("dbg", [("sync", dbg_ph0_sync),
                               ("gpsimd", dbg_ph0_gpsimd)]))
    if dbg == "ph1":
        phases.append(("dbg", [("gpsimd", dbg_ph1_gp)]))
    if dbg == "ph2":
        phases.append(("dbg", [("gpsimd", dbg_ph2_gp)]))
    if dbg == "ph3":
        phases.append(("dbg", [("gpsimd", dbg_ph3_gp)]))
    if dbg == "ph5":
        phases.append(("dbg", [("gpsimd", dbg_ph5_gp)]))
    if dbg == "pfull":
        phases.insert([i for i, (n, _) in enumerate(phases)
                       if n == "ph3"][0] + 1,
                      ("dbgy", [("gpsimd", dbg_pfull_y)]))
        phases.insert([i for i, (n, _) in enumerate(phases)
                       if n == "ph5"][0] + 1,
                      ("dbgm", [("gpsimd", dbg_pfull_m)]))
        phases.append(("dbgf", [("gpsimd", dbg_pfull_end)]))
    X.run_phases(phases)
    return nc, X


def ph0_sync(X, e):
    c, dr = X.c, X.dr
    for dst, src in [(c["vec"], dr["vecs"]), (c["msk"], dr["msks"]),
                     (c["alog"], dr["alog_in"]), (c["clip"], dr["clip_in"])]:
        X.idma(e.dma_start(out=dst[:, :], in_=src[:, :]), "qs")
    X.lbl("consts")


def ph0_gpsimd(X, e):
    c, dr = X.c, X.dr
    e.memset(c["ones_b"][:, :], 1.0)
    e.memset(c["onesW"][:, :], 1.0)
    if X.dbg:
        e.memset(c["p0h"][:, :], 0.0)
        e.memset(c["p0x"][:, :], 0.0)
        e.memset(c["p0sq"][:, :], 0.0)
        e.memset(c["p0hb"][:, :], 0.0)
    for i, v in enumerate(_FCONSTS):
        e.memset(c["fc"][0:1, i:i + 1], v)
    for i, v in enumerate(_FCONSTS128):
        e.memset(c["fc128"][:, i:i + 1], v)
    X.inc(e.memset(c["onesr_f"][:, :], 1.0), "gS")
    X.lbl("ones")
    wmb = X.c["wmb"]
    for k in range(DCH):
        if k >= 2:
            X.wait(e, f"modmm{k - 2}", "tS")
        src = bass.AP(dr["wmod_sl"], k * 128 * WSH, [[WSH, 128], [1, WSH]])
        X.idma(e.dma_start(out=wmb[k % 2][:, :], in_=src), f"qwm{k % 2}")
        X.lbl(f"wm{k}")
    X.idma(e.dma_start(out=c["bT"][:, :], in_=dr["bT_t"][:, :]), "qbc")
    X.idma(e.dma_start(out=c["cTs"][:, :], in_=dr["cT"][:, :]), "qbc")
    def xload(j):
        g0, w = P0L[j]
        src = bass.AP(dr["xT"], g0, [[TP, 128], [128 * TP, DCH], [1, w]])
        dst = sb3(c["p0x"], (j % 2) * DCH * 515, 515, DCH, w)
        X.idma(e.dma_start(out=dst, in_=src), f"qx{j % 2}")
        X.lbl(f"x{j}")
    xload(0)
    xload(1)
    X.wait(e, "mod_row", "vS")
    X.idma(e.dma_start(out=dr["ag_mod_i"][:, :], in_=c["rs0"][0:1, 0:WSH]), "qagm")
    if not X.pass1:
        e.wait_ge(X.sems["qagm"].h, X.sems["qagm"].n)
    X.inc(e.collective_compute(
        "AllGather", ALU.bypass, replica_groups=[list(range(NC))],
        ins=[dr["ag_mod_i"].ap()], outs=[dr["ag_mod_o"].ap()]), "cS")
    if not X.pass1:
        e.wait_ge(X.sems["cS"].h, X.sems["cS"].n)
    for col in range(48):
        v, ch = col // 8, col % 8
        s2 = bass.AP(dr["ag_mod_o"], v * 1024 + ch * 128, [[1, 128], [1, 1]])
        X.idma(e.dma_start(out=c["mod"][:, col:col + 1], in_=s2), "qmod")
    X.lbl("mod_in")
    xg_last = {0: 1, 1: 2, 2: 3, 3: 4}
    for j in range(2, 4):
        X.wait(e, f"h2_{xg_last[j - 2]}", "vS")
        xload(j)


def ph0_scalar(X, e):
    c = X.c
    X.wait(e, "consts", "qs")
    X.inc(e.activation(c["csil"][:, :], vsl(X, "cvec"), AF.Silu), "aS")
    X.lbl("csil")
    X.inc(e.activation(c["Aneg"][:, :], c["alog"][:, :], AF.Exp), "aS")
    X.inc(e.mul(c["Aneg"][:, :], c["Aneg"][:, :], -1.0), "aS")
    for gi, (lj, loff, w) in enumerate(P0G):
        g0 = GRPS[gi][0]
        buf = lj % 2
        X.wait(e, f"x{lj}", f"qx{lj % 2}")
        if gi >= 1:
            X.wait(e, f"h2_{gi - 1}", "vS")
        X.inc(e.activation(
            sb3(c["p0sq"], loff, 515, DCH, w),
            sb3(c["p0x"], buf * DCH * 515 + loff, 515, DCH, w),
            AF.Square), "aS")
        X.lbl(f"sq1_{gi}")

        X.wait(e, f"var1_{gi}", "vS")
        X.inc(e.activation(c["rs3"][0:1, 0:w], c["rs3"][0:1, 0:w], AF.Ln,
                           bias=EPS), "aS")
        X.inc(e.activation(c["r1row"][0:1, g0:g0 + w], c["rs3"][0:1, 0:w],
                           AF.Exp, scale=-0.5), "aS")
        X.lbl(f"r1_{gi}")
        X.wait(e, f"h_{gi}", "vS")
        X.inc(e.activation(
            sb3(c["p0sq"], loff, 515, DCH, w),
            sb3(c["p0h"], loff, 515, DCH, w),
            AF.Square), "aS")
        X.lbl(f"sq2_{gi}")
        X.wait(e, f"var2_{gi}", "vS")
        X.inc(e.activation(c["rs3"][0:1, 0:w], c["rs3"][0:1, 0:w], AF.Ln,
                           bias=EPS), "aS")
        X.inc(e.activation(c["rs2"][0:1, 0:w], c["rs3"][0:1, 0:w],
                           AF.Exp, scale=-0.5), "aS")
        X.lbl(f"r2_{gi}")


def ph0_tensor(X, e):
    c, ps = X.c, X.ps
    wmb = X.c["wmb"]
    X.wait(e, "csil", "aS")
    X.wait(e, "ones", "gS")
    for k in range(DCH):
        X.wait(e, f"wm{k}", f"qwm{k % 2}")
        for j in range(2):
            inst = e.matmul(ps[6 + j][0:1, 0:WSH // 2], c["csil"][:, k:k + 1],
                            wmb[k % 2][:, j * (WSH // 2):(j + 1) * (WSH // 2)],
                            start=(k == 0), stop=(k == DCH - 1))
        X.inc(inst, "tS")
        X.lbl(f"modmm{k}")
    X.lbl("mod_mm")
    for gi, (lj, loff, w) in enumerate(P0G):
        g0 = GRPS[gi][0]
        buf = lj % 2
        X.wait(e, f"x{lj}", f"qx{lj % 2}")
        if gi == 0:
            X.wait(e, "mod_row", "vS")
        if gi >= 1:
            X.wait(e, f"var2_{gi - 1}", "vS")
        for k in range(DCH):
            e.matmul(ps[7][:, 0:w], c["onesW"][:, :],
                     p0xap(X, buf, k, w, off=loff),
                     start=(k == 0), stop=(k == DCH - 1))
        X.wait(e, f"sq1_{gi}", "aS")
        for k in range(DCH):
            inst = e.matmul(ps[6][:, 0:w], c["onesW"][:, :],
                            c["p0sq"][:, k * 515 + loff:k * 515 + loff + w],
                            start=(k == 0), stop=(k == DCH - 1))
        X.inc(inst, "tS")
        X.lbl(f"st1_{gi}")
        X.wait(e, f"mr1_{gi}", "vS")
        if gi >= 1:
            X.wait(e, f"h_{gi - 1}", "vS")
        e.matmul(ps[5][:, 0:w], c["onesr_f"][0:1, :],
                 c["r1row"][0:1, g0:g0 + w], start=True, stop=True)
        inst = e.matmul(ps[4][:, 0:w], c["onesr_f"][0:1, :],
                        c["mr1row"][0:1, g0:g0 + w], start=True, stop=True)
        X.inc(inst, "tS")
        X.lbl(f"bc1_{gi}")
        X.wait(e, f"hb_{gi}", "vS")
        X.wait(e, f"rows1_{gi}", "vS")
        for k in range(DCH):
            e.matmul(ps[7][:, 0:w], c["onesW"][:, :],
                     c["p0hb"][:, k * 515 + loff:k * 515 + loff + w],
                     start=(k == 0), stop=(k == DCH - 1))
        X.wait(e, f"sq2_{gi}", "aS")
        for k in range(DCH):
            inst = e.matmul(ps[6][:, 0:w], c["onesW"][:, :],
                            c["p0sq"][:, k * 515 + loff:k * 515 + loff + w],
                            start=(k == 0), stop=(k == DCH - 1))
        X.inc(inst, "tS")
        X.lbl(f"st2_{gi}")
        X.wait(e, f"mr2_{gi}", "vS")
        if gi >= 1:
            X.wait(e, f"h2_{gi - 1}", "vS")
        e.matmul(ps[3][:, 0:w], c["onesr_f"][0:1, :], c["rs2"][0:1, 0:w],
                 start=True, stop=True)
        inst = e.matmul(ps[2][:, 0:w], c["onesr_f"][0:1, :], c["rs3"][0:1, 0:w],
                        start=True, stop=True)
        X.inc(inst, "tS")
        X.lbl(f"bc2_{gi}")


def ph0_vector(X, e):
    c, ps = X.c, X.ps
    X.wait(e, "mod_mm", "tS")
    e.tensor_copy(c["rs0"][0:1, 0:WSH // 2], ps[6][0:1, 0:WSH // 2])
    X.inc(e.tensor_copy(c["rs0"][0:1, WSH // 2:WSH], ps[7][0:1, 0:WSH // 2]),
          "vS")
    X.lbl("mod_row")
    X.wait(e, "mod_in", "qmod")
    md, mo = c["modd"], c["mod"]
    X.sw(e, e.tensor_scalar_add(md[:, 0:8], mo[:, 8:16], 1.0), "vS")
    X.sw(e, e.tensor_scalar_add(md[:, 8:16], mo[:, 32:40], 1.0), "vS")
    for (srcc, dstc) in ((16, 16), (40, 24)):
        g = mo[:, srcc:srcc + 8]
        X.sw(e, e.tensor_scalar_mul(c["tiny"][:, 0:8], g, -1.0), "vS")
        X.sw(e, e.tensor_tensor(c["tiny"][:, 0:8], c["tiny"][:, 0:8], g,
                                ALU.max), "vS")
        X.sw(e, e.tensor_scalar_max(c["tiny"][:, 0:8], c["tiny"][:, 0:8],
                                    0.25), "vS")
        sg = md[:, dstc:dstc + 8]
        X.sw(e, e.tensor_scalar_add(sg, g, 1e-38), "vS")
        for _ in range(5):
            X.sw(e, e.tensor_scalar_mul(sg, sg, 1e9), "vS")
            X.sw(e, e.tensor_scalar_min(sg, sg, 1.0), "vS")
            X.sw(e, e.tensor_scalar_max(sg, sg, -1.0), "vS")
        X.sw(e, e.tensor_mul(md[:, dstc:dstc + 8], md[:, dstc:dstc + 8],
                             c["tiny"][:, 0:8]), "vS")
    X.sw(e, e.tensor_mul(md[:, 32:40], md[:, 16:24], md[:, 0:8]), "vS")
    X.inc(e.tensor_mul(md[:, 40:48], md[:, 16:24], mo[:, 0:8]), "vS")
    X.lbl("mod_done")
    for gi, (lj, loff, w) in enumerate(P0G):
        g0 = GRPS[gi][0]
        buf = lj % 2
        X.wait(e, f"st1_{gi}", "tS")
        e.tensor_scalar_mul(c["rs1"][0:1, 0:w], ps[7][0:1, 0:w], 1.0 / D)
        e.tensor_scalar_mul(c["rs2"][0:1, 0:w], ps[6][0:1, 0:w], 1.0 / D)
        e.tensor_mul(c["rs3"][0:1, 0:w], c["rs1"][0:1, 0:w], c["rs1"][0:1, 0:w])
        X.inc(e.tensor_sub(c["rs3"][0:1, 0:w], c["rs2"][0:1, 0:w],
                           c["rs3"][0:1, 0:w]), "vS")
        X.lbl(f"var1_{gi}")
        X.lbl(f"rows1_{gi}")
        X.wait(e, f"r1_{gi}", "aS")
        X.inc(e.scalar_tensor_tensor(c["mr1row"][0:1, g0:g0 + w],
                                     c["rs1"][0:1, 0:w], -1.0,
                                     c["r1row"][0:1, g0:g0 + w],
                                     ALU.mult, ALU.mult), "vS")
        X.lbl(f"mr1_{gi}")
        X.wait(e, f"bc1_{gi}", "tS")
        X.wait(e, "mod_done", "vS")
        for k in range(DCH):
            kb = k * 515 + loff
            e.tensor_scalar(
                c["p0h"][:, kb:kb + w], ps[4][:, 0:w],
                mcol(X, "ca1", k), mcol(X, "sa", k), ALU.mult, ALU.add)
            e.tensor_mul(c["p0sq"][:, kb:kb + w],
                         p0xap(X, buf, k, w, off=loff), ps[5][:, 0:w])
            e.scalar_tensor_tensor(
                c["p0h"][:, kb:kb + w],
                c["p0sq"][:, kb:kb + w], mcol(X, "ca1", k),
                c["p0h"][:, kb:kb + w], ALU.mult, ALU.add)
        X.inc(e.engine_nop(), "vS")
        X.lbl(f"h_{gi}")
        X.inc(e.tensor_copy(
            sb3(c["p0hb"], loff, 515, DCH, w),
            sb3(c["p0h"], loff, 515, DCH, w)), "vS")
        X.lbl(f"hb_{gi}")
        X.wait(e, f"st2_{gi}", "tS")
        e.tensor_scalar_mul(c["rs1"][0:1, 0:w], ps[7][0:1, 0:w], 1.0 / D)
        e.tensor_scalar_mul(c["rs2"][0:1, 0:w], ps[6][0:1, 0:w], 1.0 / D)
        e.tensor_mul(c["rs3"][0:1, 0:w], c["rs1"][0:1, 0:w], c["rs1"][0:1, 0:w])
        X.inc(e.tensor_sub(c["rs3"][0:1, 0:w], c["rs2"][0:1, 0:w],
                           c["rs3"][0:1, 0:w]), "vS")
        X.lbl(f"var2_{gi}")
        X.wait(e, f"r2_{gi}", "aS")
        X.inc(e.scalar_tensor_tensor(c["rs3"][0:1, 0:w], c["rs1"][0:1, 0:w],
                                     -1.0, c["rs2"][0:1, 0:w],
                                     ALU.mult, ALU.mult), "vS")
        X.lbl(f"mr2_{gi}")
        X.wait(e, f"bc2_{gi}", "tS")
        for k in range(DCH):
            kb = k * 515 + loff
            e.tensor_scalar(
                c["p0h"][:, kb:kb + w], ps[2][:, 0:w],
                vsl(X, "gam")[:, k:k + 1], vsl(X, "bet")[:, k:k + 1],
                ALU.mult, ALU.add)
            e.tensor_mul(c["p0sq"][:, kb:kb + w],
                         c["p0hb"][:, kb:kb + w], ps[3][:, 0:w])
            e.scalar_tensor_tensor(
                c["h2"][:, k * TP + g0:k * TP + g0 + w],
                c["p0sq"][:, kb:kb + w],
                vsl(X, "gam")[:, k:k + 1],
                c["p0h"][:, kb:kb + w], ALU.mult, ALU.add)
        X.inc(e.engine_nop(), "vS")
        X.lbl(f"h2_{gi}")
    X.lbl("h2_all")


def _noop(e):
    return e.engine_nop()


def dbg_ph0_sync(X, e):
    c, dr = X.c, X.dr
    X.wait(e, "h2_all", "vS")
    X.idma(e.dma_start(out=dr["dbg_rows"][0:1, 0:TP], in_=c["r1row"][:, :]), "qs")
    X.idma(e.dma_start(out=dr["dbg_rows"][0:1, TP:2 * TP], in_=c["mr1row"][:, :]),
           "qs")
    X.idma(e.dma_start(out=dr["dbg_mod"][:, 0:48], in_=c["mod"][:, :]), "qs")
    X.idma(e.dma_start(out=dr["dbg_mod"][:, 48:96], in_=c["modd"][:, :]), "qs")
    if not X.pass1:
        e.wait_ge(X.sems["qs"].h, X.sems["qs"].n)


def dbg_ph0_gpsimd(X, e):
    c, dr = X.c, X.dr
    X.wait(e, "h2_all", "vS")
    X.idma(e.dma_start(out=dr["dbg_h2"][:, :], in_=c["h2"][:, :]), "qdbg")
    X.idma(e.dma_start(out=dr["dbg_h"][:, :], in_=c["p0h"][:, :]), "qdbg")
    if not X.pass1:
        e.wait_ge(X.sems["qdbg"].h, X.sems["qdbg"].n)


def _stage_inputs(inputs):
    x = np.asarray(inputs["L0_attn_x"])[0]
    xTfull = np.ascontiguousarray(x.T)

    def tile_vec(v, w):
        return np.ascontiguousarray(np.asarray(v).reshape(w, 128).T).astype(np.float32)

    gam = tile_vec(inputs["L0_attn_ln_gamma"], DCH)
    bet = tile_vec(inputs["L0_attn_ln_beta"], DCH)
    cwr = np.asarray(inputs["L0_attn_conv_w"]).reshape(DI, 4)
    cw = np.ascontiguousarray(
        cwr.reshape(DICH, 128, 4).transpose(1, 0, 2).reshape(128, DICH * 4))
    cb = tile_vec(inputs["L0_attn_conv_b"], DICH)
    dv = tile_vec(inputs["L0_attn_D_vec"], DICH)
    bd = tile_vec(inputs["L0_attn_bd"], DICH)
    cvec = np.ascontiguousarray(
        np.asarray(inputs["L0_c"]).reshape(DCH, 128).T).astype(np.float32)
    vecs = np.concatenate([gam, bet, cw.astype(np.float32), cb, dv, bd, cvec],
                          axis=1)
    assert vecs.shape == (128, 136)

    bT_t = np.ascontiguousarray(
        np.asarray(inputs["L0_attn_B_mat"]).T.reshape(DICH, 128, NS)
        .transpose(1, 0, 2).reshape(128, DICH * NS)).astype(np.float32)
    cT = np.ascontiguousarray(np.asarray(inputs["L0_attn_C_mat"]).T).astype(np.float32)
    alog = np.asarray(inputs["L0_attn_A_log"]).reshape(NS, 1).astype(np.float32)
    clip2 = np.array([[float(np.asarray(inputs["L0_ffn_clip_x"]).reshape(())),
                       float(np.asarray(inputs["L0_ffn_clip_down"]).reshape(()))]],
                     np.float32)
    wgT = np.ascontiguousarray(np.asarray(inputs["L0_ffn_w_gate"]).T)
    wuT = np.ascontiguousarray(np.asarray(inputs["L0_ffn_w_up"]).T)
    wdnT = np.ascontiguousarray(np.asarray(inputs["L0_ffn_w_down"]).T)
    wmod = np.asarray(inputs["L0_wmod"])

    shared = dict(
        w_in=np.ascontiguousarray(np.asarray(inputs["L0_attn_W_in"])),
        wd=np.ascontiguousarray(np.asarray(inputs["L0_attn_Wd"])),
        w_out=np.ascontiguousarray(np.asarray(inputs["L0_attn_W_out"])),
        bT_t=bT_t, cT=cT,
        wgN=np.ascontiguousarray(np.asarray(inputs["L0_ffn_w_gate"])),
        wuN=np.ascontiguousarray(np.asarray(inputs["L0_ffn_w_up"])),
        wdnN=np.ascontiguousarray(np.asarray(inputs["L0_ffn_w_down"])),
        wgT=wgT, wuT=wuT, wdnT=wdnT,
        alog_in=alog, clip_in=clip2,
    )
    in_maps = []
    for cid in range(NC):
        s = cid * T
        if cid == 0:
            xc = np.concatenate([np.zeros((D, PAD), np.float32),
                                 xTfull[:, :T]], axis=1)
        else:
            xc = np.ascontiguousarray(xTfull[:, s - PAD:s + T])
        m = np.zeros((16, 17), np.float32)
        m[:, :8] = (np.arange(8) < cid).astype(np.float32)
        m[:, 8:16] = (np.arange(8) >= cid).astype(np.float32)
        m[:, 16] = 0.0 if cid == 0 else 1.0
        im = dict(shared)
        flag = np.full((128, 1), 0.0 if cid == 0 else 1.0, np.float32)
        im["vecs"] = np.ascontiguousarray(np.concatenate([vecs, flag], axis=1))
        im["xT"] = np.ascontiguousarray(xc)
        im["msks"] = m
        im["wmod_sl"] = np.ascontiguousarray(wmod[:, cid * WSH:(cid + 1) * WSH])
        in_maps.append(im)
    return in_maps


def kernel(**inputs):
    in_maps = _stage_inputs(inputs)
    if "main" not in _CACHE:
        _CACHE["main"] = build_program(dbg=DBG)
    nc, X = _CACHE["main"]
    res = run_bass_kernel_spmd(nc, in_maps, core_ids=list(range(NC)))
    outs = [res.results[cid]["outT"] for cid in range(NC)]
    full = np.concatenate([np.asarray(o).T for o in outs], axis=0)
    return np.ascontiguousarray(full[None]).astype(np.float32)


def _ph1_tiles():
    tiles = []
    for oc in range(32):
        for gi, (g0, w) in enumerate(GRPS):
            if oc >= 16 and gi == 0:
                continue
            tiles.append((oc, gi, g0, w))
    return tiles


def _ev_sem(oc):
    return "vS" if oc < 16 else "aS"


def ph1_sync(X, e):
    c, dr = X.c, X.dr
    for ic in range(IFCH):
        if ic >= 1:
            X.wait(e, f"g_red{ic - 1}", "vS")
        src = bass.AP(dr["wgN"], ic * 128 * D, [[D, 128], [1, D]])
        X.idma(e.dma_start(out=c["qnat"][:, :], in_=src), "qn")
        X.lbl(f"g_natld{ic}")
    X.wait(e, "g_scg_done", "vS")
    for ic in range(IFCH):
        dst = bass.AP(dr["srow"], ic * 128, [[1, 128], [1, 1]])
        X.idma(e.dma_start(out=dst, in_=c["scg"][:, ic:ic + 1]), "qsr")
    if not X.pass1:
        e.wait_ge(X.sems["qsr"].h, X.sems["qsr"].n)
    X.idma(e.dma_start(out=c["qrow"][0:1, 0:IF],
                       in_=bass.AP(dr["srow"], 0, [[1, 1], [1, IF]])), "qsr")
    X.lbl("g_srow_in")
    X.wait(e, "g_recip_row", "vS")
    X.idma(e.dma_start(out=dr["srow2"].ap(), in_=c["qrow"][0:1, 0:IF]), "qrbc")
    if not X.pass1:
        e.wait_ge(X.sems["qrbc"].h, X.sems["qrbc"].n)
    X.idma(e.dma_start(out=c["rbc"][:, 0:IF],
                       in_=bass.AP(dr["srow2"], 0, [[0, 128], [1, IF]])),
           "qrbc")
    X.lbl("rbc_g")


def ph1_gpsimd(X, e):
    c, dr = X.c, X.dr
    tiles = _ph1_tiles()
    last_of_oc = {}
    for ti, (oc, gi, g0, w) in enumerate(tiles):
        last_of_oc[oc] = ti

    def wcol_load(oc):
        if oc >= 2:
            X.wait(e, f"wmm{oc - 2}", "tS")
        src = bass.AP(dr["w_in"], oc * 128,
                      [[2 * DI, 128], [128 * 2 * DI, DCH], [1, 128]])
        X.idma(e.dma_start(out=c["wcol"][oc % 2][:, :], in_=src), f"qw{oc % 2}")
        X.lbl(f"wld{oc}")

    wcol_load(0)
    wcol_load(1)
    nxt = 2
    while nxt < 23:
        wcol_load(nxt)
        nxt += 1
    while nxt < 32:
        wcol_load(nxt)
        nxt += 1


def ph1_tensor(X, e):
    c, ps = X.c, X.ps
    tiles = _ph1_tiles()
    ev_engine = {}
    for ti, (oc, gi, g0, w) in enumerate(tiles):
        ev_engine[ti] = _ev_sem(oc)
    first_of_oc = set()
    seen = set()
    for ti, (oc, gi, g0, w) in enumerate(tiles):
        if oc not in seen:
            seen.add(oc)
            first_of_oc.add(ti)
    for ti, (oc, gi, g0, w) in enumerate(tiles):
        if ti in first_of_oc:
            X.wait(e, f"wld{oc}", f"qw{oc % 2}")
            if oc == 0:
                X.wait(e, "h2_all", "vS")
        bank = ti % 4
        if ti >= 4:
            X.wait(e, f"ev{ti - 4}", ev_engine[ti - 4])
        for k in range(DCH):
            inst = e.matmul(ps[bank][:, 0:w],
                            c["wcol"][oc % 2][:, k * 128:(k + 1) * 128],
                            c["h2"][:, k * TP + g0:k * TP + g0 + w],
                            start=(k == 0), stop=(k == DCH - 1))
        X.inc(inst, "tS")
        X.lbl(f"mmt{ti}")
        if ti == max(t2 for t2, (o2, _, _, _) in enumerate(tiles) if o2 == oc):
            X.lbl(f"wmm{oc}")


def ph1_vector(X, e):
    c, ps = X.c, X.ps
    tiles = _ph1_tiles()
    for ti, (oc, gi, g0, w) in enumerate(tiles):
        if oc >= 16:
            continue
        bank = ti % 4
        X.wait(e, f"mmt{ti}", "tS")
        X.inc(e.tensor_copy(c["xraw"][:, g0:g0 + w], ps[bank][:, 0:w]), "vS")
        X.lbl(f"ev{ti}")
        if gi == len(GRPS) - 1:
            if oc >= 1:
                X.wait(e, f"xs{oc - 1}", "aS")
            e.tensor_scalar_mul(c["xraw"][:, 0:PAD], c["xraw"][:, 0:PAD],
                                vsl(X, "mflag")[:, 0:1])
            e.tensor_scalar_mul(c["cacc"][:, :], c["xraw"][:, 0:T],
                                vsl(X, "cw")[:, oc * 4:oc * 4 + 1])
            for j in range(1, 4):
                inst = e.scalar_tensor_tensor(
                    c["cacc"][:, :], c["xraw"][:, j:j + T],
                    vsl(X, "cw")[:, oc * 4 + j:oc * 4 + j + 1],
                    c["cacc"][:, :], ALU.mult, ALU.add)
            X.inc(inst, "vS")
            X.lbl(f"conv{oc}")
            for _ in range(2):
                ic = X.scratch.setdefault("g_red_i", 0)
                if ic < IFCH:
                    X.wait(e, f"g_natld{ic}", "qn")
                    X.inc(e.reduce_max(c["scg"][:, ic:ic + 1], c["qnat"][:, :],
                                       axis=AX, apply_absolute_value=True),
                          "vS")
                    X.lbl(f"g_red{ic}")
                    X.scratch["g_red_i"] = ic + 1
    X.lbl("g_scg_done")
    X.wait(e, "g_srow_in", "qsr")
    e.tensor_scalar_mul(c["qrow"][0:1, 0:IF], c["qrow"][0:1, 0:IF], 1.0 / 127.0)
    X.inc(e.reciprocal(c["qrow"][0:1, 0:IF], c["qrow"][0:1, 0:IF]), "vS")
    X.lbl("g_recip_row")


def ph1_scalar(X, e):
    c, dr, ps = X.c, X.dr, X.ps
    tiles = _ph1_tiles()
    for oc in range(16):
        X.wait(e, f"conv{oc}", "vS")
        X.inc(e.activation(c["xs"][:, oc * T:(oc + 1) * T], c["cacc"][:, :],
                           AF.Silu, bias=vsl(X, "cb")[:, oc:oc + 1]), "aS")
        X.lbl(f"xs{oc}")
    for ti, (oc, gi, g0, w) in enumerate(tiles):
        if oc < 16:
            continue
        ic = oc - 16
        bank = ti % 4
        X.wait(e, f"mmt{ti}", "tS")
        if gi == 1 and ic >= 2:
            X.wait(e, f"sgsp{ic - 2}", f"qsg{ic % 2}")
        X.inc(e.activation(c["sgrow"][ic % 2][:, g0 - PAD:g0 - PAD + w],
                           ps[bank][:, 0:w], AF.Silu), "aS")
        X.lbl(f"ev{ti}")
        if gi == len(GRPS) - 1:
            X.fence(e, "aS")
            dst = bass.AP(dr["sg_sp"], ic * 128 * T, [[T, 128], [1, T]])
            X.idma(e.dma_start(out=dst, in_=c["sgrow"][ic % 2][:, :]),
                   f"qsg{ic % 2}")
            X.lbl(f"sgsp{ic}")
    X.lbl("ph1_done")


def dbg_ph1_gp(X, e):
    c, dr = X.c, X.dr
    X.wait(e, "xs15", "aS")
    X.wait(e, "sgsp15", f"qsg{15 % 2}")
    X.idma(e.dma_start(out=dr["dbg_xs"][:, :], in_=c["xs"][:, :]), "qdbg")
    if not X.pass1:
        e.wait_ge(X.sems["qdbg"].h, X.sems["qdbg"].n)


def ph2_sync(X, e):
    c, dr = X.c, X.dr
    for hc in range(DCH):
        if hc >= 1:
            X.wait(e, f"g_qmul{hc - 1}", "vS")
            dst = bass.AP(dr["wq_g"], (hc - 1) * 128,
                          [[DCH * 128, 128], [128 * DCH * 128, IFCH],
                           [1, 128]])
            X.idma(e.dma_start(out=dst, in_=c["qout"][0][:, 0:IF]), "qqs2")
            X.lbl(f"g_qsp{hc - 1}")
        gsrc = bass.AP(dr["wgT"], hc * 128 * IF, [[IF, 128], [1, IF]])
        X.idma(e.dma_start(out=c["qtT"][:, :], in_=gsrc), "qt")
        X.lbl(f"g_qtld{hc}")
    X.wait(e, f"g_qmul{DCH - 1}", "vS")
    dst = bass.AP(dr["wq_g"], (DCH - 1) * 128,
                  [[DCH * 128, 128], [128 * DCH * 128, IFCH], [1, 128]])
    X.idma(e.dma_start(out=dst, in_=c["qout"][0][:, 0:IF]), "qqs2")
    X.lbl(f"g_qsp{DCH - 1}")
    for ic in range(IFCH):
        if ic >= 1:
            X.wait(e, f"u_red{ic - 1}", "vS")
        src = bass.AP(dr["wuN"], ic * 128 * D, [[D, 128], [1, D]])
        X.idma(e.dma_start(out=c["qnat"][:, :], in_=src), "qn")
        X.lbl(f"u_natld{ic}")
    X.wait(e, "u_scg_done", "vS")
    X.wait(e, "rbc_g", "qrbc")
    for ic in range(IFCH):
        dst = bass.AP(dr["srow"], ic * 128, [[1, 128], [1, 1]])
        X.idma(e.dma_start(out=dst, in_=c["scu"][:, ic:ic + 1]), "qsr")
    if not X.pass1:
        e.wait_ge(X.sems["qsr"].h, X.sems["qsr"].n)
    X.idma(e.dma_start(out=c["qrow"][0:1, 0:IF],
                       in_=bass.AP(dr["srow"], 0, [[1, 1], [1, IF]])), "qsr")
    X.lbl("u_srow_in")
    X.wait(e, "g_qmul7", "vS")
    X.wait(e, "u_recip_row", "vS")
    X.idma(e.dma_start(out=dr["srow2"].ap(), in_=c["qrow"][0:1, 0:IF]), "qrbc")
    if not X.pass1:
        e.wait_ge(X.sems["qrbc"].h, X.sems["qrbc"].n)
    X.idma(e.dma_start(out=c["rbc"][:, 0:IF],
                       in_=bass.AP(dr["srow2"], 0, [[0, 128], [1, IF]])),
           "qrbc")
    X.lbl("rbc_u")
    for hc in range(DCH):
        if hc == 0:
            X.wait(e, "wmm31", "tS")
        if hc >= 1:
            X.wait(e, f"d_red{hc - 1}", "vS")
        src = bass.AP(dr["wdnN"], hc * 128 * IF, [[IF, 128], [1, IF]])
        X.idma(e.dma_start(out=c["dnat"][:, :], in_=src), "qt")
        X.lbl(f"d_natld{hc}")
    X.wait(e, "d_scd_done", "vS")
    for hc in range(DCH):
        dst = bass.AP(dr["srow"], hc * 128, [[1, 128], [1, 1]])
        X.idma(e.dma_start(out=dst, in_=c["scd"][:, hc:hc + 1]), "qsr")
    if not X.pass1:
        e.wait_ge(X.sems["qsr"].h, X.sems["qsr"].n)
    X.idma(e.dma_start(out=c["qrow"][0:1, 0:D],
                       in_=bass.AP(dr["srow"], 0, [[1, 1], [1, D]])), "qsr")
    X.lbl("d_srow_in")
    X.wait(e, "d_recip_row", "vS")
    X.idma(e.dma_start(out=bass.AP(dr["srow2"], 0, [[1, 1], [1, D]]),
                       in_=c["qrow"][0:1, 0:D]), "qrbc")
    if not X.pass1:
        e.wait_ge(X.sems["qrbc"].h, X.sems["qrbc"].n)
    X.idma(e.dma_start(out=c["rbcd"][:, 0:D],
                       in_=bass.AP(dr["srow2"], 0, [[0, 128], [1, D]])),
           "qrbc")
    X.lbl("rbc_d")
    for hc in range(DCH):
        if hc == 0:
            X.wait(e, f"g_qmul{DCH - 1}", "vS")
        if hc >= 1:
            X.wait(e, f"u_qmul{hc - 1}", "vS")
            dst = bass.AP(dr["wq_u"], (hc - 1) * 128,
                          [[DCH * 128, 128], [128 * DCH * 128, IFCH],
                           [1, 128]])
            X.idma(e.dma_start(out=dst, in_=c["qout"][0][:, 0:IF]), "qqs2")
            X.lbl(f"u_qsp{hc - 1}")
        src = bass.AP(dr["wuT"], hc * 128 * IF, [[IF, 128], [1, IF]])
        X.idma(e.dma_start(out=c["qtT"][:, :], in_=src), "qt")
        X.lbl(f"u_qtld{hc}")
    X.wait(e, f"u_qmul{DCH - 1}", "vS")
    dst = bass.AP(dr["wq_u"], (DCH - 1) * 128,
                  [[DCH * 128, 128], [128 * DCH * 128, IFCH], [1, 128]])
    X.idma(e.dma_start(out=dst, in_=c["qout"][0][:, 0:IF]), "qqs2")
    X.lbl(f"u_qsp{DCH - 1}")
    for ic in range(IFCH):
        buf = "qnat" if ic % 2 == 0 else "qtT"
        if ic == 1:
            X.wait(e, "u_qmul7", "vS")
        if ic >= 1:
            X.wait(e, f"d_qmul{ic - 1}", "vS")
            dst = bass.AP(dr["wq_d"], (ic - 1) * 128,
                          [[IFCH * 128, 128], [128 * IFCH * 128, DCH],
                           [1, 128]])
            X.idma(e.dma_start(out=dst, in_=c["qout"][0][:, 0:D]), "qqs2")
            X.lbl(f"d_qsp{ic - 1}")
        src = bass.AP(dr["wdnT"], ic * 128 * D, [[D, 128], [1, D]])
        X.idma(e.dma_start(out=c[buf][:, 0:D], in_=src), "qt2")
        X.lbl(f"d_qtld{ic}")
    X.wait(e, f"d_qmul{IFCH - 1}", "vS")
    dst = bass.AP(dr["wq_d"], (IFCH - 1) * 128,
                  [[IFCH * 128, 128], [128 * IFCH * 128, DCH], [1, 128]])
    X.idma(e.dma_start(out=dst, in_=c["qout"][0][:, 0:D]), "qqs2")
    X.lbl(f"d_qsp{IFCH - 1}")


def ph2_gpsimd(X, e):
    c, dr = X.c, X.dr

    def wdcol_load(oc):
        if oc >= 2:
            X.wait(e, f"wdmm{oc - 2}", "tS")
        src = bass.AP(dr["wd"], oc * 128,
                      [[DI, 128], [128 * DI, DICH], [1, 128]])
        X.idma(e.dma_start(out=c["wdcol"][oc % 2][:, :], in_=src),
               f"qd{oc % 2}")
        X.lbl(f"wdld{oc}")

    for oc in range(DICH):
        wdcol_load(oc)
    X.wait(e, "DcFc_done", "vS")
    src = bass.AP(c["DcFc"], 0, [[2, 16], [1, 2]])
    X.idma(e.dma_start(out=bass.AP(dr["ag_car_i"], 0, [[1, 1], [1, 32]]),
                       in_=src), "qagc")
    if not X.pass1:
        e.wait_ge(X.sems["qagc"].h, X.sems["qagc"].n)
    X.inc(e.collective_compute(
        "AllGather", ALU.bypass, replica_groups=[list(range(NC))],
        ins=[dr["ag_car_i"].ap()], outs=[dr["ag_car_o"].ap()]), "cS")
    if not X.pass1:
        e.wait_ge(X.sems["cS"].h, X.sems["cS"].n)
    for j in range(NC):
        s1 = bass.AP(dr["ag_car_o"], j * 32, [[2, 16], [1, 1]])
        X.idma(e.dma_start(out=c["gath"][:, j:j + 1], in_=s1,
                           allow_slow_non_contiguous=True), "qgth")
        s2 = bass.AP(dr["ag_car_o"], j * 32 + 1, [[2, 16], [1, 1]])
        X.idma(e.dma_start(out=c["gath"][:, 8 + j:9 + j], in_=s2,
                           allow_slow_non_contiguous=True), "qgth")
    X.lbl("gath_in")
    X.wait(e, f"d_qsp{IFCH - 1}", "qqs2")
    src = bass.AP(dr["w_out"], 0, [[D, 128], [128 * D, DICH], [1, D]])
    dst = bass.AP(c["wo"], 0, [[DICH * D, 128], [D, DICH], [1, D]])
    X.idma(e.dma_start(out=dst, in_=src), "qwo")
    X.lbl("wo_ld")


def ph2_tensor(X, e):
    c, ps = X.c, X.ps
    X.wait(e, "ph1_done", "aS")
    ti = 0
    for oc in range(DICH):
        X.wait(e, f"wdld{oc}", f"qd{oc % 2}")
        for g in range(4):
            bank = ti % 3
            if ti >= 3:
                X.wait(e, f"wdev{ti - 3}", "aS")
            for k in range(DICH):
                inst = e.matmul(
                    ps[bank][:, 0:512],
                    c["wdcol"][oc % 2][:, k * 128:(k + 1) * 128],
                    c["xs"][:, k * T + g * 512:k * T + (g + 1) * 512],
                    start=(k == 0), stop=(k == DICH - 1))
            X.inc(inst, "tS")
            X.lbl(f"wdmt{ti}")
            X.wait(e, f"wdln{ti}", "aS")
            inst = e.matmul(ps[3 + g][:, 0:512], c["onesW"][:, :],
                            c["spb"][ti % 2][:, :],
                            start=(oc == 0), stop=(oc == DICH - 1))
            X.inc(inst, "tS")
            X.lbl(f"dmm{ti}")
            ti += 1
        X.lbl(f"wdmm{oc}")
    for g in range(4):
        if g > 0:
            X.wait(e, f"bev{g - 1}", "vS")
        for k in range(DICH):
            inst = e.matmul(ps[7][0:16, 0:512],
                            c["bT"][:, k * NS:(k + 1) * NS],
                            c["xs"][:, k * T + g * 512:k * T + (g + 1) * 512],
                            start=(k == 0), stop=(k == DICH - 1))
        X.inc(inst, "tS")
        X.lbl(f"bmm{g}")
    for g in range(4):
        X.wait(e, f"dle{g}", "vS")
        inst = e.matmul(ps[3 + g][0:16, 0:512], c["onesr_f"][0:1, 0:16],
                        c["drow"][0:1, g * 512:(g + 1) * 512],
                        start=True, stop=True)
        X.inc(inst, "tS")
        X.lbl(f"damm{g}")
    X.wait(e, "stot", "vS")
    X.wait(e, "bev3", "vS")
    inst = e.matmul(ps[7][0:16, 0:1], c["onesr_f"][0:1, 0:16],
                    c["rs1"][0:1, 0:1], start=True, stop=True)
    X.inc(inst, "tS")
    X.lbl("dcmm")


def ph2_scalar(X, e):
    c, ps = X.c, X.ps
    ti = 0
    for oc in range(DICH):
        for g in range(4):
            bank = ti % 3
            X.wait(e, f"wdmt{ti}", "tS")
            X.inc(e.activation(c["exps"][ti % 2][:, :], ps[bank][:, 0:512],
                               AF.Exp), "aS")
            X.lbl(f"wdev{ti}")
            if ti >= 2:
                X.wait(e, f"dmm{ti - 2}", "tS")
            X.inc(e.activation(c["spb"][ti % 2][:, :], c["exps"][ti % 2][:, :],
                               AF.Ln, bias=1.0), "aS")
            X.lbl(f"wdln{ti}")
            ti += 1
    for g in range(4):
        X.wait(e, f"damm{g}", "tS")
        X.inc(e.activation(c["dA"][:, g * 512:(g + 1) * 512],
                           ps[3 + g][0:16, 0:512], AF.Exp,
                           scale=c["Aneg"][:, 0:1]), "aS")
        X.lbl(f"dAa{g}")
    X.wait(e, "dcmm", "tS")
    X.inc(e.activation(c["DcFc"][:, 0:1], ps[7][0:16, 0:1], AF.Exp,
                       scale=c["Aneg"][:, 0:1]), "aS")
    X.lbl("Dc_done")


def ph2_vector(X, e):
    c, ps = X.c, X.ps
    for hc in range(DCH):
        X.wait(e, f"g_qtld{hc}", "qt")
        if hc == 0:
            X.wait(e, "rbc_g", "qrbc")
        e.tensor_mul(c["qtT"][:, :], c["qtT"][:, :], c["rbc"][:, 0:IF])
        if hc >= 1:
            X.wait(e, f"g_qsp{hc - 1}", "qqs2")
        X.inc(e.tensor_scalar(c["qout"][0][:, 0:IF], c["qtT"][:, :],
                              MAGIC, MAGIC, ALU.add, ALU.subtract), "vS")
        X.lbl(f"g_qmul{hc}")
    for ic in range(IFCH):
        X.wait(e, f"u_natld{ic}", "qn")
        X.inc(e.reduce_max(c["scu"][:, ic:ic + 1], c["qnat"][:, :], axis=AX,
                           apply_absolute_value=True), "vS")
        X.lbl(f"u_red{ic}")
    X.lbl("u_scg_done")
    X.wait(e, "u_srow_in", "qsr")
    e.tensor_scalar_mul(c["qrow"][0:1, 0:IF], c["qrow"][0:1, 0:IF], 1.0 / 127.0)
    X.inc(e.reciprocal(c["qrow"][0:1, 0:IF], c["qrow"][0:1, 0:IF]), "vS")
    X.lbl("u_recip_row")
    for hc in range(DCH):
        X.wait(e, f"d_natld{hc}", "qt")
        X.inc(e.reduce_max(c["scd"][:, hc:hc + 1], c["dnat"][:, :], axis=AX,
                           apply_absolute_value=True), "vS")
        X.lbl(f"d_red{hc}")
    X.lbl("d_scd_done")
    X.wait(e, "d_srow_in", "qsr")
    e.tensor_scalar_mul(c["qrow"][0:1, 0:D], c["qrow"][0:1, 0:D], 1.0 / 127.0)
    X.inc(e.reciprocal(c["qrow"][0:1, 0:D], c["qrow"][0:1, 0:D]), "vS")
    X.lbl("d_recip_row")
    for hc in range(DCH):
        X.wait(e, f"u_qtld{hc}", "qt")
        if hc == 0:
            X.wait(e, "rbc_u", "qrbc")
        e.tensor_mul(c["qtT"][:, :], c["qtT"][:, :], c["rbc"][:, 0:IF])
        if hc >= 1:
            X.wait(e, f"u_qsp{hc - 1}", "qqs2")
        else:
            X.wait(e, f"g_qsp{DCH - 1}", "qqs")
        X.inc(e.tensor_scalar(c["qout"][0][:, 0:IF], c["qtT"][:, :],
                              MAGIC, MAGIC, ALU.add, ALU.subtract), "vS")
        X.lbl(f"u_qmul{hc}")
    for ic in range(IFCH):
        buf = "qnat" if ic % 2 == 0 else "qtT"
        X.wait(e, f"d_qtld{ic}", "qt2")
        if ic == 0:
            X.wait(e, "rbc_d", "qrbc")
        e.tensor_mul(c[buf][:, 0:D], c[buf][:, 0:D], c["rbcd"][:, 0:D])
        X.wait(e, f"d_qsp{ic - 1}" if ic >= 1 else f"u_qsp{DCH - 1}", "qqs2")
        X.inc(e.tensor_scalar(c["qout"][0][:, 0:D], c[buf][:, 0:D],
                              MAGIC, MAGIC, ALU.add, ALU.subtract), "vS")
        X.lbl(f"d_qmul{ic}")
    for g in range(4):
        X.wait(e, f"bmm{g}", "tS")
        X.inc(e.tensor_copy(c["Bx"][:, g * 512:(g + 1) * 512],
                            ps[7][0:16, 0:512]), "vS")
        X.lbl(f"bev{g}")
    for g in range(4):
        X.wait(e, f"dmm{60 + g}", "tS")
        X.inc(e.tensor_scalar_mul(c["drow"][0:1, g * 512:(g + 1) * 512],
                                  ps[3 + g][0:1, 0:512], 1.0 / DI), "vS")
        X.lbl(f"dle{g}")
    for g in range(4):
        X.wait(e, f"dAa{g}", "aS")
        X.inc(e.tensor_mul(c["Bx"][:, g * 512:(g + 1) * 512],
                           c["Bx"][:, g * 512:(g + 1) * 512],
                           ps[3 + g][0:16, 0:512]), "vS")
        X.lbl(f"bxm{g}")
    X.sw(e, e.tensor_tensor_scan(c["hs"][:, :], c["dA"][:, :], c["Bx"][:, :],
                                 0.0, ALU.mult, ALU.add), "vS")
    X.sw(e, e.tensor_copy(c["DcFc"][:, 1:2], c["hs"][:, T - 1:T]), "vS")
    X.inc(e.reduce_sum(c["rs1"][0:1, 0:1], c["drow"][0:1, 0:T], axis=AX),
          "vS")
    X.lbl("stot")
    X.wait(e, "Dc_done", "aS")
    X.inc(e.engine_nop(), "vS")
    X.lbl("DcFc_done")
    X.wait(e, "gath_in", "qgth")
    X.sw(e, e.tensor_mul(c["gath"][:, 0:8], c["gath"][:, 0:8],
                         c["msk"][:, 0:8]), "vS")
    X.sw(e, e.tensor_add(c["gath"][:, 0:8], c["gath"][:, 0:8],
                         c["msk"][:, 8:16]), "vS")
    X.sw(e, e.tensor_mul(c["gath"][:, 8:16], c["gath"][:, 8:16],
                         c["msk"][:, 0:8]), "vS")
    X.sw(e, e.memset(c["carry"][:, :], 0.0), "vS")
    for j in range(NC):
        X.sw(e, e.tensor_mul(c["carry"][:, 0:1], c["carry"][:, 0:1],
                             c["gath"][:, j:j + 1]), "vS")
        X.sw(e, e.tensor_add(c["carry"][:, 0:1], c["carry"][:, 0:1],
                             c["gath"][:, 8 + j:9 + j]), "vS")
    X.inc(e.tensor_tensor_scan(c["hs"][:, :], c["dA"][:, :], c["Bx"][:, :],
                               c["carry"][:, 0:1], ALU.mult, ALU.add), "vS")
    X.lbl("scan2")


def dbg_ph2_gp(X, e):
    c, dr = X.c, X.dr
    X.wait(e, "scan2", "vS")
    X.wait(e, "wo_ld", "qwo")
    d = dr["dbg_sc"]
    X.idma(e.dma_start(out=bass.AP(d, 0, [[1, 1], [1, T]]),
                       in_=c["drow"][0:1, :], allow_slow_non_contiguous=True), "qdbg")
    X.idma(e.dma_start(out=bass.AP(d, (3 * T + 8) * 0 + T, [[3 * T + 8, 16], [1, T]]),
                       in_=c["dA"][:, :], allow_slow_non_contiguous=True), "qdbg")
    X.idma(e.dma_start(out=bass.AP(d, 2 * T, [[3 * T + 8, 16], [1, T]]),
                       in_=c["hs"][:, :], allow_slow_non_contiguous=True), "qdbg")
    X.idma(e.dma_start(out=bass.AP(d, 3 * T, [[3 * T + 8, 16], [1, 1]]),
                       in_=c["carry"][:, 0:1], allow_slow_non_contiguous=True), "qdbg")
    X.idma(e.dma_start(out=bass.AP(d, 3 * T + 1, [[3 * T + 8, 16], [1, 2]]),
                       in_=c["DcFc"][:, :], allow_slow_non_contiguous=True), "qdbg")
    if not X.pass1:
        e.wait_ge(X.sems["qdbg"].h, X.sems["qdbg"].n)


def ph3_sync(X, e):
    c, dr = X.c, X.dr
    X.wait(e, "sgsp14", "qsg0")
    X.wait(e, "sgsp15", "qsg1")
    for g in range(4):
        if g >= 2:
            X.wait(e, f"yg{g - 2}", "vS")
        src = bass.AP(dr["sg_sp"], g * 512, [[T, 128], [128 * T, DICH], [1, 512]])
        X.idma(e.dma_start(out=c["sgg"][g % 2][:, :], in_=src), f"qsg2{g % 2}")
        X.lbl(f"sgld{g}")
    for g in range(4):
        if g >= 1:
            X.wait(e, f"x2d{g - 1}", "vS")
        src = bass.AP(dr["xT"], PAD + g * 512,
                      [[TP, 128], [128 * TP, DCH], [1, 512]])
        X.idma(e.dma_start(out=c["xrel"][:, :], in_=src), "qxr")
        X.lbl(f"xrl{g}")
    X.wait(e, "x2sp3", "qx2s")
    X.wait(e, "macc_done", "vS")
    for oc in range(DCH):
        X.wait(e, f"h3sp3_{oc}", f"qh3{oc % 2}")
        src = bass.AP(dr["h3_sp"], oc * 128 * T, [[T, 128], [1, T]])
        X.idma(e.dma_start(out=c["h3"][:, oc * T:(oc + 1) * T], in_=src),
               "qh3r")
    X.lbl("h3rl")


def ph3_tensor(X, e):
    c, ps = X.c, X.ps
    X.wait(e, "scan2", "vS")
    ti = 0
    for g in range(4):
        for ic in range(DICH):
            bank = ti % 3
            if ti >= 3:
                X.wait(e, f"yev{ti - 3}", "vS")
            inst = e.matmul(ps[bank][:, 0:512],
                            c["cTs"][:, ic * 128:(ic + 1) * 128],
                            c["hs"][:, g * 512:(g + 1) * 512],
                            start=True, stop=True)
            X.inc(inst, "tS")
            X.lbl(f"cmm{ti}")
            ti += 1
    for g in range(4):
        g0 = PAD + g * 512
        if g > 0:
            X.wait(e, f"h3u{g - 1}_7", "vS")
        e.matmul(ps[6][:, 0:512], c["onesr_f"][0:1, :],
                 c["r1row"][0:1, g0:g0 + 512], start=True, stop=True)
        inst = e.matmul(ps[7][:, 0:512], c["onesr_f"][0:1, :],
                        c["mr1row"][0:1, g0:g0 + 512], start=True, stop=True)
        X.inc(inst, "tS")
        X.lbl(f"b1mm{g}")
        X.wait(e, f"yg{g}", "vS")
        if g == 0:
            X.wait(e, "wo_ld", "qwo")
        for oc in range(DCH):
            wi = g * DCH + oc
            bank = 3 + wi % 3
            if wi >= 3:
                X.wait(e, f"woev{wi - 3}", "vS")
            for k in range(DICH):
                inst = e.matmul(ps[bank][:, 0:512],
                                c["wo"][:, k * D + oc * 128:k * D + (oc + 1) * 128],
                                c["xs"][:, k * T + g * 512:k * T + (g + 1) * 512],
                                start=(k == 0), stop=(k == DICH - 1))
            X.inc(inst, "tS")
            X.lbl(f"womm{wi}")
        X.wait(e, f"b1ev{g}", "aS")
        for k in range(DCH):
            X.wait(e, f"x2b{g}_{k}", "aS")
            e.matmul(ps[6][:, 0:512], c["onesW"][:, :],
                     c["x2b"][k % 2][:, :], start=(k == 0), stop=(k == DCH - 1))
            inst = e.matmul(ps[7][:, 0:512], c["onesW"][:, :],
                            c["sqb"][k % 2][:, :], start=(k == 0),
                            stop=(k == DCH - 1))
            X.inc(inst, "tS")
            X.lbl(f"s3m{g}_{k}")
        X.lbl(f"s3mm{g}")
        X.wait(e, f"r3row{g}", "vS")
        e.matmul(ps[6][:, 0:512], c["onesr_f"][0:1, :],
                 c["s3r"][1][0:1, 0:512], start=True, stop=True)
        inst = e.matmul(ps[7][:, 0:512], c["onesr_f"][0:1, :],
                        c["s3r"][0][0:1, 0:512], start=True, stop=True)
        X.inc(inst, "tS")
        X.lbl(f"b2mm{g}")
    X.wait(e, "sxr_row", "vS")
    inst = e.matmul(ps[0][:, 0:1], c["onesr_f"][0:1, :],
                    c["s3r"][0][0:1, 0:1], start=True, stop=True)
    X.inc(inst, "tS")
    X.lbl("sxrmm")


def ph3_gpsimd(X, e):
    c, dr, ps = X.c, X.dr, X.ps
    X.wait(e, "macc_done", "vS")
    X.inc(e.tensor_reduce(c["s3r"][0][0:1, 0:1], c["macc"][:, 0:1],
                          axis=mybir.AxisListType.C, op=ALU.max), "gS")
    X.lbl("gmax_loc")
    X.fence(e, "gS")
    X.idma(e.dma_start(out=bass.AP(dr["ag_m1_i"], 0, [[1, 1], [1, 1]]),
                       in_=c["s3r"][0][0:1, 0:1]), "qagm1")
    if not X.pass1:
        e.wait_ge(X.sems["qagm1"].h, X.sems["qagm1"].n)
    X.inc(e.collective_compute(
        "AllGather", ALU.bypass, replica_groups=[list(range(NC))],
        ins=[bass.AP(dr["ag_m1_i"], 0, [[1, 1], [1, 1]])],
        outs=[bass.AP(dr["ag_m1_o"], 0, [[1, 1], [1, NC]])]), "cS")
    if not X.pass1:
        e.wait_ge(X.sems["cS"].h, X.sems["cS"].n)
    X.idma(e.dma_start(out=c["s3r"][1][0:1, 0:NC],
                       in_=bass.AP(dr["ag_m1_o"], 0, [[1, 1], [1, NC]])),
           "qagm1")
    X.lbl("m1_in")


def ph3_vector(X, e):
    c, ps = X.c, X.ps
    ti = 0
    for g in range(4):
        X.wait(e, f"sgld{g}", f"qsg2{g % 2}")
        for ic in range(DICH):
            bank = ti % 3
            X.inc(e.scalar_tensor_tensor(
                c["gscr"][ti % 2][:, :],
                c["xs"][:, ic * T + g * 512:ic * T + (g + 1) * 512],
                vsl(X, "dv")[:, ic:ic + 1],
                c["sgg"][g % 2][:, ic * 512:(ic + 1) * 512],
                ALU.mult, ALU.mult), "v2S")
            X.lbl(f"sxd{ti}")
            X.wait(e, f"cmm{ti}", "tS")
            e.tensor_mul(c["yscr"][:, :], ps[bank][:, 0:512],
                         c["sgg"][g % 2][:, ic * 512:(ic + 1) * 512])
            X.inc(e.tensor_add(
                c["xs"][:, ic * T + g * 512:ic * T + (g + 1) * 512],
                c["yscr"][:, :], c["gscr"][ti % 2][:, :]), "vS")
            X.lbl(f"yev{ti}")
            ti += 1
        X.lbl(f"yg{g}")
    wi = 0
    for g in range(4):
        X.wait(e, f"xrl{g}", "qxr")
        X.wait(e, f"b1evs{g}", "aS")
        for oc in range(DCH):
            bank = 3 + wi % 3
            e.tensor_mul(c["hpre"][wi % 2][:, :],
                         c["xrel"][:, oc * 512:(oc + 1) * 512],
                         c["br1s"][:, :])
            X.inc(e.tensor_add(c["hpre"][wi % 2][:, :], c["hpre"][wi % 2][:, :],
                               c["bmr1s"][:, :]), "v2S")
            X.lbl(f"hpre{wi}")
            X.wait(e, f"womm{wi}", "tS")
            X.wait(e, f"haff{wi}", "aS")
            e.tensor_add(c["hpre"][wi % 2][:, :], c["hpre"][wi % 2][:, :],
                         ps[bank][:, 0:512])
            X.inc(e.scalar_tensor_tensor(
                c["x2g"][:, oc * 512:(oc + 1) * 512],
                c["hpre"][wi % 2][:, :], mcol(X, "gap", oc),
                c["xrel"][:, oc * 512:(oc + 1) * 512],
                ALU.mult, ALU.add), "vS")
            X.lbl(f"woev{wi}")
            wi += 1
        X.lbl(f"x2d{g}")
        X.wait(e, f"s3mm{g}", "tS")
        e.tensor_scalar_mul(c["s3r"][0][0:1, 0:512], ps[6][0:1, 0:512], 1.0 / D)
        e.tensor_scalar_mul(c["s3r"][1][0:1, 0:512], ps[7][0:1, 0:512], 1.0 / D)
        e.tensor_mul(c["bmr3s"][0:1, 0:512], c["s3r"][0][0:1, 0:512],
                     c["s3r"][0][0:1, 0:512])
        X.inc(e.tensor_sub(c["s3r"][1][0:1, 0:512], c["s3r"][1][0:1, 0:512],
                           c["bmr3s"][0:1, 0:512]), "vS")
        X.lbl(f"var3{g}")
        X.wait(e, f"r3a{g}", "aS")
        X.inc(e.scalar_tensor_tensor(c["s3r"][0][0:1, 0:512],
                                     c["s3r"][0][0:1, 0:512], -1.0,
                                     c["s3r"][1][0:1, 0:512],
                                     ALU.mult, ALU.mult), "vS")
        X.lbl(f"r3row{g}")
        def h3u(oc):
            X.wait(e, f"b2mm{g}", "tS")
            X.wait(e, f"s3m{g}_{oc}", "tS")
            if g > 0 or oc >= 2:
                X.wait(e, f"h3a{g if oc >= 2 else g - 1}_{(oc - 2) % DCH}",
                       "aS")
            e.tensor_mul(c["sqb"][oc % 2][:, :],
                         c["x2g"][:, oc * 512:(oc + 1) * 512], ps[6][:, 0:512])
            X.inc(e.tensor_tensor(c["sqb"][oc % 2][:, :], c["sqb"][oc % 2][:, :],
                                  ps[7][:, 0:512], ALU.add), "vS")
            X.lbl(f"h3u{g}_{oc}")

        def h3r(oc):
            X.wait(e, f"h3a{g}_{oc}", "aS")
            X.inc(e.reduce_max(c["maccw"][:, g * DCH + oc:g * DCH + oc + 1],
                               c["x2b"][oc % 2][:, :],
                               axis=AX, apply_absolute_value=True), "vS")
            X.lbl(f"h3r{g}_{oc}")

        h3u(0)
        h3u(1)
        for oc in range(2, DCH):
            h3r(oc - 2)
            h3u(oc)
        h3r(DCH - 2)
        h3r(DCH - 1)
        if g < 3:
            X.wait(e, f"x2sp{g}", "qx2s")
    X.sw(e, e.engine_nop(), "vS")
    X.inc(e.reduce_max(c["macc"][:, 0:1], c["maccw"][:, 0:32], axis=AX), "vS")
    if not X.pass1:
        e.wait_ge(X.sems["vS"].h, X.sems["vS"].n)
    X.lbl("macc_done")
    X.wait(e, "m1_in", "qagm1")
    X.sw(e, e.reduce_max(c["s3r"][0][0:1, 0:1], c["s3r"][1][0:1, 0:NC],
                         axis=AX), "vS")
    X.lbl("gmax1")
    X.wait(e, "sig1", "aS")
    X.sw(e, e.tensor_scalar_add(c["tiny"][0:1, 1:2], c["tiny"][0:1, 1:2], 1.0),
         "vS")
    X.sw(e, e.reciprocal(c["tiny"][0:1, 1:2], c["tiny"][0:1, 1:2]), "vS")
    X.sw(e, e.tensor_mul(c["s3r"][0][0:1, 0:1], c["s3r"][0][0:1, 0:1],
                         c["tiny"][0:1, 1:2]), "vS")
    X.sw(e, e.tensor_scalar_mul(c["s3r"][0][0:1, 0:1], c["s3r"][0][0:1, 0:1],
                                1.0 / 127.0), "vS")
    X.inc(e.reciprocal(c["s3r"][0][0:1, 0:1], c["s3r"][0][0:1, 0:1]), "vS")
    X.lbl("sxr_row")
    X.wait(e, "sxrmm", "tS")
    X.sw(e, e.tensor_copy(c["sxr"][:, :], ps[0][:, 0:1]), "vS")
    X.sw(e, e.reciprocal(c["s2r"][:, 0:1], c["sxr"][:, 0:1]), "vS")
    X.sw(e, e.tensor_scalar_mul(c["cdg"][:, :], c["scg"][:, :],
                                c["s2r"][:, 0:1]), "vS")
    X.sw(e, e.tensor_scalar_mul(c["cdg"][:, :], c["cdg"][:, :], 1.0 / 127.0),
         "vS")
    X.sw(e, e.tensor_scalar_mul(c["cdu"][:, :], c["scu"][:, :],
                                c["s2r"][:, 0:1]), "vS")
    X.inc(e.tensor_scalar_mul(c["cdu"][:, :], c["cdu"][:, :], 1.0 / 127.0),
          "vS")
    X.lbl("cdg_done")


def ph3_scalar(X, e):
    c, dr, ps = X.c, X.dr, X.ps
    for g in range(4):
        X.wait(e, f"b1mm{g}", "tS")
        e.activation(c["br1s"][:, :], ps[6][:, 0:512], AF.Identity)
        X.inc(e.activation(c["bmr1s"][:, :], ps[7][:, 0:512], AF.Identity),
              "aS")
        X.lbl(f"b1evs{g}")
        X.lbl(f"b1ev{g}")
        if X.dbg == "pfull" and g == 0:
            X.fence(e, "aS")
            X.idma(e.dma_start(out=X.dr["dbg_b1"][:, 0:512],
                               in_=c["br1s"][:, :]), "qdbgb")
            X.idma(e.dma_start(out=X.dr["dbg_b1"][:, 512:1024],
                               in_=c["bmr1s"][:, :]), "qdbgb")
        for oc in range(DCH):
            wi = g * DCH + oc
            X.wait(e, f"hpre{wi}", "v2S")
            X.inc(e.activation(c["hpre"][wi % 2][:, :], c["hpre"][wi % 2][:, :],
                               AF.Identity, bias=mcol(X, "sa", oc),
                               scale=mcol(X, "ca1", oc)), "aS")
            X.lbl(f"haff{wi}")
        for oc in range(DCH):
            wi = g * DCH + oc
            X.wait(e, f"woev{wi}", "vS")
            if oc >= 2:
                X.wait(e, f"s3m{g}_{oc - 2}", "tS")
            elif g > 0:
                X.wait(e, f"h3r{g - 1}_{DCH - 2 + oc}", "vS")
                X.wait(e, f"h3sp{g - 1}_{DCH - 2 + oc}", f"qh3{oc % 2}")
            e.activation(c["x2b"][oc % 2][:, :],
                         c["x2g"][:, oc * 512:(oc + 1) * 512], AF.Identity)
            X.inc(e.activation(c["sqb"][oc % 2][:, :], c["x2b"][oc % 2][:, :],
                               AF.Square), "aS")
            X.lbl(f"x2b{g}_{oc}")
        X.wait(e, f"var3{g}", "vS")
        e.activation(c["s3r"][1][0:1, 0:512], c["s3r"][1][0:1, 0:512],
                     AF.Ln, bias=EPS)
        X.inc(e.activation(c["s3r"][1][0:1, 0:512], c["s3r"][1][0:1, 0:512],
                           AF.Exp, scale=-0.5), "aS")
        X.lbl(f"r3a{g}")
        for oc in range(DCH):
            X.wait(e, f"h3u{g}_{oc}", "vS")
            if g > 0 or oc >= 2:
                po, pg = (oc - 2, g) if oc >= 2 else (DCH - 2 + oc, g - 1)
                X.wait(e, f"h3sp{pg}_{po}", f"qh3{oc % 2}")
                X.wait(e, f"h3r{pg}_{po}", "vS")
            X.inc(e.activation(c["x2b"][oc % 2][:, :], c["sqb"][oc % 2][:, :],
                               AF.Identity, bias=mcol(X, "sb", oc),
                               scale=mcol(X, "cb1", oc)), "aS")
            X.lbl(f"h3a{g}_{oc}")
            X.fence(e, "aS")
            dst = bass.AP(dr["h3_sp"], (oc * 128) * T + g * 512,
                          [[T, 128], [1, 512]])
            X.idma(e.dma_start(out=dst, in_=c["x2b"][oc % 2][:, :]),
                   f"qh3{oc % 2}")
            X.lbl(f"h3sp{g}_{oc}")
        for oc in range(DCH):
            dst = bass.AP(dr["x2_sp"], (oc * 128) * T + g * 512,
                          [[T, 128], [1, 512]])
            X.idma(e.dma_start(out=dst, in_=c["x2g"][:, oc * 512:(oc + 1) * 512]),
                   "qx2s")
        X.lbl(f"x2sp{g}")
    X.inc(e.activation(c["tiny"][0:1, 1:2], c["clip"][0:1, 0:1], AF.Exp,
                       scale=-1.0), "aS")
    X.lbl("sig1")


def dbg_ph3_gp(X, e):
    c, dr = X.c, X.dr
    X.wait(e, "cdg_done", "vS")
    X.idma(e.dma_start(out=dr["dbg_wo"][:, :], in_=c["wo"][:, :]), "qdbg")
    X.idma(e.dma_start(out=dr["dbg_y"][:, :], in_=c["xs"][:, :]), "qdbg")
    X.idma(e.dma_start(out=bass.AP(dr["dbg_wo"], 0, [[1, 1], [1, 1]]),
                       in_=c["cTs"][0:1, 0:1]) if False else e.engine_nop(),
           "qdbg") if False else None
    X.idma(e.dma_start(out=dr["dbg_ct"][:, :], in_=c["cTs"][:, :]), "qdbg")
    d = dr["dbg_p3"]
    X.idma(e.dma_start(out=bass.AP(d, 0, [[16, 128], [1, 1]]),
                       in_=c["sxr"][:, :], allow_slow_non_contiguous=True), "qdbg")
    X.idma(e.dma_start(out=bass.AP(d, 1, [[16, 128], [1, 8]]),
                       in_=c["cdg"][:, 0:8], allow_slow_non_contiguous=True), "qdbg")
    if not X.pass1:
        e.wait_ge(X.sems["qdbg"].h, X.sems["qdbg"].n)


def msl(X, k):
    if k < 16:
        return X.c["m_a"], k * T
    return X.c["m_b"], (k - 16) * T


def ph5_sync(X, e):
    c, dr = X.c, X.dr
    X.wait(e, f"d_qsp{IFCH - 1}", "qqs2")
    for oc in range(IFCH):
        if oc >= 2:
            X.wait(e, f"fmm{oc - 2}", "tS")
        src = bass.AP(dr["wq_g"], oc * DCH * 128 * 128, [[DCH * 128, 128], [1, DCH * 128]])
        X.idma(e.dma_start(out=c["gcol"][oc % 2][:, :], in_=src), f"qfg{oc % 2}")
        src = bass.AP(dr["wq_u"], oc * DCH * 128 * 128, [[DCH * 128, 128], [1, DCH * 128]])
        X.idma(e.dma_start(out=c["ucol"][oc % 2][:, :], in_=src), f"qfu{oc % 2}")
        X.lbl(f"fld{oc}")


def ph5_vector(X, e):
    c, ps = X.c, X.ps
    ti = 0
    for g in range(4):
        for k in range(DCH):
            X.wait(e, f"xra{ti}", "aS")
            X.inc(e.tensor_scalar(
                c["h3"][:, k * T + g * 512:k * T + (g + 1) * 512],
                c["obuf"][ti % 2][:, :], 255.0, 128.0,
                ALU.min, ALU.subtract), "vS")
            X.lbl(f"xr{ti}")
            ti += 1
    X.lbl("xr_all")
    fi = 0
    for oc in range(IFCH):
        mt, mb = msl(X, oc)
        for g in range(4):
            bank = 2 * (fi % 2)
            X.wait(e, f"sgf{fi}", "aS")
            X.wait(e, f"umm{fi}", "tS")
            X.inc(e.scalar_tensor_tensor(
                mt[:, mb + g * 512:mb + (g + 1) * 512],
                ps[bank + 1][:, 0:512], c["cdu"][:, oc:oc + 1],
                c["hpre"][fi % 2][:, :], ALU.mult, ALU.mult), "vS")
            X.lbl(f"mev{fi}")
            X.inc(e.reduce_max(c["maccw"][:, 32 + fi:33 + fi],
                               mt[:, mb + g * 512:mb + (g + 1) * 512],
                               axis=AX, apply_absolute_value=True), "vS")
            X.lbl(f"mred{fi}")
            fi += 1
    X.sw(e, e.engine_nop(), "vS")
    X.inc(e.reduce_max(c["macc"][:, 1:2], c["maccw"][:, 32:120], axis=AX),
          "vS")
    if not X.pass1:
        e.wait_ge(X.sems["vS"].h, X.sems["vS"].n)
    X.lbl("m_done")


def ph5_gpsimd(X, e):
    c, dr = X.c, X.dr
    X.wait(e, "m_done", "vS")
    X.inc(e.tensor_reduce(c["s3r"][0][0:1, 1:2], c["macc"][:, 1:2],
                          axis=mybir.AxisListType.C, op=ALU.max), "gS")
    X.fence(e, "gS")
    X.idma(e.dma_start(out=bass.AP(dr["ag_m2_i"], 0, [[1, 1], [1, 1]]),
                       in_=c["s3r"][0][0:1, 1:2]), "qagm2")
    if not X.pass1:
        e.wait_ge(X.sems["qagm2"].h, X.sems["qagm2"].n)
    X.inc(e.collective_compute(
        "AllGather", ALU.bypass, replica_groups=[list(range(NC))],
        ins=[bass.AP(dr["ag_m2_i"], 0, [[1, 1], [1, 1]])],
        outs=[bass.AP(dr["ag_m2_o"], 0, [[1, 1], [1, NC]])]), "cS")
    if not X.pass1:
        e.wait_ge(X.sems["cS"].h, X.sems["cS"].n)
    X.idma(e.dma_start(out=c["s3r"][1][0:1, 0:NC],
                       in_=bass.AP(dr["ag_m2_o"], 0, [[1, 1], [1, NC]])),
           "qagm2")
    X.lbl("m2_in")


def ph5_scalar(X, e):
    c, ps = X.c, X.ps
    X.wait(e, "h3rl", "qh3r")
    X.wait(e, "cdg_done", "vS")
    ti = 0
    for g in range(4):
        for k in range(DCH):
            if ti >= 2:
                X.wait(e, f"xr{ti - 2}", "vS")
            e.activation(c["obuf"][ti % 2][:, :],
                         c["h3"][:, k * T + g * 512:k * T + (g + 1) * 512],
                         AF.Identity, scale=c["sxr"][:, 0:1], bias=MAGIC)
            X.inc(e.activation(c["obuf"][ti % 2][:, :], c["obuf"][ti % 2][:, :],
                               AF.Relu, bias=128.0 - MAGIC), "aS")
            X.lbl(f"xra{ti}")
            ti += 1
    X.inc(e.activation(c["tiny"][0:1, 0:1], c["clip"][0:1, 1:2], AF.Exp,
                       scale=-1.0), "aS")
    X.lbl("sig2")
    fi = 0
    for oc in range(IFCH):
        for g in range(4):
            bank = 2 * (fi % 2)
            X.wait(e, f"gmm{fi}", "tS")
            if fi >= 2:
                X.wait(e, f"mev{fi - 2}", "vS")
            X.inc(e.activation(c["hpre"][fi % 2][:, :], ps[bank][:, 0:512],
                               AF.Silu, scale=c["cdg"][:, oc:oc + 1]), "aS")
            X.lbl(f"sgf{fi}")
            fi += 1


def ph5_tensor(X, e):
    c, ps = X.c, X.ps
    fi = 0
    for oc in range(IFCH):
        X.wait(e, f"fld{oc}", f"qfg{oc % 2}", f"qfu{oc % 2}")
        for g in range(4):
            bank = 2 * (fi % 2)
            if fi >= 2:
                X.wait(e, f"mev{fi - 2}", "vS")
                X.wait(e, f"sgf{fi - 2}", "aS")
            if oc == 0:
                X.wait(e, f"xr{g * DCH + DCH - 1}", "vS")
            for k in range(DCH):
                inst = e.matmul(ps[bank][:, 0:512],
                                c["gcol"][oc % 2][:, k * 128:(k + 1) * 128],
                                c["h3"][:, k * T + g * 512:k * T + (g + 1) * 512],
                                start=(k == 0), stop=(k == DCH - 1))
            X.inc(inst, "tS")
            X.lbl(f"gmm{fi}")
            for k in range(DCH):
                inst = e.matmul(ps[bank + 1][:, 0:512],
                                c["ucol"][oc % 2][:, k * 128:(k + 1) * 128],
                                c["h3"][:, k * T + g * 512:k * T + (g + 1) * 512],
                                start=(k == 0), stop=(k == DCH - 1))
            X.inc(inst, "tS")
            X.lbl(f"umm{fi}")
            fi += 1
        X.lbl(f"fmm{oc}")


def _noop_t(e):
    return e.engine_nop()


def ph6_sync(X, e):
    c, dr = X.c, X.dr
    X.wait(e, "x2sp3", "qx2s")
    X.wait(e, f"fmm{IFCH - 1}", "tS")
    for g in range(4):
        if g >= 2:
            X.wait(e, f"outg{g - 2}", "vS")
        src = bass.AP(dr["x2_sp"], g * 512, [[T, 128], [128 * T, DCH], [1, 512]])
        X.idma(e.dma_start(out=c["x2rl"][:, (g % 2) * DCH * 512:
                                         (g % 2) * DCH * 512 + DCH * 512],
                           in_=src), f"qx2r{g % 2}")
        X.lbl(f"x2rl{g}")
        for oc in range(DCH):
            di = g * DCH + oc
            if di >= 2:
                X.wait(e, f"dmmt{di - 2}", "tS")
            src = bass.AP(dr["wq_d"], oc * IFCH * 128 * 128,
                          [[IFCH * 128, 128], [1, IFCH * 128]])
            X.idma(e.dma_start(out=c["dcol"][di % 2][:, :], in_=src),
                   f"qfd{di % 2}")
            X.lbl(f"dld{di}")


def ph6_vector(X, e):
    c, ps = X.c, X.ps
    X.wait(e, "m2_in", "qagm2")
    X.sw(e, e.reduce_max(c["s3r"][0][0:1, 1:2], c["s3r"][1][0:1, 0:NC],
                         axis=AX), "vS")
    X.wait(e, "sig2", "aS")
    X.sw(e, e.tensor_scalar_add(c["tiny"][0:1, 0:1], c["tiny"][0:1, 0:1], 1.0),
         "vS")
    X.sw(e, e.reciprocal(c["tiny"][0:1, 0:1], c["tiny"][0:1, 0:1]), "vS")
    X.sw(e, e.tensor_mul(c["s3r"][0][0:1, 1:2], c["s3r"][0][0:1, 1:2],
                         c["tiny"][0:1, 0:1]), "vS")
    X.inc(e.tensor_scalar_mul(c["s3r"][0][0:1, 1:2], c["s3r"][0][0:1, 1:2],
                              1.0 / 127.0), "vS")
    if not X.pass1:
        e.wait_ge(X.sems["vS"].h, X.sems["vS"].n)
    X.lbl("s2_row")
    X.wait(e, "s2mm", "tS")
    X.sw(e, e.tensor_copy(c["s2r"][:, 0:1], ps[6][:, 0:1]), "vS")
    X.sw(e, e.tensor_scalar_mul(c["cdd"][:, :], c["scd"][:, :],
                                c["s2r"][:, 0:1]), "vS")
    X.sw(e, e.tensor_scalar_mul(c["cdd"][:, :], c["cdd"][:, :], 1.0 / 127.0),
         "vS")
    X.sw(e, e.tensor_mul(c["cdd"][:, :], c["cdd"][:, :], c["modd"][:, 24:32]),
         "vS")
    X.sw(e, e.reciprocal(c["s2r"][:, 0:1], c["s2r"][:, 0:1]), "vS")
    X.lbl("cdd_done")
    if X.dbg == "pfull":
        X.wait(e, "m5_dump", "qdbgm")
    ti = 0
    di = 0
    for g in range(4):
        for k in range(IFCH):
            mt, mb = msl(X, k)
            X.wait(e, f"h2a{ti}", "aS")
            X.inc(e.tensor_scalar(
                mt[:, mb + g * 512:mb + (g + 1) * 512],
                c["hpre"][ti % 2][:, :], 255.0, 128.0,
                ALU.min, ALU.subtract), "vS")
            X.lbl(f"h2r{ti}")
            ti += 1
        for oc in range(DCH):
            bank = 6 + di % 2
            X.wait(e, f"dmmt{di}", "tS")
            X.wait(e, f"x2rl{g}", f"qx2r{g % 2}")
            X.inc(e.scalar_tensor_tensor(
                c["yscr"][:, :] if False else c["obuf"][di % 2][:, :],
                ps[bank][:, 0:512], c["cdd"][:, oc:oc + 1],
                c["x2rl"][:, (g % 2) * DCH * 512 + oc * 512:
                          (g % 2) * DCH * 512 + (oc + 1) * 512],
                ALU.mult, ALU.add), "vS")
            X.lbl(f"oev{di}")
            di += 1
        X.lbl(f"outg{g}")


def ph6_gpsimd(X, e):
    c, dr = X.c, X.dr
    di = 0
    for g in range(4):
        for oc in range(DCH):
            X.wait(e, f"oev{di}", "vS")
            dst = bass.AP(dr["outT"], (oc * 128) * T + g * 512,
                          [[T, 128], [1, 512]])
            X.idma(e.dma_start(out=dst, in_=c["obuf"][di % 2][:, :]),
                   f"qout{di % 2}")
            X.lbl(f"ost{di}")
            di += 1
    if not X.pass1:
        e.wait_ge(X.sems["qout0"].h, X.sems["qout0"].n)
        e.wait_ge(X.sems["qout1"].h, X.sems["qout1"].n)


def ph6_tensor(X, e):
    c, ps = X.c, X.ps
    X.wait(e, "s2_row", "vS")
    inst = e.matmul(ps[6][:, 0:1], c["onesr_f"][0:1, :],
                    c["s3r"][0][0:1, 1:2], start=True, stop=True)
    X.inc(inst, "tS")
    X.lbl("s2mm")
    X.wait(e, "cdd_done", "vS")
    di = 0
    for g in range(4):
        for oc in range(DCH):
            bank = 6 + di % 2
            X.wait(e, f"dld{di}", f"qfd{di % 2}")
            if di >= 2:
                X.wait(e, f"oev{di - 2}", "vS")
            for k in range(IFCH):
                mt, mb = msl(X, k)
                X.wait(e, f"h2r{g * IFCH + k}", "vS")
                inst = e.matmul(ps[bank][:, 0:512],
                                c["dcol"][di % 2][:, k * 128:(k + 1) * 128],
                                mt[:, mb + g * 512:mb + (g + 1) * 512],
                                start=(k == 0), stop=(k == IFCH - 1))
            X.inc(inst, "tS")
            X.lbl(f"dmmt{di}")
            di += 1


def ph6_scalar(X, e):
    c, ps = X.c, X.ps
    X.wait(e, "m_done", "vS")
    X.wait(e, "cdd_done", "vS")
    ti = 0
    for g in range(4):
        for k in range(IFCH):
            mt, mb = msl(X, k)
            if ti >= 2:
                X.wait(e, f"h2r{ti - 2}", "vS")
            e.activation(c["hpre"][ti % 2][:, :],
                         mt[:, mb + g * 512:mb + (g + 1) * 512],
                         AF.Identity, scale=c["s2r"][:, 0:1], bias=MAGIC)
            X.inc(e.activation(c["hpre"][ti % 2][:, :], c["hpre"][ti % 2][:, :],
                               AF.Relu, bias=128.0 - MAGIC), "aS")
            X.lbl(f"h2a{ti}")
            ti += 1


def dbg_ph5_gp(X, e):
    c, dr = X.c, X.dr
    X.wait(e, "m_done", "vS")
    X.wait(e, "xr_all", "vS")
    X.idma(e.dma_start(out=dr["dbg_m"][:, 0:16 * T], in_=c["m_a"][:, :]), "qdbg")
    X.idma(e.dma_start(out=dr["dbg_m"][:, 16 * T:], in_=c["m_b"][:, :]), "qdbg")
    X.idma(e.dma_start(out=dr["dbg_xr"][:, :], in_=c["h3"][:, :]), "qdbg")
    if not X.pass1:
        e.wait_ge(X.sems["qdbg"].h, X.sems["qdbg"].n)


def dbg_pfull_m(X, e):
    c, dr = X.c, X.dr
    X.wait(e, "m_done", "vS")
    X.wait(e, "xr_all", "vS")
    X.idma(e.dma_start(out=dr["dbg_m5"][:, 0:16 * T], in_=c["m_a"][:, :]),
           "qdbgm")
    X.idma(e.dma_start(out=dr["dbg_m5"][:, 16 * T:22 * T], in_=c["m_b"][:, :]),
           "qdbgm")
    X.lbl("m5_dump")


def dbg_pfull_end(X, e):
    c, dr = X.c, X.dr
    X.wait(e, "ost30", "qout0")
    X.wait(e, "ost31", "qout1")
    X.idma(e.dma_start(out=dr["dbg_hr"][:, 0:16 * T], in_=c["m_a"][:, :]),
           "qdbgf")
    X.idma(e.dma_start(out=dr["dbg_hr"][:, 16 * T:22 * T], in_=c["m_b"][:, :]),
           "qdbgf")
    X.idma(e.dma_start(out=dr["dbg_x2e"][:, :], in_=c["h3"][:, :]), "qdbgf")
    X.idma(e.dma_start(out=dr["dbg_sc2"][:, 0:DCH], in_=c["cdd"][:, :]),
           "qdbgf")
    X.idma(e.dma_start(out=dr["dbg_sc2"][:, DCH:DCH + 1], in_=c["s2r"][:, 0:1],
                       allow_slow_non_contiguous=True), "qdbgf")
    if not X.pass1:
        e.wait_ge(X.sems["qdbgf"].h, X.sems["qdbgf"].n)


def dbg_pfull_y(X, e):
    c, dr = X.c, X.dr
    X.wait(e, "cdg_done", "vS")
    X.wait(e, "macc_done", "vS")
    X.idma(e.dma_start(out=dr["dbg_y"][:, :], in_=c["xs"][:, :]), "qdbgy")
    if not X.pass1:
        e.wait_ge(X.sems["qdbgy"].h, X.sems["qdbgy"].n)
